# version 05
# baseline (speedup 1.0000x reference)
import numpy as np
import ml_dtypes
from contextlib import ExitStack
import concourse.bass as bass
import concourse.mybir as mybir
from concourse.bass_utils import run_bass_kernel_spmd

F32 = mybir.dt.float32
BF16 = mybir.dt.bfloat16
U16 = mybir.dt.uint16
U8 = mybir.dt.uint8
FP8 = mybir.dt.float8e4
AF = mybir.ActivationFunctionType
ALU = mybir.AluOpType
AX = mybir.AxisListType
NPBF = ml_dtypes.bfloat16

D = 2048
T = 2048
NSEQ = 2
DC = 16
TT = 4
EPS = 1e-5
MSH = 15.0
C_GLA = 32
NCHUNK = T // C_GLA
CLAMP = -85.0
ISQ = float(1.0 / np.sqrt(128.0))
M0F, M07, M08, M01 = 0x0F0F, 0x0707, 0x0808, 0x0101

ENGS = ("tensor", "vector", "scalar", "gpsimd", "sync")
DEBUG = False


class P:

    NSLOT = 8

    def __init__(self, nc):
        self.nc = nc
        self.ops = {e: [] for e in ENGS}
        self.last = {}
        self.sems = {}
        for e in ENGS:
            self.sems["e" + e] = self._mk("es_" + e)
        for qn in ("sync", "scalar"):
            for s in range(self.NSLOT):
                self.sems["d%s%d" % (qn, s)] = self._mk("ds_%s%d" % (qn, s))
        self.dcnt = {qn: 0 for qn in ("sync", "scalar")}
        self.slotprev = {}
        self.lastcomp = {}
        self.barrier_deps = []
        self.exempt = False
        self.exempt_last = {}

    def _mk(self, name):
        ctx = self.nc.semaphore(name)
        h = ctx.__enter__()
        return h

    @staticmethod
    def _key(b):
        if isinstance(b, str):
            return b
        n = getattr(b, "name", None)
        if n is not None:
            return n
        t = getattr(b, "tensor", None)
        if t is not None:
            return t.name
        raise TypeError(b)

    def _deps(self, r, w, accum):
        deps = []
        for b in r:
            e = self.last.get(self._key(b))
            if e and e[0]:
                deps.append(e[0])
        acc_keys = {self._key(b) for b in accum}
        for b in list(w) + list(accum):
            kb = self._key(b)
            e = self.last.get(kb)
            if e:
                if e[0] and kb not in acc_keys:
                    deps.append(e[0])
                deps.extend(e[1])
        return deps

    def _record(self, eng, fn, r, w, accum, is_dma):
        deps = self._deps(r, w, accum)
        deps.extend(self.barrier_deps)
        if is_dma:
            di = self.dcnt[eng]
            self.dcnt[eng] = di + 1
            slot = di % self.NSLOT
            semkey = "d%s%d" % (eng, slot)
            prev = self.slotprev.get((eng, slot))
            if prev is not None:
                deps.append(prev)
        else:
            semkey = "e" + eng
        idx = len(self.ops[eng])
        h = (eng, semkey, idx)
        if is_dma:
            self.slotprev[(eng, slot)] = h
        self.ops[eng].append([fn, deps, is_dma, False, semkey])
        if not is_dma:
            if self.exempt:
                self.exempt_last[eng] = h
            else:
                self.lastcomp[eng] = h
        for b in r:
            kb = self._key(b)
            self.last.setdefault(kb, [None, []])[1].append(h)
        for b in list(w) + list(accum):
            self.last[self._key(b)] = [h, []]
        return h

    def barrier(self):
        hs = list(self.lastcomp.values()) + list(self.slotprev.values())
        self.barrier_deps = hs

    def barrier_full(self):
        hs = (list(self.lastcomp.values()) + list(self.slotprev.values())
              + list(self.exempt_last.values()))
        self.barrier_deps = hs
        self.exempt_last = {}

    def step(self, eng, fn, r=(), w=(), accum=()):
        wk = {self._key(b) for b in w} | {self._key(b) for b in accum}
        rr = list(r) if "K0" in wk else list(r) + ["K0"]
        return self._record(eng, fn, rr, w, accum, False)

    def dma(self, src_ap, dst_ap, q="sync", r=(), w=(), nokey=False):
        def _f(E, s=src_ap, t=dst_ap):
            return E.dma_start(out=t, in_=s)
        if nokey:
            return self._record(q, _f, list(r) + ["K0"], list(w), (), True)
        return self._record(q, _f, list(r) + [src_ap, "K0"],
                            list(w) + [dst_ap], (), True)

    def finalize(self):
        needed = set()
        for e in ENGS:
            for op in self.ops[e]:
                for d in op[1]:
                    needed.add(d)
        val = {}
        counters = {k: 0 for k in self.sems}
        for e in ENGS:
            for idx, op in enumerate(self.ops[e]):
                semkey = op[4]
                if (e, semkey, idx) in needed or op[2]:
                    op[3] = True
                    counters[semkey] += 16 if op[2] else 1
                    val[(e, semkey, idx)] = counters[semkey]
                else:
                    op[3] = False
        finals = dict(counters)
        sems = self.sems
        wm = {e: {} for e in ENGS}

        def run_engine(E, e):
            for idx, (fn, deps, is_dma, inc, semkey0) in enumerate(self.ops[e]):
                byk = {}
                for (de, dk, di) in deps:
                    v = val.get((de, dk, di))
                    if v is None:
                        raise RuntimeError("unmarked dep")
                    if v > byk.get(dk, 0):
                        byk[dk] = v
                for dk, v in byk.items():
                    if v > wm[e].get(dk, 0):
                        E.wait_ge(sems[dk], v)
                        wm[e][dk] = v
                ins = fn(E)
                if inc:
                    ins.then_inc(sems[semkey0], 16 if is_dma else 1)

        with self.nc.Block() as blk:
            blk.tensor(lambda E: run_engine(E, "tensor"))
            blk.vector(lambda E: run_engine(E, "vector"))
            blk.scalar(lambda E: run_engine(E, "scalar"))
            blk.gpsimd(lambda E: run_engine(E, "gpsimd"))

            def sync_body(E):
                run_engine(E, "sync")
                for kk2, v in finals.items():
                    if v:
                        E.wait_ge(sems[kk2], v)
            blk.sync(sync_body)


def fmv(hbm_ap, tsl=None):
    a = hbm_ap if tsl is None else hbm_ap[:, tsl]
    return a.rearrange("(c p) t -> p c t", p=128)


def v3(sb_ap, tw):
    return sb_ap.rearrange("p (c t) -> p c t", t=tw)


class K:
    def __init__(self):
        self.nc = bass.Bass("TRN2", target_bir_lowering=False)
        self.stk = ExitStack()
        self.t = {}

    def dram_in(self, name, shape, dtype):
        self.t[name] = self.nc.dram_tensor(name, shape, dtype,
                                           kind="ExternalInput").ap()

    def dram_out(self, name, shape, dtype):
        self.t[name] = self.nc.dram_tensor(name, shape, dtype,
                                           kind="ExternalOutput").ap()

    def dram_tmp(self, name, shape, dtype):
        self.t[name] = self.nc.dram_tensor(name, shape, dtype,
                                           kind="Internal").ap()

    def sb(self, name, shape, dtype):
        return self.stk.enter_context(self.nc.sbuf_tensor(name, shape, dtype))


def build():
    k = K()
    nc = k.nc
    kt = k.t
    k.dram_in("xT", [NSEQ, D, T], F32)
    for nm, sh in [("Wq", [D, D]), ("Wk", [D, 512]), ("Wv", [D, 512]),
                   ("Wo", [D, D]), ("W1q", [D, D]), ("W1k", [D, D]),
                   ("W1i", [D, D]), ("W1o", [D, D])]:
        k.dram_in(nm, sh, F32)
    k.dram_in("PKg", [1024, 16384], U8)
    k.dram_in("PKu", [1024, 16384], U8)
    k.dram_in("PKd", [8192, 2048], U8)
    k.dram_in("SCg", [64, 16384], F32)
    k.dram_in("SCu", [64, 16384], F32)
    k.dram_in("SCd", [512, 2048], F32)
    k.dram_in("GWT", [D, 11264], F32)
    k.dram_in("DWT", [5632, D], F32)
    k.dram_in("gw", [128, 16], F32)
    k.dram_in("gnw", [128, 16], F32)
    k.dram_in("dnw", [128, 44], F32)
    k.dram_in("ident", [128, 128], BF16)
    k.dram_in("rmat", [128, 128], BF16)
    k.dram_in("e4g", [4, 128], BF16)
    k.dram_in("cosT", [128, T], F32)
    k.dram_in("sinT", [128, T], F32)
    k.dram_in("amask", [512, 512], BF16)
    k.dram_in("gmask", [32, 32], BF16)
    k.dram_out("yT", [NSEQ, D, T], F32)
    if DEBUG:
        for i in range(4):
            k.dram_out("dbg%d" % i, [D, T], F32)

    k.dram_tmp("s_x", [D, T], F32)
    k.dram_tmp("s_xn", [D, T], BF16)
    k.dram_tmp("s_q", [D, T], BF16)
    k.dram_tmp("s_k", [512, T], BF16)
    k.dram_tmp("s_v", [T, 512], BF16)
    k.dram_tmp("s_o", [D, T], BF16)
    k.dram_tmp("s_z", [D, T], F32)
    k.dram_tmp("s_wgq", [D, 16384], BF16)
    k.dram_tmp("s_wuq", [D, 16384], BF16)
    k.dram_tmp("s_wdq", [16384, D], BF16)
    k.dram_tmp("s_y1", [16384, T], F32)
    k.dram_tmp("s_y2", [16384, T], F32)
    k.dram_tmp("s_int", [16384, T], BF16)
    k.dram_tmp("s_t1", [D, T], F32)
    k.dram_tmp("s_kr", [D, T], F32)
    k.dram_tmp("s_ks", [D, T], F32)
    k.dram_tmp("s_ar", [D, T], F32)
    k.dram_tmp("s_gq", [D, T], BF16)
    k.dram_tmp("s_gk", [D, T], BF16)
    k.dram_tmp("s_kh", [D, T], BF16)
    k.dram_tmp("s_khT", [T, D], BF16)
    k.dram_tmp("s_vg", [T, D], BF16)
    k.dram_tmp("s_lam", [D, NCHUNK], F32)
    k.dram_tmp("s_xr", [D, T], F32)
    k.dram_tmp("s_xq", [D, T], BF16)
    k.dram_tmp("s_h", [5632, T], F32)
    k.dram_tmp("s_hr", [5632, T], F32)
    k.dram_tmp("s_hq", [5632, T], BF16)
    k.dram_tmp("s_g8", [D, 11264], BF16)
    k.dram_tmp("s_d8", [5632, D], BF16)
    k.dram_tmp("s_rows", [8, T], F32)
    k.dram_tmp("s_rows_bf", [8, T], BF16)
    k.dram_tmp("s_wsc", [1, 8], F32)

    p = P(nc)

    cst = k.sb("cst", [128, 16], F32)
    CVALS = [0.0, 1.0, -1.0, 0.5, float(EPS), MSH, -MSH, ISQ, 127.0, CLAMP,
             128.5, 1.0 / 127.0]
    for i, v in enumerate(CVALS):
        nc.const_aps.register(F32, v, cst[:, i:i + 1])
        p.step("vector", lambda E, i=i, v=v: E.memset(cst[:, i:i + 1], v),
               w=(cst, "K0"))
    ones_col = k.sb("ones_col", [128, 1], F32)
    ones_row = k.sb("ones_row", [1, 128], F32)
    ones_col_bf = k.sb("ones_col_bf", [128, 1], BF16)
    ones_row_bf = k.sb("ones_row_bf", [1, 128], BF16)
    p.step("vector", lambda E: E.memset(ones_col[:, :], 1.0),
           w=(ones_col, "K0"))
    p.step("vector", lambda E: E.memset(ones_row[:, :], 1.0),
           w=(ones_row, "K0"))
    p.step("vector", lambda E: E.memset(ones_col_bf[:, :], 1.0),
           w=(ones_col_bf, "K0"))
    p.step("vector", lambda E: E.memset(ones_row_bf[:, :], 1.0),
           w=(ones_row_bf, "K0"))
    ident = k.sb("identS", [128, 128], BF16)
    rmat = k.sb("rmatS", [128, 128], BF16)
    e4g = k.sb("e4gS", [4, 128], BF16)
    gmask = k.sb("gmaskS", [32, 32], BF16)
    p.dma(kt["ident"], ident[:, :], w=("K0",))
    p.dma(kt["rmat"], rmat[:, :], w=("K0",))
    p.dma(kt["e4g"], e4g[:, :], w=("K0",))
    p.dma(kt["gmask"], gmask[:, :], w=("K0",))
    DQNS = 2048
    dq_stk = ExitStack()
    dqb = {nm: dq_stk.enter_context(nc.sbuf_tensor("dq_" + nm, [128, DQNS], U8))
           for nm in ("d", "2", "3", "4", "5", "6")}
    dqb["s4"] = dq_stk.enter_context(nc.sbuf_tensor("dq_s4", [4, DQNS], F32))
    dqb["sb"] = dq_stk.enter_context(nc.sbuf_tensor("dq_sb", [4, DQNS], BF16))
    dqb["o"] = dq_stk.enter_context(nc.sbuf_tensor("dq_o", [128, DQNS], BF16))
    dq_psA = dq_stk.enter_context(nc.psum_tensor("dq_psA", [128, 512], F32))

    def ph_layernorm(src, dst, Din, src_dtype=F32, out_dtype=BF16,
                     mean_center=True, gamma_hbm=None, tag="ln"):
        dch = Din // 128
        NBUF = 2 if dch <= 16 else 1
        with ExitStack() as st:
            sbl = lambda n, sh, dt: st.enter_context(nc.sbuf_tensor(tag + n, sh, dt))
            xs = [sbl("_x%d" % i, [128, dch * 512], F32) for i in range(NBUF)]
            xi = ([sbl("_xi%d" % i, [128, dch * 512], src_dtype)
                   for i in range(NBUF)] if src_dtype != F32 else None)
            sq = [sbl("_sq%d" % i, [128, 512], F32) for i in range(2)]
            rows = sbl("_rw", [1, 1024], F32)
            outs = (xs if (out_dtype == F32 and dch > 16) else
                    [sbl("_o%d" % i, [128, dch * 512], out_dtype)
                     for i in range(NBUF)])
            gcol = sbl("_g", [128, dch], F32) if gamma_hbm is not None else None
            with nc.psum_tensor(tag + "_p1", [1, 512], F32) as ps_s, \
                 nc.psum_tensor(tag + "_p2", [1, 512], F32) as ps_q, \
                 nc.psum_tensor(tag + "_pbr", [128, 512], F32) as psb_r, \
                 nc.psum_tensor(tag + "_pbm", [128, 512], F32) as psb_m:
                if gamma_hbm is not None:
                    p.dma(gamma_hbm, gcol[:, :])
                for tt in range(TT):
                    pr = tt % NBUF
                    X = xs[pr]
                    O = outs[pr]
                    tsl = slice(tt * 512, (tt + 1) * 512)
                    if src_dtype != F32:
                        p.dma(fmv(src, tsl), v3(xi[pr][:, :], 512))
                        p.step("vector", lambda E, pr=pr: E.tensor_copy(
                            out=xs[pr][:, :], in_=xi[pr][:, :]),
                            r=(xi[pr],), w=(X,))
                    else:
                        p.dma(fmv(src, tsl), v3(X[:, :], 512))
                    for c in range(dch):
                        csl = slice(c * 512, (c + 1) * 512)
                        sqc = sq[c % 2]
                        if mean_center:
                            p.step("tensor", lambda E, c=c, csl=csl, X=X: E.matmul(
                                ps_s[:, :], ones_col[:, :], X[:, csl],
                                start=(c == 0), stop=(c == dch - 1)),
                                r=(ones_col, X),
                                w=(ps_s,) if c == 0 else (), accum=(ps_s,) if c else ())
                        p.step("scalar", lambda E, csl=csl, X=X, sqc=sqc: E.activation(
                            out=sqc[:, :], in_=X[:, csl], func=AF.Square),
                            r=(X,), w=(sqc,))
                        p.step("tensor", lambda E, c=c, sqc=sqc: E.matmul(
                            ps_q[:, :], ones_col[:, :], sqc[:, :],
                            start=(c == 0), stop=(c == dch - 1)),
                            r=(ones_col, sqc),
                            w=(ps_q,) if c == 0 else (), accum=(ps_q,) if c else ())
                    r_r = rows[:, 0:512]
                    r_mr = rows[:, 512:1024]
                    p.step("vector", lambda E, r_r=r_r: E.tensor_scalar_mul(
                        r_r, ps_q[0:1, :], 1.0 / Din), r=(ps_q,), w=(rows,))
                    if mean_center:
                        p.step("vector", lambda E, r_mr=r_mr: E.tensor_scalar_mul(
                            r_mr, ps_s[0:1, :], 1.0 / Din), r=(ps_s,), w=(rows,))
                        p.step("vector", lambda E, r_mr=r_mr: E.scalar_tensor_tensor(
                            out=rows[0:1, 512:1024] if False else r_mr,
                            in0=r_mr, scalar=-1.0, in1=r_mr,
                            op0=ALU.mult, op1=ALU.mult), r=(rows,), w=("lnrtmp",)) \
                            if False else None
                        p.step("vector", lambda E, r_mr=r_mr: E.scalar_tensor_tensor(
                            out=sq[0][0:1, 0:512], in0=r_mr, scalar=-1.0,
                            in1=r_mr, op0=ALU.mult, op1=ALU.mult),
                            r=(rows,), w=(sq[0],))
                        p.step("vector", lambda E, r_r=r_r: E.tensor_add(
                            r_r, r_r, sq[0][0:1, 0:512]), r=(rows, sq[0]),
                            w=(rows,))
                    p.step("scalar", lambda E, r_r=r_r: E.activation(
                        out=r_r, in_=r_r, func=AF.Sqrt, bias=float(EPS)),
                        r=(rows, cst), w=(rows,))
                    p.step("vector", lambda E, r_r=r_r: E.reciprocal(r_r, r_r),
                           r=(rows,), w=(rows,))
                    if mean_center:
                        p.step("vector", lambda E, r_mr=r_mr, r_r=r_r: E.tensor_mul(
                            r_mr, r_mr, r_r), r=(rows,), w=(rows,))
                    p.step("tensor", lambda E: E.matmul(
                        psb_r[:, :], ones_row[:, :], rows[0:1, 0:512],
                        start=True, stop=True), r=(ones_row, rows), w=(psb_r,))
                    if mean_center:
                        p.step("tensor", lambda E: E.matmul(
                            psb_m[:, :], ones_row[:, :], rows[0:1, 512:1024],
                            start=True, stop=True), r=(ones_row, rows),
                            w=(psb_m,))
                    for c in range(dch):
                        csl = slice(c * 512, (c + 1) * 512)
                        p.step("vector", lambda E, csl=csl, X=X: E.tensor_mul(
                            X[:, csl], X[:, csl], psb_r[:, :]),
                            r=(X, psb_r), w=(X,))
                        if mean_center:
                            p.step("vector", lambda E, csl=csl, X=X: E.tensor_sub(
                                X[:, csl], X[:, csl], psb_m[:, :]),
                                r=(X, psb_m), w=(X,))
                        if gamma_hbm is not None:
                            p.step("vector", lambda E, c=c, csl=csl, X=X, O=O:
                                   E.tensor_scalar(
                                out=O[:, csl], in0=X[:, csl],
                                scalar1=gcol[:, c:c + 1], scalar2=None,
                                op0=ALU.mult), r=(X, gcol), w=(O,))
                        else:
                            p.step("vector", lambda E, csl=csl, X=X, O=O:
                                   E.tensor_copy(out=O[:, csl], in_=X[:, csl]),
                                   r=(X,), w=(O,))
                    p.dma(v3(O[:, :], 512), fmv(dst, tsl), q="scalar")

    def ph_resadd(xsrc, zsrc, dst, tag):
        NB2 = 1
        with ExitStack() as st:
            sbl = lambda n, sh, dt: st.enter_context(nc.sbuf_tensor(tag + n, sh, dt))
            zs = [sbl("_z%d" % i, [128, DC * 512], F32) for i in range(NB2)]
            xsb = [sbl("_x%d" % i, [128, DC * 512], F32) for i in range(NB2)]
            sq = [sbl("_sq%d" % i, [128, 512], F32) for i in range(2)]
            rows = sbl("_rw", [1, 1024], F32)
            outs = [sbl("_o%d" % i, [128, DC * 512], F32) for i in range(NB2)]
            with nc.psum_tensor(tag + "_p1", [1, 512], F32) as ps_s, \
                 nc.psum_tensor(tag + "_p2", [1, 512], F32) as ps_q, \
                 nc.psum_tensor(tag + "_pbr", [128, 512], F32) as psb_r, \
                 nc.psum_tensor(tag + "_pbm", [128, 512], F32) as psb_m:
                for tt in range(TT):
                    pr = tt % NB2
                    Z, X, O = zs[pr], xsb[pr], outs[pr]
                    tsl = slice(tt * 512, (tt + 1) * 512)
                    p.dma(fmv(zsrc, tsl), v3(Z[:, :], 512))
                    p.dma(fmv(xsrc, tsl), v3(X[:, :], 512))
                    for c in range(DC):
                        csl = slice(c * 512, (c + 1) * 512)
                        sqc = sq[c % 2]
                        p.step("tensor", lambda E, c=c, csl=csl, Z=Z: E.matmul(
                            ps_s[:, :], ones_col[:, :], Z[:, csl],
                            start=(c == 0), stop=(c == DC - 1)),
                            r=(ones_col, Z),
                            w=(ps_s,) if c == 0 else (), accum=(ps_s,) if c else ())
                        p.step("scalar", lambda E, csl=csl, Z=Z, sqc=sqc:
                               E.activation(out=sqc[:, :], in_=Z[:, csl],
                                            func=AF.Square), r=(Z,), w=(sqc,))
                        p.step("tensor", lambda E, c=c, sqc=sqc: E.matmul(
                            ps_q[:, :], ones_col[:, :], sqc[:, :],
                            start=(c == 0), stop=(c == DC - 1)),
                            r=(ones_col, sqc),
                            w=(ps_q,) if c == 0 else (), accum=(ps_q,) if c else ())
                    r_r = rows[:, 0:512]
                    r_mr = rows[:, 512:1024]
                    p.step("vector", lambda E, r_r=r_r: E.tensor_scalar_mul(
                        r_r, ps_q[0:1, :], 1.0 / D), r=(ps_q,), w=(rows,))
                    p.step("vector", lambda E, r_mr=r_mr: E.tensor_scalar_mul(
                        r_mr, ps_s[0:1, :], 1.0 / D), r=(ps_s,), w=(rows,))
                    p.step("vector", lambda E, r_mr=r_mr: E.scalar_tensor_tensor(
                        out=sq[0][0:1, 0:512], in0=r_mr, scalar=-1.0, in1=r_mr,
                        op0=ALU.mult, op1=ALU.mult), r=(rows,), w=(sq[0],))
                    p.step("vector", lambda E, r_r=r_r: E.tensor_add(
                        r_r, r_r, sq[0][0:1, 0:512]), r=(rows, sq[0]), w=(rows,))
                    p.step("scalar", lambda E, r_r=r_r: E.activation(
                        out=r_r, in_=r_r, func=AF.Sqrt, bias=float(EPS)),
                        r=(rows, cst), w=(rows,))
                    p.step("vector", lambda E, r_r=r_r: E.reciprocal(r_r, r_r),
                           r=(rows,), w=(rows,))
                    p.step("vector", lambda E, r_mr=r_mr, r_r=r_r: E.tensor_mul(
                        r_mr, r_mr, r_r), r=(rows,), w=(rows,))
                    p.step("tensor", lambda E: E.matmul(
                        psb_r[:, :], ones_row[:, :], rows[0:1, 0:512],
                        start=True, stop=True), r=(ones_row, rows), w=(psb_r,))
                    p.step("tensor", lambda E: E.matmul(
                        psb_m[:, :], ones_row[:, :], rows[0:1, 512:1024],
                        start=True, stop=True), r=(ones_row, rows), w=(psb_m,))
                    for c in range(DC):
                        csl = slice(c * 512, (c + 1) * 512)
                        p.step("vector", lambda E, csl=csl, Z=Z: E.tensor_mul(
                            Z[:, csl], Z[:, csl], psb_r[:, :]),
                            r=(Z, psb_r), w=(Z,))
                        p.step("vector", lambda E, csl=csl, Z=Z: E.tensor_sub(
                            Z[:, csl], Z[:, csl], psb_m[:, :]),
                            r=(Z, psb_m), w=(Z,))
                        p.step("vector", lambda E, csl=csl, Z=Z, X=X, O=O:
                               E.tensor_add(O[:, csl], Z[:, csl], X[:, csl]),
                               r=(Z, X), w=(O,))
                    p.dma(v3(O[:, :], 512), fmv(dst, tsl), q="scalar")

    def ph_linear(w_hbm, x_hbm, out_hbm, Kdim, Ndim, tag, w_dtype=F32,
                  out_dtype=BF16, row_scale=None, out_keys=None, post=None):
        kslab = min(Kdim, 2048)
        nslab = (Kdim + kslab - 1) // kslab
        if nslab > 1:
            kslab = 1024
            nslab = (Kdim + kslab - 1) // kslab
        NB = 256
        acc_needed = nslab > 1
        with ExitStack() as st:
            sbl = lambda n, sh, dt: st.enter_context(nc.sbuf_tensor(tag + n, sh, dt))
            xs = sbl("_x", [128, (kslab // 128) * T], BF16)
            wf = [sbl("_wf%d" % i, [128, (kslab // 128) * NB], w_dtype)
                  for i in range(2)]
            wb = ([sbl("_wb%d" % i, [128, (kslab // 128) * NB], BF16)
                   for i in range(2)] if w_dtype != BF16 else wf)
            outs = [sbl("_o%d" % i, [128, (1 if acc_needed else 2) * T],
                        out_dtype) for i in range(1 if acc_needed else 2)]
            acc = sbl("_a", [128, (Ndim // 128) * T], F32) if acc_needed else None
            scl = sbl("_s", [128, TT * 512], BF16) if row_scale is not None else None
            srow = sbl("_sr", [1, T], BF16) if row_scale is not None else None
            with ExitStack() as pst:
                ps4 = [pst.enter_context(nc.psum_tensor("%s_ps%d" % (tag, i),
                                                        [128, 512], F32))
                       for i in range(4)]
                ps = [ps4[i % 4] for i in range(8)]
                if row_scale is not None:
                    p.dma(row_scale, srow[:, :])
                    for tt in range(TT):
                        p.step("tensor", lambda E, tt=tt: E.matmul(
                            ps[0][:, :], ones_row_bf[:, :],
                            srow[0:1, tt * 512:(tt + 1) * 512],
                            start=True, stop=True), r=(ones_row_bf, srow),
                            w=(ps[0],))
                        p.step("vector", lambda E, tt=tt: E.tensor_copy(
                            out=scl[:, tt * 512:(tt + 1) * 512],
                            in_=ps[0][:, :]), r=(ps[0],), w=(scl,))
                for sl in range(nslab):
                    k0 = sl * kslab
                    kc = min(kslab, Kdim - k0) // 128
                    p.dma(fmv(x_hbm[k0:k0 + kc * 128, :]),
                          v3(xs[:, :kc * T], T))
                    for nb in range(Ndim // NB):
                        n0 = nb * NB
                        pr = nb % 2
                        WF, WB = wf[pr], wb[pr]
                        O = outs[pr % len(outs)]
                        p.dma(w_hbm[k0:k0 + kc * 128, n0:n0 + NB].rearrange(
                            "(c p) n -> p c n", p=128),
                            v3(WF[:, :kc * NB], NB))
                        if w_dtype != BF16:
                            p.step("vector", lambda E, kc=kc, WF=WF, WB=WB:
                                   E.tensor_copy(out=WB[:, :kc * NB],
                                                 in_=WF[:, :kc * NB]),
                                   r=(WF,), w=(WB,))
                        for t4 in range(TT):
                            for n2 in range(2):
                                bank = t4 * 2 + n2
                                for c in range(kc):
                                    p.step("tensor", lambda E, c=c, n2=n2,
                                           t4=t4, bank=bank, kc=kc, WB=WB:
                                           E.matmul(
                                        ps[bank][:, :],
                                        WB[:, c * NB + n2 * 128:
                                           c * NB + n2 * 128 + 128],
                                        xs[:, c * T + t4 * 512:
                                           c * T + (t4 + 1) * 512],
                                        start=(c == 0), stop=(c == kc - 1)),
                                        r=(WB, xs),
                                        w=(ps[bank],) if c == 0 else (),
                                        accum=(ps[bank],) if c else ())
                                if acc_needed:
                                    nch = n0 // 128 + n2
                                    asl = slice(nch * T + t4 * 512,
                                                nch * T + (t4 + 1) * 512)
                                    if sl == 0:
                                        p.step("vector", lambda E, bank=bank,
                                               asl=asl: E.tensor_copy(
                                            out=acc[:, asl],
                                            in_=ps[bank][:, :]),
                                            r=(ps[bank],), w=(acc,))
                                    else:
                                        p.step("vector", lambda E, bank=bank,
                                               asl=asl: E.tensor_add(
                                            acc[:, asl], acc[:, asl],
                                            ps[bank][:, :]),
                                            r=(ps[bank], acc), w=(acc,))
                                else:
                                    osl = slice(n2 * T + t4 * 512,
                                                n2 * T + (t4 + 1) * 512)
                                    if row_scale is not None:
                                        ssl = slice(t4 * 512, (t4 + 1) * 512)
                                        p.step("vector", lambda E, bank=bank,
                                               osl=osl, ssl=ssl, O=O:
                                               E.tensor_mul(
                                            O[:, osl], ps[bank][:, :],
                                            scl[:, ssl]),
                                            r=(ps[bank], scl), w=(O,))
                                    else:
                                        p.step("vector", lambda E, bank=bank,
                                               osl=osl, O=O: E.tensor_copy(
                                            out=O[:, osl], in_=ps[bank][:, :]),
                                            r=(ps[bank],), w=(O,))
                        if not acc_needed:
                            if out_keys is not None:
                                p.dma(v3(O[:, :], T),
                                      fmv(out_hbm[n0:n0 + NB, :]),
                                      q="scalar", r=(O,),
                                      w=("%s#%d" % (out_keys, nb),),
                                      nokey=True)
                            else:
                                p.dma(v3(O[:, :], T),
                                      fmv(out_hbm[n0:n0 + NB, :]),
                                      q="scalar")
                if acc_needed:
                    for nch in range(Ndim // 128):
                        O = outs[0]
                        if row_scale is not None:
                            for t4 in range(TT):
                                p.step("vector", lambda E, nch=nch, t4=t4, O=O:
                                       E.tensor_mul(
                                    O[:, t4 * 512:(t4 + 1) * 512],
                                    acc[:, nch * T + t4 * 512:
                                        nch * T + (t4 + 1) * 512],
                                    scl[:, t4 * 512:(t4 + 1) * 512]),
                                    r=(acc, scl), w=(O,))
                        else:
                            p.step("vector", lambda E, nch=nch, O=O:
                                   E.tensor_copy(
                                out=O[:, 0:T], in_=acc[:, nch * T:(nch + 1) * T]),
                                r=(acc,), w=(O,))
                        p.dma(v3(O[:, 0:T], T),
                              fmv(out_hbm[nch * 128:(nch + 1) * 128, :]),
                              q="scalar")

                if post is not None:
                    post()

    def ph_linear_multi(pairs, x_hbm, Kdim, Ndim, tag, w_dtype=BF16,
                        out_dtype=F32, post=None):
        kslab = Kdim
        NB = 256
        with ExitStack() as st:
            sbl = lambda n, sh, dt: st.enter_context(nc.sbuf_tensor(tag + n, sh, dt))
            xs = sbl("_x", [128, (kslab // 128) * T], BF16)
            wf = [sbl("_wf%d" % i, [128, (kslab // 128) * NB], w_dtype)
                  for i in range(2)]
            outs = [sbl("_o%d" % i, [128, 2 * T], out_dtype) for i in range(2)]
            with ExitStack() as pstk:
                ps4 = [pstk.enter_context(nc.psum_tensor("%s_ps%d" % (tag, i),
                                                         [128, 512], F32))
                       for i in range(4)]
                ps = [ps4[i % 4] for i in range(8)]
                kc = Kdim // 128
                p.dma(fmv(x_hbm), v3(xs[:, :kc * T], T))
                for w_hbm, out_hbm, pfx in pairs:
                    for nb in range(Ndim // NB):
                        n0 = nb * NB
                        pr = nb % 2
                        WB, O = wf[pr], outs[pr]
                        p.dma(w_hbm[0:kc * 128, n0:n0 + NB].rearrange(
                            "(c p) n -> p c n", p=128),
                            v3(WB[:, :kc * NB], NB))
                        for t4 in range(TT):
                            for n2 in range(2):
                                bank = t4 * 2 + n2
                                for c in range(kc):
                                    p.step("tensor", lambda E, c=c, n2=n2,
                                           t4=t4, bank=bank, WB=WB: E.matmul(
                                        ps[bank][:, :],
                                        WB[:, c * NB + n2 * 128:
                                           c * NB + n2 * 128 + 128],
                                        xs[:, c * T + t4 * 512:
                                           c * T + (t4 + 1) * 512],
                                        start=(c == 0), stop=(c == kc - 1)),
                                        r=(WB, xs),
                                        w=(ps[bank],) if c == 0 else (),
                                        accum=(ps[bank],) if c else ())
                                osl = slice(n2 * T + t4 * 512,
                                            n2 * T + (t4 + 1) * 512)
                                p.step("vector", lambda E, bank=bank, osl=osl,
                                       O=O: E.tensor_copy(
                                    out=O[:, osl], in_=ps[bank][:, :]),
                                    r=(ps[bank],), w=(O,))
                        p.dma(v3(O[:, :], T), fmv(out_hbm[n0:n0 + NB, :]),
                              q="scalar", r=(O,), w=("%s#%d" % (pfx, nb),),
                              nokey=True)
                if post is not None:
                    post()

    def em_gelu_mult(tag):
        with ExitStack() as st:
            sbl = lambda n, sh, dt: st.enter_context(nc.sbuf_tensor(tag + n, sh, dt))
            ab = [sbl("_a%d" % i, [128, T], F32) for i in range(2)]
            bb2 = [sbl("_b%d" % i, [128, T], F32) for i in range(2)]
            obb = [sbl("_o%d" % i, [128, T], BF16) for i in range(2)]
            for c in range(128):
                pr = c % 2
                a, b, ob = ab[pr], bb2[pr], obb[pr]
                rs = slice(c * 128, (c + 1) * 128)
                blk = c // 2
                p.dma(kt["s_y1"][rs, :], a[:, :], r=("y1#%d" % blk,),
                      w=(a,), nokey=True)
                p.dma(kt["s_y2"][rs, :], b[:, :], r=("y2#%d" % blk,),
                      w=(b,), nokey=True)
                p.step("scalar", lambda E, a=a: E.activation(
                    out=a[:, :], in_=a[:, :], func=AF.Gelu_apprx_tanh),
                    r=(a,), w=(a,))
                p.step("vector", lambda E, ob=ob, a=a, b=b: E.tensor_mul(
                    ob[:, :], a[:, :], b[:, :]), r=(a, b), w=(ob,))
                p.dma(ob[:, :], kt["s_int"][rs, :], q="scalar")

    def em_silu_mult(tag):
        with ExitStack() as st:
            sbl = lambda n, sh, dt: st.enter_context(nc.sbuf_tensor(tag + n, sh, dt))
            ab = [sbl("_a%d" % i, [128, T], F32) for i in range(2)]
            bb2 = [sbl("_b%d" % i, [128, T], F32) for i in range(2)]
            obb = [sbl("_o%d" % i, [128, T], F32) for i in range(2)]
            for c in range(44):
                pr = c % 2
                a, b, ob = ab[pr], bb2[pr], obb[pr]
                p.dma(kt["s_y1"][c * 128:(c + 1) * 128, :], a[:, :],
                      r=("yb#%d" % (c // 2),), w=(a,), nokey=True)
                p.dma(kt["s_y1"][(44 + c) * 128:(45 + c) * 128, :], b[:, :],
                      r=("yb#%d" % ((44 + c) // 2),), w=(b,), nokey=True)
                p.step("scalar", lambda E, a=a: E.activation(
                    out=a[:, :], in_=a[:, :], func=AF.Silu), r=(a,), w=(a,))
                p.step("vector", lambda E, ob=ob, a=a, b=b: E.tensor_mul(
                    ob[:, :], a[:, :], b[:, :]), r=(a, b), w=(ob,))
                p.dma(ob[:, :], kt["s_h"][c * 128:(c + 1) * 128, :],
                      q="scalar")

    def ph_linear_tok(w_hbm, x_hbm, out_hbm, Ndim, tag):
        with ExitStack() as st:
            sbl = lambda n, sh, dt: st.enter_context(nc.sbuf_tensor(tag + n, sh, dt))
            xs = sbl("_x", [128, DC * T], BF16)
            NQ = 8 if Ndim >= 2048 else 4
            wf = sbl("_wf", [128, DC * (Ndim // NQ)], F32)
            wbt = sbl("_wb", [128, DC * Ndim], BF16)
            outs = [sbl("_o%d" % i, [128, Ndim], BF16)
                    for i in range(1 if Ndim >= 2048 else 2)]
            with ExitStack() as pstk:
                nbk = Ndim // 512
                ps = [pstk.enter_context(nc.psum_tensor("%s_ps%d" % (tag, i),
                                                        [128, 512], F32))
                      for i in range(nbk)]
                p.dma(fmv(x_hbm), v3(xs[:, :], T))
                for qq in range(NQ):
                    q0 = qq * (Ndim // NQ)
                    p.dma(w_hbm[:, q0:q0 + Ndim // NQ].rearrange(
                        "(c p) n -> p c n", p=128), v3(wf[:, :], Ndim // NQ))
                    for c in range(DC):
                        p.step("vector", lambda E, c=c, q0=q0: E.tensor_copy(
                            out=wbt[:, c * Ndim + q0:c * Ndim + q0 + Ndim // NQ],
                            in_=wf[:, c * (Ndim // NQ):(c + 1) * (Ndim // NQ)]),
                            r=(wf,), w=("%s_wbt%d" % (tag, qq % 4),))
                for tc in range(16):
                    O = outs[tc % len(outs)]
                    for nb in range(nbk):
                        for c in range(DC):
                            p.step("tensor", lambda E, tc=tc, c=c, nb=nb:
                                   E.matmul(
                                ps[nb][:, :],
                                xs[:, c * T + tc * 128:c * T + (tc + 1) * 128],
                                wbt[:, c * Ndim + nb * 512:
                                    c * Ndim + (nb + 1) * 512],
                                start=(c == 0), stop=(c == DC - 1)),
                                r=(xs, "%s_wbt0" % tag, "%s_wbt1" % tag,
                                   "%s_wbt2" % tag, "%s_wbt3" % tag),
                                w=(ps[nb],) if c == 0 else (),
                                accum=(ps[nb],) if c else ())
                        p.step("vector", lambda E, nb=nb, O=O: E.tensor_copy(
                            out=O[:, nb * 512:(nb + 1) * 512],
                            in_=ps[nb][:, :]), r=(ps[nb],), w=(O,))
                    p.dma(O[:, :Ndim], out_hbm[tc * 128:(tc + 1) * 128, :],
                          q="scalar")

    def ph_dequant(pk, sc, dst, Kdim, Ndim, tag):
        kch = Kdim // 128
        NS = min(Ndim, DQNS)
        dup, w2, w3, w4, w5, w6 = (dqb["d"], dqb["2"], dqb["3"], dqb["4"],
                                   dqb["5"], dqb["6"])
        sc4, sc4b, wout = dqb["s4"], dqb["sb"], dqb["o"]
        psX = dq_psA
        c_i = dup[:].bitcast(U16)
        s_ = w2[:].bitcast(U16)
        c7 = w3[:].bitcast(U16)
        z1 = w4[:].bitcast(U16)
        z2 = w5[:].bitcast(U16)
        w6i = w6[:].bitcast(U16)
        d8 = w3[:].bitcast(FP8)
        eng = "vector"
        st_ = lambda f, rr, ww: p.step(eng, f, r=rr, w=ww)
        for kc in range(kch):
            j0 = kc * 64
            for n0 in range(0, Ndim, NS):
                nsl = slice(n0, n0 + NS)
                p.dma(pk[j0:j0 + 64, nsl], dup[0:64, :NS])
                p.dma(pk[j0:j0 + 64, nsl], dup[64:128, :NS])
                p.dma(sc[kc * 4:kc * 4 + 4, nsl], sc4[:, :NS])
                p.step("vector", lambda E: E.tensor_copy(
                    out=sc4b[:, :NS], in_=sc4[:, :NS]), r=(sc4,), w=(sc4b,))
                st_(lambda E: E.tensor_scalar(
                    out=c_i[0:64, :], in0=c_i[0:64, :], scalar1=M0F,
                    scalar2=None, op0=ALU.bitwise_and), (dup,), (dup,))
                st_(lambda E: E.tensor_scalar(
                    out=c_i[64:128, :], in0=c_i[64:128, :], scalar1=4,
                    scalar2=M0F, op0=ALU.logical_shift_right,
                    op1=ALU.bitwise_and), (dup,), (dup,))
                st_(lambda E: E.tensor_scalar(
                    out=s_, in0=c_i, scalar1=M08, scalar2=4,
                    op0=ALU.bitwise_and, op1=ALU.logical_shift_left),
                    (dup,), (w2,))
                st_(lambda E: E.tensor_scalar(
                    out=c7, in0=c_i, scalar1=M07, scalar2=None,
                    op0=ALU.bitwise_and), (dup,), (w3,))
                st_(lambda E: E.tensor_scalar(
                    out=z1, in0=c7, scalar1=1, scalar2=None,
                    op0=ALU.logical_shift_right), (w3,), (w4,))
                st_(lambda E: E.tensor_scalar(
                    out=z2, in0=z1, scalar1=1, scalar2=None,
                    op0=ALU.logical_shift_right), (w4,), (w5,))
                st_(lambda E: E.tensor_tensor(
                    out=z1, in0=z1, in1=z2, op=ALU.bitwise_or),
                    (w4, w5), (w4,))
                st_(lambda E: E.tensor_scalar(
                    out=z2, in0=c7, scalar1=M01, scalar2=None,
                    op0=ALU.bitwise_and), (w3,), (w5,))
                st_(lambda E: E.tensor_scalar(
                    out=w6i, in0=z1, scalar1=M01, scalar2=None,
                    op0=ALU.bitwise_xor), (w4,), (w6,))
                st_(lambda E: E.tensor_tensor(
                    out=w6i, in0=z2, in1=w6i, op=ALU.bitwise_and),
                    (w5, w6), (w6,))
                st_(lambda E: E.tensor_tensor(
                    out=z2, in0=z1, in1=c7, op=ALU.bitwise_or),
                    (w4, w3), (w5,))
                st_(lambda E: E.tensor_scalar(
                    out=z2, in0=z2, scalar1=M01, scalar2=None,
                    op0=ALU.bitwise_and), (w5,), (w5,))
                st_(lambda E: E.tensor_tensor(
                    out=c7, in0=c7, in1=w6i, op=ALU.bitwise_xor),
                    (w3, w6), (w3,))
                st_(lambda E: E.tensor_scalar(
                    out=c7, in0=c7, scalar1=4, scalar2=None,
                    op0=ALU.mult), (w3,), (w3,))
                st_(lambda E: E.tensor_scalar(
                    out=z2, in0=z2, scalar1=48, scalar2=None,
                    op0=ALU.mult), (w5,), (w5,))
                st_(lambda E: E.tensor_tensor(
                    out=c7, in0=c7, in1=z2, op=ALU.add), (w3, w5), (w3,))
                st_(lambda E: E.tensor_tensor(
                    out=c7, in0=c7, in1=s_, op=ALU.bitwise_or),
                    (w3, w2), (w3,))
                for s5 in range(NS // 512):
                    ssl = slice(s5 * 512, (s5 + 1) * 512)
                    p.step("tensor", lambda E, ssl=ssl: E.matmul(
                        psX[:, :], e4g[:, :], sc4b[:, ssl],
                        start=True, stop=True), r=(e4g, sc4b), w=(psX,))
                    p.step("vector", lambda E, ssl=ssl: E.tensor_tensor(
                        out=wout[:, ssl], in0=d8[:, ssl], in1=psX[:, :],
                        op=ALU.mult), r=(w3, psX), w=(wout,))
                p.dma(wout[0:64, :NS],
                      dst[kc * 128:kc * 128 + 128:2, nsl], q="scalar")
                p.dma(wout[64:128, :NS],
                      dst[kc * 128 + 1:kc * 128 + 128:2, nsl], q="scalar")
                yield

    def em_round2(eng, v, t1, t2, w, vk, t1k, t2k):
        p.step(eng, lambda E: E.tensor_scalar(
            out=t1[:, :w], in0=v[:, :w], scalar1=128.5, scalar2=None,
            op0=ALU.add), r=(vk,), w=(t1k,))
        for kk in range(7, -1, -1):
            th = float(1 << kk)
            p.step(eng, lambda E, th=th: E.tensor_scalar(
                out=t2[:, :w], in0=t1[:, :w], scalar1=th, scalar2=th,
                op0=ALU.is_ge, op1=ALU.mult), r=(t1k,), w=(t2k,))
            p.step(eng, lambda E: E.tensor_sub(t1[:, :w], t1[:, :w],
                                               t2[:, :w]), r=(t1k, t2k),
                   w=(t1k,))
        p.step(eng, lambda E: E.tensor_sub(t2[:, :w], v[:, :w], t1[:, :w]),
               r=(vk, t1k), w=(t2k,))
        p.step(eng, lambda E: E.tensor_scalar(
            out=t2[:, :w], in0=t2[:, :w], scalar1=0.5, scalar2=None,
            op0=ALU.add), r=(t2k,), w=(t2k,))

    def ph_ternarize(w_hbm, dst, Kdim, Ndim, scal_idx, tag):
        kch = Kdim // 128
        NSW = 1024
        nel = float(Kdim * Ndim)
        with ExitStack() as st:
            sbl = lambda n, sh, dt: st.enter_context(nc.sbuf_tensor(tag + n, sh, dt))
            wfb = [sbl("_w%d" % i, [128, NSW], F32) for i in range(2)]
            t1b = [sbl("_t1%d" % i, [128, NSW], F32) for i in range(2)]
            t2b = [sbl("_t2%d" % i, [128, NSW], F32) for i in range(2)]
            colb = [sbl("_c%d" % i, [128, 1], F32) for i in range(2)]
            colacc = sbl("_ca", [128, 1], F32)
            srow = sbl("_sr", [1, 2], F32)
            scol = sbl("_sc", [128, 1], F32)
            obb = [sbl("_ob%d" % i, [128, NSW], BF16) for i in range(2)]
            with nc.psum_tensor(tag + "_p", [1, 512], F32) as ps1, \
                 nc.psum_tensor(tag + "_pb", [128, 512], F32) as psb:
                p.step("vector", lambda E: E.memset(colacc[:, :], 0.0),
                       w=(colacc,))
                it = 0
                for kc in range(kch):
                    for n0 in range(0, Ndim, NSW):
                        w = min(NSW, Ndim - n0)
                        pr = it % 2
                        wf, col = wfb[pr], colb[pr]
                        p.dma(w_hbm[kc * 128:(kc + 1) * 128, n0:n0 + w],
                              wf[:, :w])
                        p.step("vector", lambda E, w=w, wf=wf, col=col:
                               E.reduce_sum(col[:, :], wf[:, :w], axis=AX.X,
                                            apply_absolute_value=True),
                               r=(wf,), w=(col,))
                        p.step("vector", lambda E, col=col: E.tensor_add(
                            colacc[:, :], colacc[:, :], col[:, :]),
                            r=(col, colacc), w=(colacc,))
                        it += 1
                        yield
                p.step("tensor", lambda E: E.matmul(
                    ps1[0:1, 0:1], ones_col[:, :], colacc[:, :],
                    start=True, stop=True), r=(ones_col, colacc), w=(ps1,))
                p.step("vector", lambda E: E.tensor_scalar_mul(
                    srow[0:1, 0:1], ps1[0:1, 0:1], 1.0 / nel),
                    r=(ps1,), w=(srow,))
                p.step("vector", lambda E: E.tensor_scalar_max(
                    srow[0:1, 0:1], srow[0:1, 0:1], 1e-5), r=(srow,), w=(srow,))
                p.dma(srow[0:1, 0:1], kt["s_wsc"][0:1, scal_idx:scal_idx + 1],
                      q="scalar")
                p.step("vector", lambda E: E.reciprocal(
                    srow[0:1, 1:2], srow[0:1, 0:1]), r=(srow,), w=(srow,))
                p.step("tensor", lambda E: E.matmul(
                    psb[:, 0:1], ones_row[:, :], srow[0:1, 1:2],
                    start=True, stop=True), r=(ones_row, srow), w=(psb,))
                p.step("vector", lambda E: E.tensor_copy(
                    out=scol[:, :], in_=psb[:, 0:1]), r=(psb,), w=(scol,))
                it = 0
                for kc in range(kch):
                    for n0 in range(0, Ndim, NSW):
                        w = min(NSW, Ndim - n0)
                        pr = it % 2
                        eng = "vector"
                        wf, t1, t2, ob = wfb[pr], t1b[pr], t2b[pr], obb[pr]
                        p.dma(w_hbm[kc * 128:(kc + 1) * 128, n0:n0 + w],
                              wf[:, :w])
                        p.step(eng, lambda E, w=w, wf=wf: E.tensor_scalar(
                            out=wf[:, :w], in0=wf[:, :w], scalar1=scol[:, 0:1],
                            scalar2=None, op0=ALU.mult), r=(wf, scol), w=(wf,))
                        p.step(eng, lambda E, w=w, wf=wf, t1=t1:
                               E.tensor_scalar(
                            out=t1[:, :w], in0=wf[:, :w], scalar1=0.5,
                            scalar2=None, op0=ALU.is_ge), r=(wf,), w=(t1,))
                        p.step(eng, lambda E, w=w, wf=wf, t2=t2:
                               E.tensor_scalar(
                            out=t2[:, :w], in0=wf[:, :w], scalar1=-0.5,
                            scalar2=None, op0=ALU.is_le), r=(wf,), w=(t2,))
                        p.step(eng, lambda E, w=w, t1=t1, t2=t2, ob=ob:
                               E.tensor_sub(ob[:, :w], t1[:, :w], t2[:, :w]),
                            r=(t1, t2), w=(ob,))
                        p.dma(ob[:, :w], dst[kc * 128:(kc + 1) * 128,
                                             n0:n0 + w], q="scalar")
                        it += 1
                        yield

    def ph_rope(buf, Din, tag):
        dch = Din // 128
        with ExitStack() as st:
            sbl = lambda n, sh, dt: st.enter_context(nc.sbuf_tensor(tag + n, sh, dt))
            qsb = [sbl("_q%d" % i, [128, 512], BF16) for i in range(2)]
            cosb = sbl("_c", [128, T], F32)
            sinb = sbl("_s", [128, T], F32)
            t1b = [sbl("_1%d" % i, [128, 512], F32) for i in range(2)]
            t2b = [sbl("_2%d" % i, [128, 512], F32) for i in range(2)]
            obb = [sbl("_o%d" % i, [128, 512], BF16) for i in range(2)]
            with nc.psum_tensor(tag + "_pA", [128, 512], F32) as psA, \
                 nc.psum_tensor(tag + "_pB", [128, 512], F32) as psB:
                psr = [psA, psB]
                p.dma(kt["cosT"], cosb[:, :])
                p.dma(kt["sinT"], sinb[:, :])
                it = 0
                for c in range(dch):
                    for tt in range(TT):
                        pr = it % 2
                        qs, t1, t2, ob, pp = (qsb[pr], t1b[pr], t2b[pr],
                                              obb[pr], psr[pr])
                        tsl = slice(tt * 512, (tt + 1) * 512)
                        p.dma(buf[c * 128:(c + 1) * 128, tsl], qs[:, :])
                        p.step("tensor", lambda E, qs=qs, pp=pp: E.matmul(
                            pp[:, :], rmat[:, :], qs[:, :],
                            start=True, stop=True), r=(rmat, qs), w=(pp,))
                        p.step("vector", lambda E, tsl=tsl, qs=qs, t1=t1:
                               E.tensor_mul(t1[:, :], qs[:, :], cosb[:, tsl]),
                               r=(qs, cosb), w=(t1,))
                        p.step("vector", lambda E, tsl=tsl, t2=t2, pp=pp:
                               E.tensor_mul(t2[:, :], pp[:, :], sinb[:, tsl]),
                               r=(pp, sinb), w=(t2,))
                        p.step("vector", lambda E, ob=ob, t1=t1, t2=t2:
                               E.tensor_add(ob[:, :], t1[:, :], t2[:, :]),
                               r=(t1, t2), w=(ob,))
                        p.dma(ob[:, :], buf[c * 128:(c + 1) * 128, tsl],
                              q="scalar")
                        it += 1

    def ph_attn(tag):
        with ExitStack() as st:
            sbl = lambda n, sh, dt: st.enter_context(nc.sbuf_tensor(tag + n, sh, dt))
            qh = [sbl("_q%d" % i, [128, 512], BF16) for i in range(2)]
            kh = [sbl("_k%d" % i, [128, T], BF16) for i in range(2)]
            vt = [sbl("_vt%d" % i, [128, 128 * 16], BF16) for i in range(2)]
            at = [sbl("_a%d" % i, [128, 512], BF16) for i in range(2)]
            msk = sbl("_m", [128, 2048], BF16)
            ob = [sbl("_o%d" % i, [128, 512], BF16) for i in range(2)]
            rd = sbl("_r", [1, 512], F32)
            rdb = sbl("_rdb", [128, 512], F32)
            with nc.psum_tensor(tag + "_s0", [128, 512], F32) as ps_s0, \
                 nc.psum_tensor(tag + "_s1", [128, 512], F32) as ps_s1, \
                 nc.psum_tensor(tag + "_po", [128, 512], F32) as ps_o, \
                 nc.psum_tensor(tag + "_pd", [1, 512], F32) as ps_d, \
                 nc.psum_tensor(tag + "_pb", [128, 512], F32) as ps_b:
                ps_s = [ps_s0, ps_s1]
                for r in range(4):
                    p.dma(kt["amask"][r * 128:(r + 1) * 128, :],
                          msk[:, r * 512:(r + 1) * 512])
                for h in range(16):
                    kv = h // 4
                    KH, VT = kh[kv % 2], vt[kv % 2]
                    if h % 4 == 0:
                        p.dma(kt["s_k"][kv * 128:(kv + 1) * 128, :], KH[:, :])
                        p.dma(kt["s_v"][:, kv * 128:(kv + 1) * 128].rearrange(
                            "(b p) n -> p b n", p=128), v3(VT[:, :], 128))
                    for qt in range(TT):
                        QH = qh[qt % 2]
                        OB = ob[qt % 2]
                        qsl = slice(qt * 512, (qt + 1) * 512)
                        p.dma(kt["s_q"][h * 128:(h + 1) * 128, qsl], QH[:, :])
                        nkc = 4 * qt + 4
                        for kc in range(nkc):
                            SS = ps_s[kc % 2]
                            AT = at[kc % 2]
                            p.step("tensor", lambda E, kc=kc, KH=KH, QH=QH,
                                   SS=SS: E.matmul(
                                SS[:, :], KH[:, kc * 128:(kc + 1) * 128],
                                QH[:, :], start=True, stop=True),
                                r=(KH, QH), w=(SS,))
                            p.step("scalar", lambda E, SS=SS, AT=AT:
                                   E.activation(
                                out=AT[:, :], in_=SS[:, :], func=AF.Exp,
                                bias=-MSH, scale=ISQ), r=(SS, cst), w=(AT,))
                            r = kc - 4 * qt
                            if r >= 0:
                                p.step("vector", lambda E, r=r, AT=AT:
                                       E.tensor_mul(
                                    AT[:, :], AT[:, :],
                                    msk[:, r * 512:(r + 1) * 512]),
                                    r=(AT, msk), w=(AT,))
                            p.step("tensor", lambda E, kc=kc, nkc=nkc, VT=VT,
                                   AT=AT: E.matmul(
                                ps_o[:, :], VT[:, kc * 128:(kc + 1) * 128],
                                AT[:, :], start=(kc == 0), stop=(kc == nkc - 1)),
                                r=(VT, AT),
                                w=(ps_o,) if kc == 0 else (),
                                accum=(ps_o,) if kc else ())
                            p.step("tensor", lambda E, kc=kc, nkc=nkc, AT=AT:
                                   E.matmul(
                                ps_d[:, :], ones_col_bf[:, :], AT[:, :],
                                start=(kc == 0), stop=(kc == nkc - 1)),
                                r=(ones_col_bf, AT),
                                w=(ps_d,) if kc == 0 else (),
                                accum=(ps_d,) if kc else ())
                        p.step("vector", lambda E: E.reciprocal(
                            rd[:, :], ps_d[0:1, :]), r=(ps_d,), w=(rd,))
                        p.step("tensor", lambda E: E.matmul(
                            ps_b[:, :], ones_row[:, :], rd[:, :],
                            start=True, stop=True), r=(ones_row, rd), w=(ps_b,))
                        p.step("vector", lambda E: E.tensor_copy(
                            out=rdb[:, :], in_=ps_b[:, :]), r=(ps_b,), w=(rdb,))
                        p.step("vector", lambda E, OB=OB: E.tensor_mul(
                            OB[:, :], ps_o[:, :], rdb[:, :]),
                            r=(ps_o, rdb), w=(OB,))
                        p.dma(OB[:, :], kt["s_o"][h * 128:(h + 1) * 128, qsl],
                              q="scalar")

    def ph_gla_prep(tag):
        with ExitStack() as st:
            sbl = lambda n, sh, dt: st.enter_context(nc.sbuf_tensor(tag + n, sh, dt))
            ab = [sbl("_a%d" % i, [128, T], F32) for i in range(2)]
            bbb = [sbl("_b%d" % i, [128, T], F32) for i in range(2)]
            cb = [sbl("_c%d" % i, [128, T], F32) for i in range(2)]
            db = [sbl("_d%d" % i, [128, T], F32) for i in range(2)]
            zz = sbl("_z", [128, T], F32)
            obb = [sbl("_o%d" % i, [128, T], BF16) for i in range(2)]
            ob2b = [sbl("_p%d" % i, [128, T], BF16) for i in range(2)]
            with nc.psum_tensor(tag + "_ptA", [128, 128], BF16) as pstA, \
                 nc.psum_tensor(tag + "_ptB", [128, 128], BF16) as pstB:
                pst = [pstA, pstB]
                p.step("vector", lambda E: E.memset(zz[:, :], 0.0), w=(zz,))
                for c in range(DC):
                    pr = c % 2
                    a, b, c_, d_, obx, ob2 = (ab[pr], bbb[pr], cb[pr], db[pr],
                                              obb[pr], ob2b[pr])
                    PT = pst[pr]
                    rs = slice(c * 128, (c + 1) * 128)
                    p.dma(kt["s_t1"][rs, :], a[:, :])
                    p.step("scalar", lambda E, a=a, obx=obx: E.activation(
                        out=obx[:, :], in_=a[:, :], func=AF.Relu, scale=ISQ),
                        r=(a, cst), w=(obx,))
                    p.dma(obx[:, :], kt["s_q"][rs, :], q="scalar")
                    p.dma(kt["s_kr"][rs, :], a[:, :])
                    p.step("scalar", lambda E, a=a, b=b: E.activation(
                        out=b[:, :], in_=a[:, :], func=AF.Sigmoid),
                        r=(a,), w=(b,))
                    p.step("scalar", lambda E, a=a, c_=c_: E.activation(
                        out=c_[:, :], in_=a[:, :], func=AF.Sigmoid, scale=-1.0),
                        r=(a, cst), w=(c_,))
                    p.step("scalar", lambda E, c_=c_: E.activation(
                        out=c_[:, :], in_=c_[:, :], func=AF.Ln),
                        r=(c_,), w=(c_,))
                    p.step("vector", lambda E, c_=c_, d_=d_: E.tensor_tensor_scan(
                        out=d_[:, :], data0=c_[:, :], data1=zz[:, :],
                        initial=0.0, op0=ALU.add, op1=ALU.add),
                        r=(c_, zz), w=(d_,))
                    p.step("vector", lambda E, a=a, d_=d_: E.tensor_copy(
                        out=a[:, 0:32], in_=d_[:, 0:32]), r=(d_,), w=(a,))
                    p.step("vector", lambda E, a=a, d_=d_: E.tensor_tensor(
                        out=a[:, 32:T].rearrange("p (bb cc) -> p bb cc", cc=32),
                        in0=d_[:, 32:T].rearrange("p (bb cc) -> p bb cc", cc=32),
                        in1=d_[:, 31:T - 32:32].unsqueeze(2).broadcast_to(
                            [128, T // 32 - 1, 32]),
                        op=ALU.subtract), r=(d_,), w=(a,))
                    p.step("vector", lambda E, a=a: E.tensor_scalar_max(
                        a[:, :], a[:, :], CLAMP), r=(a,), w=(a,))
                    p.step("scalar", lambda E, a=a, d_=d_: E.activation(
                        out=d_[:, 0:NCHUNK], in_=a[:, 31:T:32], func=AF.Exp),
                        r=(a,), w=(d_,))
                    p.dma(d_[:, 0:NCHUNK], kt["s_lam"][rs, :], q="scalar")
                    p.step("scalar", lambda E, a=a, c_=c_: E.activation(
                        out=c_[:, :], in_=a[:, :], func=AF.Exp),
                        r=(a,), w=(c_,))
                    p.dma(kt["s_q"][rs, :], ob2[:, :])
                    p.step("vector", lambda E, obx=obx, c_=c_, ob2=ob2:
                           E.tensor_mul(obx[:, :], c_[:, :], ob2[:, :]),
                           r=(c_, ob2), w=(obx,))
                    p.dma(obx[:, :], kt["s_gq"][rs, :], q="scalar")
                    p.step("scalar", lambda E, a=a, c_=c_: E.activation(
                        out=c_[:, :], in_=a[:, :], func=AF.Exp, scale=-1.0),
                        r=(a, cst), w=(c_,))
                    p.step("vector", lambda E, ob2=ob2, c_=c_, b=b:
                           E.tensor_mul(ob2[:, :], c_[:, :], b[:, :]),
                           r=(c_, b), w=(ob2,))
                    p.dma(ob2[:, :], kt["s_gk"][rs, :], q="scalar")
                    p.step("vector", lambda E, a=a, c_=c_: E.tensor_tensor(
                        out=c_[:, :].rearrange("p (bb cc) -> p bb cc", cc=32),
                        in0=a[:, 31:T:32].unsqueeze(2).broadcast_to(
                            [128, T // 32, 32]),
                        in1=a[:, :].rearrange("p (bb cc) -> p bb cc", cc=32),
                        op=ALU.subtract), r=(a,), w=(c_,))
                    p.step("scalar", lambda E, c_=c_: E.activation(
                        out=c_[:, :], in_=c_[:, :], func=AF.Exp),
                        r=(c_,), w=(c_,))
                    p.step("vector", lambda E, obx=obx, c_=c_, b=b:
                           E.tensor_mul(obx[:, :], c_[:, :], b[:, :]),
                           r=(c_, b), w=(obx,))
                    for tc in range(16):
                        p.step("tensor", lambda E, tc=tc, obx=obx, PT=PT:
                               E.transpose(PT[:, :],
                                           obx[:, tc * 128:(tc + 1) * 128],
                                           ident[:, :]),
                               r=(obx, ident), w=(PT,))
                        p.step("vector", lambda E, ob2=ob2, PT=PT, tc=tc:
                               E.tensor_copy(
                            out=ob2[:, tc * 128:(tc + 1) * 128], in_=PT[:, :]),
                            r=(PT,), w=(ob2,))
                    p.dma(v3(ob2[:, :], 128),
                          kt["s_khT"][:, rs].rearrange("(b q) n -> q b n",
                                                       q=128), q="scalar")

    def ph_gla_scan(tag):
        with ExitStack() as st:
            sbl = lambda n, sh, dt: st.enter_context(nc.sbuf_tensor(tag + n, sh, dt))
            Sf = [sbl("_Sf%d" % i, [128, 128], F32) for i in range(2)]
            Sb = [sbl("_Sb%d" % i, [128, 128], BF16) for i in range(2)]
            gq = [sbl("_gq%d" % i, [128, T], BF16) for i in range(2)]
            gk = [sbl("_gk%d" % i, [128, T], BF16) for i in range(2)]
            khT = [sbl("_kT%d" % i, [32, NCHUNK * 128], BF16) for i in range(2)]
            vg = [sbl("_vg%d" % i, [32, NCHUNK * 128], BF16) for i in range(2)]
            lam = [sbl("_lm%d" % i, [128, NCHUNK], F32) for i in range(2)]
            pm = [sbl("_pm%d" % i, [32, 32], BF16) for i in range(2)]
            ob = [sbl("_ob%d" % i, [128, 512], BF16) for i in range(2)]
            with ExitStack() as pstk:
                pp = lambda n: pstk.enter_context(
                    nc.psum_tensor(tag + n, [32, 32], F32))
                po = lambda n, sh: pstk.enter_context(
                    nc.psum_tensor(tag + n, sh, F32))
                ps_p = [pp("_pp0"), pp("_pp1")]
                ps_o = [po("_po0", [128, 512]), po("_po1", [128, 512])]
                _pu = po("_pu0", [128, 128])
                ps_u = [_pu, _pu]
                for h in range(16):
                    pr = h % 2
                    GQ, GK, KT, VG, LM = (gq[pr], gk[pr], khT[pr], vg[pr],
                                          lam[pr])
                    SF, SB, PM, OB = Sf[pr], Sb[pr], pm[pr], ob[pr]
                    PP, PO, PU = ps_p[pr], ps_o[pr], ps_u[pr]
                    rs = slice(h * 128, (h + 1) * 128)
                    p.dma(kt["s_gq"][rs, :], GQ[:, :])
                    p.dma(kt["s_gk"][rs, :], GK[:, :])
                    p.dma(kt["s_lam"][rs, :], LM[:, :])
                    p.dma(kt["s_khT"][:, rs].rearrange(
                        "(bb pp2) n -> pp2 bb n", pp2=32), v3(KT[:, :], 128))
                    p.dma(kt["s_vg"][:, rs].rearrange(
                        "(bb pp2) n -> pp2 bb n", pp2=32), v3(VG[:, :], 128))
                    p.step("vector", lambda E, SF=SF: E.memset(SF[:, :], 0.0),
                           w=(SF,))
                    p.step("vector", lambda E, SB=SB: E.memset(SB[:, :], 0.0),
                           w=(SB,))
                    for B in range(NCHUNK):
                        tb = slice(B * 32, (B + 1) * 32)
                        bsl = slice(B * 128, (B + 1) * 128)
                        osl = slice((B % 16) * 32, (B % 16) * 32 + 32)
                        p.step("tensor", lambda E, tb=tb, GK=GK, GQ=GQ, PP=PP:
                               E.matmul(PP[:, :], GK[:, tb], GQ[:, tb],
                                        start=True, stop=True),
                               r=(GK, GQ), w=(PP,))
                        p.step("vector", lambda E, PM=PM, PP=PP: E.tensor_mul(
                            PM[:, :], PP[:, :], gmask[:, :]),
                            r=(PP, gmask), w=(PM,))
                        p.step("tensor", lambda E, tb=tb, osl=osl, SB=SB,
                               GQ=GQ, PO=PO: E.matmul(
                            PO[:, osl], SB[:, :], GQ[:, tb],
                            start=True, stop=False),
                            r=(SB, GQ), w=(PO,) if B % 16 == 0 else (),
                            accum=(PO,) if B % 16 else ())
                        p.step("tensor", lambda E, bsl=bsl, osl=osl, VG=VG,
                               PM=PM, PO=PO: E.matmul(
                            PO[:, osl], VG[:, bsl], PM[:, :],
                            start=False, stop=True),
                            r=(VG, PM), accum=(PO,))
                        p.step("tensor", lambda E, bsl=bsl, KT=KT, VG=VG,
                               PU=PU: E.matmul(
                            PU[:, :], KT[:, bsl], VG[:, bsl],
                            start=True, stop=True), r=(KT, VG), w=(PU,))
                        p.step("vector", lambda E, B=B, SF=SF, LM=LM, PU=PU:
                               E.scalar_tensor_tensor(
                            out=SF[:, :], in0=SF[:, :], scalar=LM[:, B:B + 1],
                            in1=PU[:, :], op0=ALU.mult, op1=ALU.add),
                            r=(SF, LM, PU), w=(SF,))
                        p.step("vector", lambda E, SB=SB, SF=SF: E.tensor_copy(
                            out=SB[:, :], in_=SF[:, :]), r=(SF,), w=(SB,))
                        if B % 16 == 15:
                            p.step("vector", lambda E, OB=OB, PO=PO:
                                   E.tensor_copy(out=OB[:, :], in_=PO[:, :]),
                                   r=(PO,), w=(OB,))
                            t0 = (B // 16) * 512
                            p.dma(OB[:, :], kt["s_o"][rs, t0:t0 + 512],
                                  q="scalar")

    def ph_quant(src, dst, rq_row_idx, sw_idx, Din, tag):
        dch = Din // 128
        GW = 2048
        NBUF = 2 if dch <= 16 else 1
        with ExitStack() as st:
            sbl = lambda n, sh, dt: st.enter_context(nc.sbuf_tensor(tag + n, sh, dt))
            xs = [sbl("_x%d" % i, [128, dch * 512], F32) for i in range(NBUF)]
            mx = sbl("_m", [128, 512], F32)
            mn = sbl("_mn", [128, 512], F32)
            mxb = sbl("_mb", [128, 512], BF16)
            colr = sbl("_cr", [128, 4], F32)
            colb = sbl("_cb", [128, 4], BF16)
            rowb = sbl("_rb", [1, 512], F32)
            rowq = sbl("_rq", [1, 512], F32)
            swc = sbl("_sw", [1, 1], F32)
            sxb = sbl("_sx", [128, 512], F32)
            t1b = [sbl("_t1%d" % i, [128, GW], F32) for i in range(2)]
            t2b = [sbl("_t2%d" % i, [128, GW], F32) for i in range(2)]
            obb = [sbl("_obq%d" % i, [128, GW], BF16) for i in range(2)]
            with nc.psum_tensor(tag + "_pt", [128, 128], BF16) as pst, \
                 nc.psum_tensor(tag + "_pr", [1, 512], F32) as psr, \
                 nc.psum_tensor(tag + "_pb", [128, 512], F32) as psb:
                p.dma(kt["s_wsc"][0:1, sw_idx:sw_idx + 1], swc[:, :])
                for tt in range(TT):
                    X = xs[tt % NBUF]
                    tsl = slice(tt * 512, (tt + 1) * 512)
                    p.dma(fmv(src, tsl), v3(X[:, :], 512))
                    p.step("vector", lambda E, X=X: E.tensor_copy(
                        out=mx[:, :], in_=X[:, 0:512]), r=(X,), w=(mx,))
                    p.step("vector", lambda E, X=X: E.tensor_copy(
                        out=mn[:, :], in_=X[:, 0:512]), r=(X,), w=(mn,))
                    for c in range(1, dch):
                        csl = slice(c * 512, (c + 1) * 512)
                        p.step("vector", lambda E, csl=csl, X=X: E.tensor_tensor(
                            out=mx[:, :], in0=mx[:, :], in1=X[:, csl],
                            op=ALU.max), r=(mx, X), w=(mx,))
                        p.step("vector", lambda E, csl=csl, X=X: E.tensor_tensor(
                            out=mn[:, :], in0=mn[:, :], in1=X[:, csl],
                            op=ALU.min), r=(mn, X), w=(mn,))
                    p.step("vector", lambda E: E.tensor_scalar_mul(
                        mn[:, :], mn[:, :], -1.0), r=(mn,), w=(mn,))
                    p.step("vector", lambda E: E.tensor_tensor(
                        out=mx[:, :], in0=mx[:, :], in1=mn[:, :], op=ALU.max),
                        r=(mx, mn), w=(mx,))
                    p.step("vector", lambda E: E.tensor_copy(
                        out=mxb[:, :], in_=mx[:, :]), r=(mx,), w=(mxb,))
                    for q4 in range(4):
                        p.step("tensor", lambda E, q4=q4: E.transpose(
                            pst[:, :], mxb[:, q4 * 128:(q4 + 1) * 128],
                            ident[:, :]), r=(mxb, ident), w=(pst,))
                        p.step("vector", lambda E, q4=q4: E.reduce_max(
                            colr[:, q4:q4 + 1], pst[:, :], axis=AX.X,
                            apply_absolute_value=True), r=(pst,), w=(colr,))
                    for q4 in range(4):
                        p.step("vector", lambda E, q4=q4: E.tensor_copy(
                            out=colb[:, q4:q4 + 1], in_=colr[:, q4:q4 + 1]),
                            r=(colr,), w=(colb,))
                        p.step("tensor", lambda E, q4=q4: E.matmul(
                            psr[0:1, q4 * 128:(q4 + 1) * 128],
                            colb[:, q4:q4 + 1], ident[:, :],
                            start=True, stop=True), r=(colb, ident), w=(psr,))
                    p.step("vector", lambda E: E.tensor_scalar_max(
                        rowb[:, :], psr[0:1, :], 1e-5), r=(psr,), w=(rowb,))
                    p.step("vector", lambda E: E.reciprocal(
                        rowq[:, :], rowb[:, :]), r=(rowb,), w=(rowq,))
                    p.step("vector", lambda E: E.tensor_scalar_mul(
                        rowq[:, :], rowq[:, :], 127.0), r=(rowq,), w=(rowq,))
                    p.step("vector", lambda E: E.tensor_scalar(
                        out=rowb[:, :], in0=rowb[:, :], scalar1=swc[0:1, 0:1],
                        scalar2=1.0 / 127.0, op0=ALU.mult, op1=ALU.mult),
                        r=(rowb, swc), w=(rowb,))
                    p.dma(rowb[:, :],
                          kt["s_rows"][rq_row_idx:rq_row_idx + 1, tsl],
                          q="scalar")
                    p.step("vector", lambda E: E.tensor_copy(
                        out=colb[:, 0:1] if False else mxb[0:1, 0:512],
                        in_=rowb[:, :]), r=(rowb,), w=(mxb,))
                    p.dma(mxb[0:1, 0:512],
                          kt["s_rows_bf"][rq_row_idx:rq_row_idx + 1, tsl],
                          q="scalar")
                    p.step("tensor", lambda E: E.matmul(
                        psb[:, :], ones_row[:, :], rowq[0:1, :],
                        start=True, stop=True), r=(ones_row, rowq), w=(psb,))
                    p.step("vector", lambda E: E.tensor_copy(
                        out=sxb[:, :], in_=psb[:, :]), r=(psb,), w=(sxb,))
                    for c in range(dch):
                        csl = slice(c * 512, (c + 1) * 512)
                        p.step("vector", lambda E, csl=csl, X=X: E.tensor_mul(
                            X[:, csl], X[:, csl], sxb[:, :]),
                            r=(X, sxb), w=(X,))
                    ngr = (dch * 512) // GW
                    for g in range(ngr):
                        pr2 = g % 2
                        eng = "vector"
                        t1, t2, obq = t1b[pr2], t2b[pr2], obb[pr2]
                        gsl = slice(g * GW, (g + 1) * GW)
                        xv = X[:, gsl]
                        em_round2(eng, xv, t1, t2, GW, X, t1, t2)
                        p.step(eng, lambda E, t2=t2, obq=obq: E.tensor_copy(
                            out=obq[:, :], in_=t2[:, :]), r=(t2,), w=(obq,))
                        c0 = gsl.start // 512
                        c1 = gsl.stop // 512
                        p.dma(v3(obq[:, :], 512),
                              dst[c0 * 128:c1 * 128, tsl].rearrange(
                                  "(c p) t -> p c t", p=128), q="scalar")

    def ph_gelu_mult(tag):
        with ExitStack() as st:
            sbl = lambda n, sh, dt: st.enter_context(nc.sbuf_tensor(tag + n, sh, dt))
            ab = [sbl("_a%d" % i, [128, T], F32) for i in range(2)]
            bb2 = [sbl("_b%d" % i, [128, T], F32) for i in range(2)]
            obb = [sbl("_o%d" % i, [128, T], BF16) for i in range(2)]
            for c in range(128):
                pr = c % 2
                eng = "vector"
                a, b, ob = ab[pr], bb2[pr], obb[pr]
                rs = slice(c * 128, (c + 1) * 128)
                p.dma(kt["s_y1"][rs, :], a[:, :])
                p.dma(kt["s_y2"][rs, :], b[:, :])
                p.step("scalar", lambda E, a=a: E.activation(
                    out=a[:, :], in_=a[:, :], func=AF.Gelu_apprx_tanh),
                    r=(a,), w=(a,))
                p.step(eng, lambda E, ob=ob, a=a, b=b: E.tensor_mul(
                    ob[:, :], a[:, :], b[:, :]), r=(a, b), w=(ob,))
                p.dma(ob[:, :], kt["s_int"][rs, :], q="scalar")

    def ph_silu_mult(tag):
        with ExitStack() as st:
            sbl = lambda n, sh, dt: st.enter_context(nc.sbuf_tensor(tag + n, sh, dt))
            ab = [sbl("_a%d" % i, [128, T], F32) for i in range(2)]
            bb2 = [sbl("_b%d" % i, [128, T], F32) for i in range(2)]
            obb = [sbl("_o%d" % i, [128, T], F32) for i in range(2)]
            for c in range(44):
                pr = c % 2
                eng = "vector"
                a, b, ob = ab[pr], bb2[pr], obb[pr]
                p.dma(kt["s_y1"][c * 128:(c + 1) * 128, :], a[:, :])
                p.dma(kt["s_y1"][(44 + c) * 128:(45 + c) * 128, :], b[:, :])
                p.step("scalar", lambda E, a=a: E.activation(
                    out=a[:, :], in_=a[:, :], func=AF.Silu), r=(a,), w=(a,))
                p.step(eng, lambda E, ob=ob, a=a, b=b: E.tensor_mul(
                    ob[:, :], a[:, :], b[:, :]), r=(a, b), w=(ob,))
                p.dma(ob[:, :], kt["s_h"][c * 128:(c + 1) * 128, :], q="scalar")

    def pump(g, n):
        if g is None:
            return None
        p.exempt = True
        try:
            for _ in range(n):
                next(g)
        except StopIteration:
            g = None
        p.exempt = False
        return g

    def exhaust(g):
        g = pump(g, 10 ** 9)
        p.barrier_full()
        return None

    gen_g = ph_dequant(kt["PKg"], kt["SCg"], kt["s_wgq"], 2048, 16384, "dqg")
    gen_u = ph_dequant(kt["PKu"], kt["SCu"], kt["s_wuq"], 2048, 16384, "dqu")
    gen_d = ph_dequant(kt["PKd"], kt["SCd"], kt["s_wdq"], 16384, 2048, "dqd")
    gen_t1 = None
    gen_t2 = None

    for s in range(NSEQ):
        xin = kt["xT"][s]
        xcur = kt["s_x"]
        sfx = "%d" % s
        ph_layernorm(xin, kt["s_xn"], D, tag="ln0s" + sfx)
        p.barrier()
        gen_g = pump(gen_g, 8)
        ph_linear(kt["Wq"], kt["s_xn"], kt["s_q"], D, D, "pq" + sfx)
        p.barrier()
        gen_g = pump(gen_g, 12)
        ph_linear(kt["Wk"], kt["s_xn"], kt["s_k"], D, 512, "pk" + sfx)
        p.barrier()
        gen_g = pump(gen_g, 6)
        ph_linear_tok(kt["Wv"], kt["s_xn"], kt["s_v"], 512, "pv" + sfx)
        p.barrier()
        gen_g = pump(gen_g, 8)
        ph_rope(kt["s_q"], D, "rpq" + sfx)
        p.barrier()
        gen_g = pump(gen_g, 6)
        ph_rope(kt["s_k"], 512, "rpk" + sfx)
        p.barrier()
        gen_g = pump(gen_g, 8)
        ph_attn("at" + sfx)
        p.barrier()
        gen_g = pump(gen_g, 16)
        gen_u = pump(gen_u, 40)
        ph_linear(kt["Wo"], kt["s_o"], kt["s_z"], D, D, "po" + sfx,
                  out_dtype=F32)
        p.barrier()
        gen_u = pump(gen_u, 12)
        ph_resadd(xin, kt["s_z"], xcur, "ra0" + sfx)
        p.barrier()
        gen_d = pump(gen_d, 12)
        if DEBUG and s == 0:
            p.dma(xcur[:, :], kt["dbg0"][:, :])
        ph_layernorm(xcur, kt["s_xn"], D, tag="ln1s" + sfx)
        p.barrier()
        if s == 0:
            gen_g = exhaust(gen_g)
            gen_u = exhaust(gen_u)
        ph_linear_multi([(kt["s_wgq"], kt["s_y1"], "y1"),
                         (kt["s_wuq"], kt["s_y2"], "y2")],
                        kt["s_xn"], D, 16384, "fgu" + sfx,
                        post=lambda: em_gelu_mult("gm" + sfx))
        p.barrier()
        gen_d = pump(gen_d, 30)
        if s == 0:
            gen_d = exhaust(gen_d)
            dq_stk.close()
        ph_linear(kt["s_wdq"], kt["s_int"], kt["s_z"], 16384, D,
                  "fd" + sfx, w_dtype=BF16, out_dtype=F32)
        p.barrier()
        if s == 0:
            gen_t1 = ph_ternarize(kt["GWT"], kt["s_g8"], D, 11264, 0, "tg")
        gen_t1 = pump(gen_t1, 30)
        ph_resadd(xcur, kt["s_z"], xcur, "ra1" + sfx)
        p.barrier()
        if DEBUG and s == 0:
            p.dma(xcur[:, :], kt["dbg1"][:, :])
        ph_layernorm(xcur, kt["s_xn"], D, tag="ln2s" + sfx)
        p.barrier()
        gen_t1 = pump(gen_t1, 20)
        ph_linear(kt["W1q"], kt["s_xn"], kt["s_t1"], D, D, "gq" + sfx,
                  out_dtype=F32)
        p.barrier()
        gen_t1 = pump(gen_t1, 20)
        ph_linear(kt["W1k"], kt["s_xn"], kt["s_kr"], D, D, "gk" + sfx,
                  out_dtype=F32)
        p.barrier()
        gen_t1 = pump(gen_t1, 20)
        ph_linear_tok(kt["W1i"], kt["s_xn"], kt["s_vg"], D, "gv" + sfx)
        p.barrier()
        gen_t1 = pump(gen_t1, 20)
        ph_gla_prep("gp" + sfx)
        p.barrier()
        gen_t1 = pump(gen_t1, 30)
        ph_gla_scan("gs" + sfx)
        p.barrier()
        if s == 0:
            gen_t1 = exhaust(gen_t1)
            gen_t2 = ph_ternarize(kt["DWT"], kt["s_d8"], 5632, D, 1, "td")
            gen_t2 = exhaust(gen_t2)
        ph_layernorm(kt["s_o"], kt["s_o"], D, src_dtype=BF16,
                     mean_center=False, gamma_hbm=kt["gw"], tag="gn" + sfx)
        p.barrier()
        ph_linear(kt["W1o"], kt["s_o"], kt["s_z"], D, D, "go" + sfx,
                  out_dtype=F32)
        p.barrier()
        ph_resadd(xcur, kt["s_z"], xcur, "ra2" + sfx)
        p.barrier()
        if DEBUG and s == 0:
            p.dma(xcur[:, :], kt["dbg2"][:, :])
        ph_layernorm(xcur, kt["s_t1"], D, out_dtype=F32, tag="ln3s" + sfx)
        p.barrier()
        ph_layernorm(kt["s_t1"], kt["s_xr"], D, mean_center=False,
                     gamma_hbm=kt["gnw"], out_dtype=F32, tag="rm1s" + sfx)
        p.barrier()
        ph_quant(kt["s_xr"], kt["s_xq"], 0, 0, D, "q1" + sfx)
        p.barrier()
        ph_linear(kt["s_g8"], kt["s_xq"], kt["s_y1"], D, 11264,
                  "bg" + sfx, w_dtype=BF16, out_dtype=F32,
                  row_scale=kt["s_rows_bf"][0:1, :], out_keys="yb",
                  post=lambda: em_silu_mult("sm" + sfx))
        p.barrier()
        ph_layernorm(kt["s_h"], kt["s_hr"], 5632, mean_center=False,
                     gamma_hbm=kt["dnw"], out_dtype=F32, tag="rm2s" + sfx)
        p.barrier()
        ph_quant(kt["s_hr"], kt["s_hq"], 1, 1, 5632, "q2" + sfx)
        p.barrier()
        ph_linear(kt["s_d8"], kt["s_hq"], kt["s_z"], 5632, D,
                  "bd" + sfx, w_dtype=BF16, out_dtype=F32,
                  row_scale=kt["s_rows_bf"][1:2, :])
        p.barrier()
        ph_resadd(xcur, kt["s_z"], kt["yT"][s], "ra3" + sfx)
        p.barrier()
        if DEBUG and s == 0:
            p.dma(kt["yT"][s][:, :], kt["dbg3"][:, :])

    p.finalize()
    k.stk.close()
    return k


def _consts():
    ident = np.eye(128, dtype=NPBF)
    rmat = np.zeros((128, 128), dtype=NPBF)
    for m in range(64):
        rmat[m + 64, m] = -1.0
        rmat[m, m + 64] = 1.0
    e4g = np.zeros((4, 128), dtype=NPBF)
    for b in range(4):
        e4g[b, 16 * b:16 * (b + 1)] = 1
        e4g[b, 64 + 16 * b:64 + 16 * (b + 1)] = 1
    inv = np.power(10000.0, -2.0 * np.arange(64, dtype=np.float64) / 128.0)
    ang = np.arange(T, dtype=np.float64)[None, :] * inv[:, None]
    cosT = np.zeros((128, T), np.float32)
    sinT = np.zeros((128, T), np.float32)
    cosT[0:64] = np.cos(ang)
    cosT[64:128] = np.cos(ang)
    sinT[0:64] = np.sin(ang)
    sinT[64:128] = np.sin(ang)
    amask = np.zeros((512, 512), dtype=NPBF)
    ar = np.arange(512)
    for r in range(4):
        for kk2 in range(128):
            amask[r * 128 + kk2, :] = (ar >= (r * 128 + kk2))
    gmask = np.zeros((32, 32), dtype=NPBF)
    for j in range(32):
        gmask[j, j:] = 1
    return dict(ident=ident, rmat=rmat, e4g=e4g, cosT=cosT, sinT=sinT,
                amask=amask, gmask=gmask)


def _pk_bytes(w_packed_int32):
    v = np.ascontiguousarray(w_packed_int32).view(np.uint8)[:, 0::4]
    return np.ascontiguousarray(v.T)


def kernel(**inputs):
    cons = _consts()
    shared = dict(
        Wq=inputs["L0_attn_Wq"], Wk=inputs["L0_attn_Wk"],
        Wv=inputs["L0_attn_Wv"], Wo=inputs["L0_attn_Wo"],
        W1q=inputs["L1_attn_Wq"], W1k=inputs["L1_attn_Wk"],
        W1i=inputs["L1_attn_Wi"], W1o=inputs["L1_attn_Wo"],
        PKg=_pk_bytes(inputs["L0_ffn_gate_W_packed"]),
        PKu=_pk_bytes(inputs["L0_ffn_up_W_packed"]),
        PKd=_pk_bytes(inputs["L0_ffn_down_W_packed"]),
        SCg=np.ascontiguousarray(inputs["L0_ffn_gate_scales"].T),
        SCu=np.ascontiguousarray(inputs["L0_ffn_up_scales"].T),
        SCd=np.ascontiguousarray(inputs["L0_ffn_down_scales"].T),
        GWT=np.ascontiguousarray(inputs["L1_ffn_gate_w"].T),
        DWT=np.ascontiguousarray(inputs["L1_ffn_down_w"].T),
        gw=np.asarray(inputs["L1_attn_g_weight"]).reshape(16, 128).T,
        gnw=np.asarray(inputs["L1_ffn_gate_norm_w"]).reshape(16, 128).T,
        dnw=np.asarray(inputs["L1_ffn_down_norm_w"]).reshape(44, 128).T, **cons)
    shared = {k_: np.ascontiguousarray(v) for k_, v in shared.items()}
    x = inputs["L0_attn_x"]
    in_maps = []
    for core in range(8):
        xs = np.ascontiguousarray(
            np.asarray(x)[core * 2:core * 2 + 2].transpose(0, 2, 1)
        ).astype(np.float32)
        m = dict(shared)
        m["xT"] = xs
        in_maps.append(m)
    kk = build()
    res = run_bass_kernel_spmd(kk.nc, in_maps, core_ids=list(range(8)))
    out = np.empty((16, T, D), np.float32)
    for core in range(8):
        yt = res.results[core]["yT"]
        out[core * 2] = np.asarray(yt[0]).T
        out[core * 2 + 1] = np.asarray(yt[1]).T
    return out


# version 06
# speedup vs baseline: 1.0062x; 1.0062x over previous
import numpy as np
import ml_dtypes
from contextlib import ExitStack
import concourse.bass as bass
import concourse.mybir as mybir
from concourse.bass_utils import run_bass_kernel_spmd

F32 = mybir.dt.float32
BF16 = mybir.dt.bfloat16
U16 = mybir.dt.uint16
U8 = mybir.dt.uint8
FP8 = mybir.dt.float8e4
AF = mybir.ActivationFunctionType
ALU = mybir.AluOpType
AX = mybir.AxisListType
NPBF = ml_dtypes.bfloat16

D = 2048
T = 2048
NSEQ = 2
DC = 16
TT = 4
EPS = 1e-5
MSH = 15.0
C_GLA = 32
NCHUNK = T // C_GLA
CLAMP = -85.0
ISQ = float(1.0 / np.sqrt(128.0))
M0F, M07, M08, M01 = 0x0F0F, 0x0707, 0x0808, 0x0101

ENGS = ("tensor", "vector", "scalar", "gpsimd", "sync")
DEBUG = False


class P:

    NSLOT = 8

    def __init__(self, nc):
        self.nc = nc
        self.ops = {e: [] for e in ENGS}
        self.last = {}
        self.sems = {}
        for e in ENGS:
            self.sems["e" + e] = self._mk("es_" + e)
        for qn in ("sync", "scalar"):
            for s in range(self.NSLOT):
                self.sems["d%s%d" % (qn, s)] = self._mk("ds_%s%d" % (qn, s))
        self.dcnt = {qn: 0 for qn in ("sync", "scalar")}
        self.slotprev = {}
        self.lastcomp = {}
        self.barrier_deps = []
        self.exempt = False
        self.exempt_last = {}

    def _mk(self, name):
        ctx = self.nc.semaphore(name)
        h = ctx.__enter__()
        return h

    @staticmethod
    def _key(b):
        if isinstance(b, str):
            return b
        n = getattr(b, "name", None)
        if n is not None:
            return n
        t = getattr(b, "tensor", None)
        if t is not None:
            return t.name
        raise TypeError(b)

    def _deps(self, r, w, accum):
        deps = []
        for b in r:
            e = self.last.get(self._key(b))
            if e and e[0]:
                deps.append(e[0])
        acc_keys = {self._key(b) for b in accum}
        for b in list(w) + list(accum):
            kb = self._key(b)
            e = self.last.get(kb)
            if e:
                if e[0] and kb not in acc_keys:
                    deps.append(e[0])
                deps.extend(e[1])
        return deps

    def _record(self, eng, fn, r, w, accum, is_dma):
        deps = self._deps(r, w, accum)
        deps.extend(self.barrier_deps)
        if is_dma:
            di = self.dcnt[eng]
            self.dcnt[eng] = di + 1
            slot = di % self.NSLOT
            semkey = "d%s%d" % (eng, slot)
            prev = self.slotprev.get((eng, slot))
            if prev is not None:
                deps.append(prev)
        else:
            semkey = "e" + eng
        idx = len(self.ops[eng])
        h = (eng, semkey, idx)
        if is_dma:
            self.slotprev[(eng, slot)] = h
        self.ops[eng].append([fn, deps, is_dma, False, semkey])
        if not is_dma:
            if self.exempt:
                self.exempt_last[eng] = h
            else:
                self.lastcomp[eng] = h
        for b in r:
            kb = self._key(b)
            self.last.setdefault(kb, [None, []])[1].append(h)
        for b in list(w) + list(accum):
            self.last[self._key(b)] = [h, []]
        return h

    def barrier(self):
        hs = list(self.lastcomp.values()) + list(self.slotprev.values())
        self.barrier_deps = hs

    def barrier_full(self):
        hs = (list(self.lastcomp.values()) + list(self.slotprev.values())
              + list(self.exempt_last.values()))
        self.barrier_deps = hs
        self.exempt_last = {}

    def step(self, eng, fn, r=(), w=(), accum=()):
        wk = {self._key(b) for b in w} | {self._key(b) for b in accum}
        rr = list(r) if "K0" in wk else list(r) + ["K0"]
        return self._record(eng, fn, rr, w, accum, False)

    def dma(self, src_ap, dst_ap, q="sync", r=(), w=(), nokey=False):
        def _f(E, s=src_ap, t=dst_ap):
            return E.dma_start(out=t, in_=s)
        if nokey:
            return self._record(q, _f, list(r) + ["K0"], list(w), (), True)
        return self._record(q, _f, list(r) + [src_ap, "K0"],
                            list(w) + [dst_ap], (), True)

    def finalize(self):
        needed = set()
        for e in ENGS:
            for op in self.ops[e]:
                for d in op[1]:
                    needed.add(d)
        val = {}
        counters = {k: 0 for k in self.sems}
        for e in ENGS:
            for idx, op in enumerate(self.ops[e]):
                semkey = op[4]
                if (e, semkey, idx) in needed or op[2]:
                    op[3] = True
                    counters[semkey] += 16 if op[2] else 1
                    val[(e, semkey, idx)] = counters[semkey]
                else:
                    op[3] = False
        finals = dict(counters)
        sems = self.sems
        wm = {e: {} for e in ENGS}

        def run_engine(E, e):
            for idx, (fn, deps, is_dma, inc, semkey0) in enumerate(self.ops[e]):
                byk = {}
                for (de, dk, di) in deps:
                    v = val.get((de, dk, di))
                    if v is None:
                        raise RuntimeError("unmarked dep")
                    if v > byk.get(dk, 0):
                        byk[dk] = v
                for dk, v in byk.items():
                    if v > wm[e].get(dk, 0):
                        E.wait_ge(sems[dk], v)
                        wm[e][dk] = v
                ins = fn(E)
                if inc:
                    ins.then_inc(sems[semkey0], 16 if is_dma else 1)

        with self.nc.Block() as blk:
            blk.tensor(lambda E: run_engine(E, "tensor"))
            blk.vector(lambda E: run_engine(E, "vector"))
            blk.scalar(lambda E: run_engine(E, "scalar"))
            blk.gpsimd(lambda E: run_engine(E, "gpsimd"))

            def sync_body(E):
                run_engine(E, "sync")
                for kk2, v in finals.items():
                    if v:
                        E.wait_ge(sems[kk2], v)
            blk.sync(sync_body)


def fmv(hbm_ap, tsl=None):
    a = hbm_ap if tsl is None else hbm_ap[:, tsl]
    return a.rearrange("(c p) t -> p c t", p=128)


def v3(sb_ap, tw):
    return sb_ap.rearrange("p (c t) -> p c t", t=tw)


class K:
    def __init__(self):
        self.nc = bass.Bass("TRN2", target_bir_lowering=False)
        self.stk = ExitStack()
        self.t = {}

    def dram_in(self, name, shape, dtype):
        self.t[name] = self.nc.dram_tensor(name, shape, dtype,
                                           kind="ExternalInput").ap()

    def dram_out(self, name, shape, dtype):
        self.t[name] = self.nc.dram_tensor(name, shape, dtype,
                                           kind="ExternalOutput").ap()

    def dram_tmp(self, name, shape, dtype):
        self.t[name] = self.nc.dram_tensor(name, shape, dtype,
                                           kind="Internal").ap()

    def sb(self, name, shape, dtype):
        return self.stk.enter_context(self.nc.sbuf_tensor(name, shape, dtype))


def build():
    k = K()
    nc = k.nc
    kt = k.t
    k.dram_in("xT", [NSEQ, D, T], F32)
    for nm, sh in [("Wq", [D, D]), ("Wk", [D, 512]), ("Wv", [D, 512]),
                   ("Wo", [D, D]), ("W1q", [D, D]), ("W1k", [D, D]),
                   ("W1i", [D, D]), ("W1o", [D, D])]:
        k.dram_in(nm, sh, F32)
    k.dram_in("PKg", [1024, 16384], U8)
    k.dram_in("PKu", [1024, 16384], U8)
    k.dram_in("PKd", [8192, 2048], U8)
    k.dram_in("SCg", [64, 16384], F32)
    k.dram_in("SCu", [64, 16384], F32)
    k.dram_in("SCd", [512, 2048], F32)
    k.dram_in("GWT", [D, 11264], F32)
    k.dram_in("DWT", [5632, D], F32)
    k.dram_in("gw", [128, 16], F32)
    k.dram_in("gnw", [128, 16], F32)
    k.dram_in("dnw", [128, 44], F32)
    k.dram_in("ident", [128, 128], BF16)
    k.dram_in("rmat", [128, 128], BF16)
    k.dram_in("e4g", [4, 128], BF16)
    k.dram_in("cosT", [128, T], F32)
    k.dram_in("sinT", [128, T], F32)
    k.dram_in("amask", [512, 512], BF16)
    k.dram_in("gmask", [32, 32], BF16)
    k.dram_out("yT", [NSEQ, D, T], F32)
    if DEBUG:
        for i in range(4):
            k.dram_out("dbg%d" % i, [D, T], F32)

    k.dram_tmp("s_x", [D, T], F32)
    k.dram_tmp("s_xn", [D, T], BF16)
    k.dram_tmp("s_q", [D, T], BF16)
    k.dram_tmp("s_k", [512, T], BF16)
    k.dram_tmp("s_v", [T, 512], BF16)
    k.dram_tmp("s_o", [D, T], BF16)
    k.dram_tmp("s_z", [D, T], F32)
    k.dram_tmp("s_wgq", [D, 16384], BF16)
    k.dram_tmp("s_wuq", [D, 16384], BF16)
    k.dram_tmp("s_wdq", [16384, D], BF16)
    k.dram_tmp("s_y1", [16384, T], F32)
    k.dram_tmp("s_y2", [16384, T], F32)
    k.dram_tmp("s_int", [16384, T], BF16)
    k.dram_tmp("s_t1", [D, T], F32)
    k.dram_tmp("s_kr", [D, T], F32)
    k.dram_tmp("s_ks", [D, T], F32)
    k.dram_tmp("s_ar", [D, T], F32)
    k.dram_tmp("s_gq", [D, T], BF16)
    k.dram_tmp("s_gk", [D, T], BF16)
    k.dram_tmp("s_kh", [D, T], BF16)
    k.dram_tmp("s_khT", [T, D], BF16)
    k.dram_tmp("s_vg", [T, D], BF16)
    k.dram_tmp("s_lam", [D, NCHUNK], F32)
    k.dram_tmp("s_xr", [D, T], F32)
    k.dram_tmp("s_xq", [D, T], BF16)
    k.dram_tmp("s_h", [5632, T], F32)
    k.dram_tmp("s_hr", [5632, T], F32)
    k.dram_tmp("s_hq", [5632, T], BF16)
    k.dram_tmp("s_g8", [D, 11264], BF16)
    k.dram_tmp("s_d8", [5632, D], BF16)
    k.dram_tmp("s_rows", [8, T], F32)
    k.dram_tmp("s_rows_bf", [8, T], BF16)
    k.dram_tmp("s_wsc", [1, 8], F32)

    p = P(nc)

    cst = k.sb("cst", [128, 16], F32)
    CVALS = [0.0, 1.0, -1.0, 0.5, float(EPS), MSH, -MSH, ISQ, 127.0, CLAMP,
             128.5, 1.0 / 127.0]
    for i, v in enumerate(CVALS):
        nc.const_aps.register(F32, v, cst[:, i:i + 1])
        p.step("vector", lambda E, i=i, v=v: E.memset(cst[:, i:i + 1], v),
               w=(cst, "K0"))
    ones_col = k.sb("ones_col", [128, 1], F32)
    ones_row = k.sb("ones_row", [1, 128], F32)
    ones_col_bf = k.sb("ones_col_bf", [128, 1], BF16)
    ones_row_bf = k.sb("ones_row_bf", [1, 128], BF16)
    p.step("vector", lambda E: E.memset(ones_col[:, :], 1.0),
           w=(ones_col, "K0"))
    p.step("vector", lambda E: E.memset(ones_row[:, :], 1.0),
           w=(ones_row, "K0"))
    p.step("vector", lambda E: E.memset(ones_col_bf[:, :], 1.0),
           w=(ones_col_bf, "K0"))
    p.step("vector", lambda E: E.memset(ones_row_bf[:, :], 1.0),
           w=(ones_row_bf, "K0"))
    ident = k.sb("identS", [128, 128], BF16)
    rmat = k.sb("rmatS", [128, 128], BF16)
    e4g = k.sb("e4gS", [4, 128], BF16)
    gmask = k.sb("gmaskS", [32, 32], BF16)
    p.dma(kt["ident"], ident[:, :], w=("K0",))
    p.dma(kt["rmat"], rmat[:, :], w=("K0",))
    p.dma(kt["e4g"], e4g[:, :], w=("K0",))
    p.dma(kt["gmask"], gmask[:, :], w=("K0",))
    DQNS = 2048
    dq_stk = ExitStack()
    dqb = {nm: dq_stk.enter_context(nc.sbuf_tensor("dq_" + nm, [128, DQNS], U8))
           for nm in ("d", "2", "3", "4", "5", "6")}
    dqb["s4"] = dq_stk.enter_context(nc.sbuf_tensor("dq_s4", [4, DQNS], F32))
    dqb["sb"] = dq_stk.enter_context(nc.sbuf_tensor("dq_sb", [4, DQNS], BF16))
    dqb["o"] = dq_stk.enter_context(nc.sbuf_tensor("dq_o", [128, DQNS], BF16))
    dq_psA = dq_stk.enter_context(nc.psum_tensor("dq_psA", [128, 512], F32))

    def ph_layernorm(src, dst, Din, src_dtype=F32, out_dtype=BF16,
                     mean_center=True, gamma_hbm=None, tag="ln"):
        dch = Din // 128
        NBUF = 2 if dch <= 16 else 1
        with ExitStack() as st:
            sbl = lambda n, sh, dt: st.enter_context(nc.sbuf_tensor(tag + n, sh, dt))
            xs = [sbl("_x%d" % i, [128, dch * 512], F32) for i in range(NBUF)]
            xi = ([sbl("_xi%d" % i, [128, dch * 512], src_dtype)
                   for i in range(NBUF)] if src_dtype != F32 else None)
            sq = [sbl("_sq%d" % i, [128, 512], F32) for i in range(2)]
            rows = sbl("_rw", [1, 1024], F32)
            outs = (xs if (out_dtype == F32 and dch > 16) else
                    [sbl("_o%d" % i, [128, dch * 512], out_dtype)
                     for i in range(NBUF)])
            gcol = sbl("_g", [128, dch], F32) if gamma_hbm is not None else None
            with nc.psum_tensor(tag + "_p1", [1, 512], F32) as ps_s, \
                 nc.psum_tensor(tag + "_p2", [1, 512], F32) as ps_q, \
                 nc.psum_tensor(tag + "_pbr", [128, 512], F32) as psb_r, \
                 nc.psum_tensor(tag + "_pbm", [128, 512], F32) as psb_m:
                if gamma_hbm is not None:
                    p.dma(gamma_hbm, gcol[:, :])
                for tt in range(TT):
                    pr = tt % NBUF
                    X = xs[pr]
                    O = outs[pr]
                    tsl = slice(tt * 512, (tt + 1) * 512)
                    if src_dtype != F32:
                        p.dma(fmv(src, tsl), v3(xi[pr][:, :], 512))
                        p.step("vector", lambda E, pr=pr: E.tensor_copy(
                            out=xs[pr][:, :], in_=xi[pr][:, :]),
                            r=(xi[pr],), w=(X,))
                    else:
                        p.dma(fmv(src, tsl), v3(X[:, :], 512))
                    for c in range(dch):
                        csl = slice(c * 512, (c + 1) * 512)
                        sqc = sq[c % 2]
                        if mean_center:
                            p.step("tensor", lambda E, c=c, csl=csl, X=X: E.matmul(
                                ps_s[:, :], ones_col[:, :], X[:, csl],
                                start=(c == 0), stop=(c == dch - 1)),
                                r=(ones_col, X),
                                w=(ps_s,) if c == 0 else (), accum=(ps_s,) if c else ())
                        p.step("scalar", lambda E, csl=csl, X=X, sqc=sqc: E.activation(
                            out=sqc[:, :], in_=X[:, csl], func=AF.Square),
                            r=(X,), w=(sqc,))
                        p.step("tensor", lambda E, c=c, sqc=sqc: E.matmul(
                            ps_q[:, :], ones_col[:, :], sqc[:, :],
                            start=(c == 0), stop=(c == dch - 1)),
                            r=(ones_col, sqc),
                            w=(ps_q,) if c == 0 else (), accum=(ps_q,) if c else ())
                    r_r = rows[:, 0:512]
                    r_mr = rows[:, 512:1024]
                    p.step("vector", lambda E, r_r=r_r: E.tensor_scalar_mul(
                        r_r, ps_q[0:1, :], 1.0 / Din), r=(ps_q,), w=(rows,))
                    if mean_center:
                        p.step("vector", lambda E, r_mr=r_mr: E.tensor_scalar_mul(
                            r_mr, ps_s[0:1, :], 1.0 / Din), r=(ps_s,), w=(rows,))
                        p.step("vector", lambda E, r_mr=r_mr: E.scalar_tensor_tensor(
                            out=rows[0:1, 512:1024] if False else r_mr,
                            in0=r_mr, scalar=-1.0, in1=r_mr,
                            op0=ALU.mult, op1=ALU.mult), r=(rows,), w=("lnrtmp",)) \
                            if False else None
                        p.step("vector", lambda E, r_mr=r_mr: E.scalar_tensor_tensor(
                            out=sq[0][0:1, 0:512], in0=r_mr, scalar=-1.0,
                            in1=r_mr, op0=ALU.mult, op1=ALU.mult),
                            r=(rows,), w=(sq[0],))
                        p.step("vector", lambda E, r_r=r_r: E.tensor_add(
                            r_r, r_r, sq[0][0:1, 0:512]), r=(rows, sq[0]),
                            w=(rows,))
                    p.step("scalar", lambda E, r_r=r_r: E.activation(
                        out=r_r, in_=r_r, func=AF.Sqrt, bias=float(EPS)),
                        r=(rows, cst), w=(rows,))
                    p.step("vector", lambda E, r_r=r_r: E.reciprocal(r_r, r_r),
                           r=(rows,), w=(rows,))
                    if mean_center:
                        p.step("vector", lambda E, r_mr=r_mr, r_r=r_r: E.tensor_mul(
                            r_mr, r_mr, r_r), r=(rows,), w=(rows,))
                    p.step("tensor", lambda E: E.matmul(
                        psb_r[:, :], ones_row[:, :], rows[0:1, 0:512],
                        start=True, stop=True), r=(ones_row, rows), w=(psb_r,))
                    if mean_center:
                        p.step("tensor", lambda E: E.matmul(
                            psb_m[:, :], ones_row[:, :], rows[0:1, 512:1024],
                            start=True, stop=True), r=(ones_row, rows),
                            w=(psb_m,))
                    for c in range(dch):
                        csl = slice(c * 512, (c + 1) * 512)
                        p.step("vector", lambda E, csl=csl, X=X: E.tensor_mul(
                            X[:, csl], X[:, csl], psb_r[:, :]),
                            r=(X, psb_r), w=(X,))
                        if mean_center:
                            p.step("vector", lambda E, csl=csl, X=X: E.tensor_sub(
                                X[:, csl], X[:, csl], psb_m[:, :]),
                                r=(X, psb_m), w=(X,))
                        if gamma_hbm is not None:
                            p.step("vector", lambda E, c=c, csl=csl, X=X, O=O:
                                   E.tensor_scalar(
                                out=O[:, csl], in0=X[:, csl],
                                scalar1=gcol[:, c:c + 1], scalar2=None,
                                op0=ALU.mult), r=(X, gcol), w=(O,))
                        else:
                            p.step("vector", lambda E, csl=csl, X=X, O=O:
                                   E.tensor_copy(out=O[:, csl], in_=X[:, csl]),
                                   r=(X,), w=(O,))
                    p.dma(v3(O[:, :], 512), fmv(dst, tsl), q="scalar")

    def ph_resadd(xsrc, zsrc, dst, tag):
        NB2 = 1
        with ExitStack() as st:
            sbl = lambda n, sh, dt: st.enter_context(nc.sbuf_tensor(tag + n, sh, dt))
            zs = [sbl("_z%d" % i, [128, DC * 512], F32) for i in range(NB2)]
            xsb = [sbl("_x%d" % i, [128, DC * 512], F32) for i in range(NB2)]
            sq = [sbl("_sq%d" % i, [128, 512], F32) for i in range(2)]
            rows = sbl("_rw", [1, 1024], F32)
            outs = [sbl("_o%d" % i, [128, DC * 512], F32) for i in range(NB2)]
            with nc.psum_tensor(tag + "_p1", [1, 512], F32) as ps_s, \
                 nc.psum_tensor(tag + "_p2", [1, 512], F32) as ps_q, \
                 nc.psum_tensor(tag + "_pbr", [128, 512], F32) as psb_r, \
                 nc.psum_tensor(tag + "_pbm", [128, 512], F32) as psb_m:
                for tt in range(TT):
                    pr = tt % NB2
                    Z, X, O = zs[pr], xsb[pr], outs[pr]
                    tsl = slice(tt * 512, (tt + 1) * 512)
                    p.dma(fmv(zsrc, tsl), v3(Z[:, :], 512))
                    p.dma(fmv(xsrc, tsl), v3(X[:, :], 512))
                    for c in range(DC):
                        csl = slice(c * 512, (c + 1) * 512)
                        sqc = sq[c % 2]
                        p.step("tensor", lambda E, c=c, csl=csl, Z=Z: E.matmul(
                            ps_s[:, :], ones_col[:, :], Z[:, csl],
                            start=(c == 0), stop=(c == DC - 1)),
                            r=(ones_col, Z),
                            w=(ps_s,) if c == 0 else (), accum=(ps_s,) if c else ())
                        p.step("scalar", lambda E, csl=csl, Z=Z, sqc=sqc:
                               E.activation(out=sqc[:, :], in_=Z[:, csl],
                                            func=AF.Square), r=(Z,), w=(sqc,))
                        p.step("tensor", lambda E, c=c, sqc=sqc: E.matmul(
                            ps_q[:, :], ones_col[:, :], sqc[:, :],
                            start=(c == 0), stop=(c == DC - 1)),
                            r=(ones_col, sqc),
                            w=(ps_q,) if c == 0 else (), accum=(ps_q,) if c else ())
                    r_r = rows[:, 0:512]
                    r_mr = rows[:, 512:1024]
                    p.step("vector", lambda E, r_r=r_r: E.tensor_scalar_mul(
                        r_r, ps_q[0:1, :], 1.0 / D), r=(ps_q,), w=(rows,))
                    p.step("vector", lambda E, r_mr=r_mr: E.tensor_scalar_mul(
                        r_mr, ps_s[0:1, :], 1.0 / D), r=(ps_s,), w=(rows,))
                    p.step("vector", lambda E, r_mr=r_mr: E.scalar_tensor_tensor(
                        out=sq[0][0:1, 0:512], in0=r_mr, scalar=-1.0, in1=r_mr,
                        op0=ALU.mult, op1=ALU.mult), r=(rows,), w=(sq[0],))
                    p.step("vector", lambda E, r_r=r_r: E.tensor_add(
                        r_r, r_r, sq[0][0:1, 0:512]), r=(rows, sq[0]), w=(rows,))
                    p.step("scalar", lambda E, r_r=r_r: E.activation(
                        out=r_r, in_=r_r, func=AF.Sqrt, bias=float(EPS)),
                        r=(rows, cst), w=(rows,))
                    p.step("vector", lambda E, r_r=r_r: E.reciprocal(r_r, r_r),
                           r=(rows,), w=(rows,))
                    p.step("vector", lambda E, r_mr=r_mr, r_r=r_r: E.tensor_mul(
                        r_mr, r_mr, r_r), r=(rows,), w=(rows,))
                    p.step("tensor", lambda E: E.matmul(
                        psb_r[:, :], ones_row[:, :], rows[0:1, 0:512],
                        start=True, stop=True), r=(ones_row, rows), w=(psb_r,))
                    p.step("tensor", lambda E: E.matmul(
                        psb_m[:, :], ones_row[:, :], rows[0:1, 512:1024],
                        start=True, stop=True), r=(ones_row, rows), w=(psb_m,))
                    for c in range(DC):
                        csl = slice(c * 512, (c + 1) * 512)
                        p.step("vector", lambda E, csl=csl, Z=Z: E.tensor_mul(
                            Z[:, csl], Z[:, csl], psb_r[:, :]),
                            r=(Z, psb_r), w=(Z,))
                        p.step("vector", lambda E, csl=csl, Z=Z: E.tensor_sub(
                            Z[:, csl], Z[:, csl], psb_m[:, :]),
                            r=(Z, psb_m), w=(Z,))
                        p.step("vector", lambda E, csl=csl, Z=Z, X=X, O=O:
                               E.tensor_add(O[:, csl], Z[:, csl], X[:, csl]),
                               r=(Z, X), w=(O,))
                    p.dma(v3(O[:, :], 512), fmv(dst, tsl), q="scalar")

    def ph_linear(w_hbm, x_hbm, out_hbm, Kdim, Ndim, tag, w_dtype=F32,
                  out_dtype=BF16, row_scale=None, out_keys=None, post=None):
        kslab = min(Kdim, 2048)
        nslab = (Kdim + kslab - 1) // kslab
        if nslab > 1:
            kslab = 1024
            nslab = (Kdim + kslab - 1) // kslab
        NB = 256
        acc_needed = nslab > 1
        with ExitStack() as st:
            sbl = lambda n, sh, dt: st.enter_context(nc.sbuf_tensor(tag + n, sh, dt))
            xs = sbl("_x", [128, (kslab // 128) * T], BF16)
            wf = [sbl("_wf%d" % i, [128, (kslab // 128) * NB], w_dtype)
                  for i in range(2)]
            wb = ([sbl("_wb%d" % i, [128, (kslab // 128) * NB], BF16)
                   for i in range(2)] if w_dtype != BF16 else wf)
            outs = [sbl("_o%d" % i, [128, (1 if acc_needed else 2) * T],
                        out_dtype) for i in range(1 if acc_needed else 2)]
            acc = sbl("_a", [128, (Ndim // 128) * T], F32) if acc_needed else None
            scl = sbl("_s", [128, TT * 512], BF16) if row_scale is not None else None
            srow = sbl("_sr", [1, T], BF16) if row_scale is not None else None
            with ExitStack() as pst:
                ps4 = [pst.enter_context(nc.psum_tensor("%s_ps%d" % (tag, i),
                                                        [128, 512], F32))
                       for i in range(4)]
                ps = [ps4[i % 4] for i in range(8)]
                if row_scale is not None:
                    p.dma(row_scale, srow[:, :])
                    for tt in range(TT):
                        p.step("tensor", lambda E, tt=tt: E.matmul(
                            ps[0][:, :], ones_row_bf[:, :],
                            srow[0:1, tt * 512:(tt + 1) * 512],
                            start=True, stop=True), r=(ones_row_bf, srow),
                            w=(ps[0],))
                        p.step("vector", lambda E, tt=tt: E.tensor_copy(
                            out=scl[:, tt * 512:(tt + 1) * 512],
                            in_=ps[0][:, :]), r=(ps[0],), w=(scl,))
                for sl in range(nslab):
                    k0 = sl * kslab
                    kc = min(kslab, Kdim - k0) // 128
                    p.dma(fmv(x_hbm[k0:k0 + kc * 128, :]),
                          v3(xs[:, :kc * T], T))
                    for nb in range(Ndim // NB):
                        n0 = nb * NB
                        pr = nb % 2
                        WF, WB = wf[pr], wb[pr]
                        O = outs[pr % len(outs)]
                        p.dma(w_hbm[k0:k0 + kc * 128, n0:n0 + NB].rearrange(
                            "(c p) n -> p c n", p=128),
                            v3(WF[:, :kc * NB], NB))
                        if w_dtype != BF16:
                            p.step("vector", lambda E, kc=kc, WF=WF, WB=WB:
                                   E.tensor_copy(out=WB[:, :kc * NB],
                                                 in_=WF[:, :kc * NB]),
                                   r=(WF,), w=(WB,))
                        for t4 in range(TT):
                            for n2 in range(2):
                                bank = t4 * 2 + n2
                                for c in range(kc):
                                    p.step("tensor", lambda E, c=c, n2=n2,
                                           t4=t4, bank=bank, kc=kc, WB=WB:
                                           E.matmul(
                                        ps[bank][:, :],
                                        WB[:, c * NB + n2 * 128:
                                           c * NB + n2 * 128 + 128],
                                        xs[:, c * T + t4 * 512:
                                           c * T + (t4 + 1) * 512],
                                        start=(c == 0), stop=(c == kc - 1)),
                                        r=(WB, xs),
                                        w=(ps[bank],) if c == 0 else (),
                                        accum=(ps[bank],) if c else ())
                                if acc_needed:
                                    nch = n0 // 128 + n2
                                    asl = slice(nch * T + t4 * 512,
                                                nch * T + (t4 + 1) * 512)
                                    if sl == 0:
                                        p.step("vector", lambda E, bank=bank,
                                               asl=asl: E.tensor_copy(
                                            out=acc[:, asl],
                                            in_=ps[bank][:, :]),
                                            r=(ps[bank],), w=(acc,))
                                    else:
                                        p.step("vector", lambda E, bank=bank,
                                               asl=asl: E.tensor_add(
                                            acc[:, asl], acc[:, asl],
                                            ps[bank][:, :]),
                                            r=(ps[bank], acc), w=(acc,))
                                else:
                                    osl = slice(n2 * T + t4 * 512,
                                                n2 * T + (t4 + 1) * 512)
                                    if row_scale is not None:
                                        ssl = slice(t4 * 512, (t4 + 1) * 512)
                                        p.step("vector", lambda E, bank=bank,
                                               osl=osl, ssl=ssl, O=O:
                                               E.tensor_mul(
                                            O[:, osl], ps[bank][:, :],
                                            scl[:, ssl]),
                                            r=(ps[bank], scl), w=(O,))
                                    else:
                                        p.step("vector", lambda E, bank=bank,
                                               osl=osl, O=O: E.tensor_copy(
                                            out=O[:, osl], in_=ps[bank][:, :]),
                                            r=(ps[bank],), w=(O,))
                        if not acc_needed:
                            if out_keys is not None:
                                p.dma(v3(O[:, :], T),
                                      fmv(out_hbm[n0:n0 + NB, :]),
                                      q="scalar", r=(O,),
                                      w=("%s#%d" % (out_keys, nb),),
                                      nokey=True)
                            else:
                                p.dma(v3(O[:, :], T),
                                      fmv(out_hbm[n0:n0 + NB, :]),
                                      q="scalar")
                if acc_needed:
                    for nch in range(Ndim // 128):
                        O = outs[0]
                        if row_scale is not None:
                            for t4 in range(TT):
                                p.step("vector", lambda E, nch=nch, t4=t4, O=O:
                                       E.tensor_mul(
                                    O[:, t4 * 512:(t4 + 1) * 512],
                                    acc[:, nch * T + t4 * 512:
                                        nch * T + (t4 + 1) * 512],
                                    scl[:, t4 * 512:(t4 + 1) * 512]),
                                    r=(acc, scl), w=(O,))
                        else:
                            p.step("vector", lambda E, nch=nch, O=O:
                                   E.tensor_copy(
                                out=O[:, 0:T], in_=acc[:, nch * T:(nch + 1) * T]),
                                r=(acc,), w=(O,))
                        p.dma(v3(O[:, 0:T], T),
                              fmv(out_hbm[nch * 128:(nch + 1) * 128, :]),
                              q="scalar")

                if post is not None:
                    post()

    def ph_linear_multi(pairs, x_hbm, Kdim, Ndim, tag, w_dtype=BF16,
                        out_dtype=F32, post=None):
        kslab = Kdim
        NB = 256
        with ExitStack() as st:
            sbl = lambda n, sh, dt: st.enter_context(nc.sbuf_tensor(tag + n, sh, dt))
            xs = sbl("_x", [128, (kslab // 128) * T], BF16)
            wf = [sbl("_wf%d" % i, [128, (kslab // 128) * NB], w_dtype)
                  for i in range(2)]
            outs = [sbl("_o%d" % i, [128, 2 * T], out_dtype) for i in range(2)]
            with ExitStack() as pstk:
                ps4 = [pstk.enter_context(nc.psum_tensor("%s_ps%d" % (tag, i),
                                                         [128, 512], F32))
                       for i in range(4)]
                ps = [ps4[i % 4] for i in range(8)]
                kc = Kdim // 128
                p.dma(fmv(x_hbm), v3(xs[:, :kc * T], T))
                for w_hbm, out_hbm, pfx in pairs:
                    for nb in range(Ndim // NB):
                        n0 = nb * NB
                        pr = nb % 2
                        WB, O = wf[pr], outs[pr]
                        p.dma(w_hbm[0:kc * 128, n0:n0 + NB].rearrange(
                            "(c p) n -> p c n", p=128),
                            v3(WB[:, :kc * NB], NB))
                        for t4 in range(TT):
                            for n2 in range(2):
                                bank = t4 * 2 + n2
                                for c in range(kc):
                                    p.step("tensor", lambda E, c=c, n2=n2,
                                           t4=t4, bank=bank, WB=WB: E.matmul(
                                        ps[bank][:, :],
                                        WB[:, c * NB + n2 * 128:
                                           c * NB + n2 * 128 + 128],
                                        xs[:, c * T + t4 * 512:
                                           c * T + (t4 + 1) * 512],
                                        start=(c == 0), stop=(c == kc - 1)),
                                        r=(WB, xs),
                                        w=(ps[bank],) if c == 0 else (),
                                        accum=(ps[bank],) if c else ())
                                osl = slice(n2 * T + t4 * 512,
                                            n2 * T + (t4 + 1) * 512)
                                p.step("vector", lambda E, bank=bank, osl=osl,
                                       O=O: E.tensor_copy(
                                    out=O[:, osl], in_=ps[bank][:, :]),
                                    r=(ps[bank],), w=(O,))
                        p.dma(v3(O[:, :], T), fmv(out_hbm[n0:n0 + NB, :]),
                              q="scalar", r=(O,), w=("%s#%d" % (pfx, nb),),
                              nokey=True)
                if post is not None:
                    post()

    def em_gelu_mult(tag):
        with ExitStack() as st:
            sbl = lambda n, sh, dt: st.enter_context(nc.sbuf_tensor(tag + n, sh, dt))
            ab = [sbl("_a%d" % i, [128, T], F32) for i in range(2)]
            bb2 = [sbl("_b%d" % i, [128, T], F32) for i in range(2)]
            obb = [sbl("_o%d" % i, [128, T], BF16) for i in range(2)]
            for c in range(128):
                pr = c % 2
                a, b, ob = ab[pr], bb2[pr], obb[pr]
                rs = slice(c * 128, (c + 1) * 128)
                blk = c // 2
                p.dma(kt["s_y1"][rs, :], a[:, :], r=("y1#%d" % blk,),
                      w=(a,), nokey=True)
                p.dma(kt["s_y2"][rs, :], b[:, :], r=("y2#%d" % blk,),
                      w=(b,), nokey=True)
                p.step("scalar", lambda E, a=a: E.activation(
                    out=a[:, :], in_=a[:, :], func=AF.Gelu_apprx_tanh),
                    r=(a,), w=(a,))
                p.step("vector", lambda E, ob=ob, a=a, b=b: E.tensor_mul(
                    ob[:, :], a[:, :], b[:, :]), r=(a, b), w=(ob,))
                p.dma(ob[:, :], kt["s_int"][rs, :], q="scalar")

    def em_silu_mult(tag):
        with ExitStack() as st:
            sbl = lambda n, sh, dt: st.enter_context(nc.sbuf_tensor(tag + n, sh, dt))
            ab = [sbl("_a%d" % i, [128, T], F32) for i in range(2)]
            bb2 = [sbl("_b%d" % i, [128, T], F32) for i in range(2)]
            obb = [sbl("_o%d" % i, [128, T], F32) for i in range(2)]
            for c in range(44):
                pr = c % 2
                a, b, ob = ab[pr], bb2[pr], obb[pr]
                p.dma(kt["s_y1"][c * 128:(c + 1) * 128, :], a[:, :],
                      r=("yb#%d" % (c // 2),), w=(a,), nokey=True)
                p.dma(kt["s_y1"][(44 + c) * 128:(45 + c) * 128, :], b[:, :],
                      r=("yb#%d" % ((44 + c) // 2),), w=(b,), nokey=True)
                p.step("scalar", lambda E, a=a: E.activation(
                    out=a[:, :], in_=a[:, :], func=AF.Silu), r=(a,), w=(a,))
                p.step("vector", lambda E, ob=ob, a=a, b=b: E.tensor_mul(
                    ob[:, :], a[:, :], b[:, :]), r=(a, b), w=(ob,))
                p.dma(ob[:, :], kt["s_h"][c * 128:(c + 1) * 128, :],
                      q="scalar")

    def ph_linear_tok(w_hbm, x_hbm, out_hbm, Ndim, tag):
        with ExitStack() as st:
            sbl = lambda n, sh, dt: st.enter_context(nc.sbuf_tensor(tag + n, sh, dt))
            xs = sbl("_x", [128, DC * T], BF16)
            NQ = 8 if Ndim >= 2048 else 4
            wf = sbl("_wf", [128, DC * (Ndim // NQ)], F32)
            wbt = sbl("_wb", [128, DC * Ndim], BF16)
            outs = [sbl("_o%d" % i, [128, Ndim], BF16)
                    for i in range(1 if Ndim >= 2048 else 2)]
            with ExitStack() as pstk:
                nbk = Ndim // 512
                ps = [pstk.enter_context(nc.psum_tensor("%s_ps%d" % (tag, i),
                                                        [128, 512], F32))
                      for i in range(nbk)]
                p.dma(fmv(x_hbm), v3(xs[:, :], T))
                for qq in range(NQ):
                    q0 = qq * (Ndim // NQ)
                    p.dma(w_hbm[:, q0:q0 + Ndim // NQ].rearrange(
                        "(c p) n -> p c n", p=128), v3(wf[:, :], Ndim // NQ))
                    for c in range(DC):
                        p.step("vector", lambda E, c=c, q0=q0: E.tensor_copy(
                            out=wbt[:, c * Ndim + q0:c * Ndim + q0 + Ndim // NQ],
                            in_=wf[:, c * (Ndim // NQ):(c + 1) * (Ndim // NQ)]),
                            r=(wf,), w=("%s_wbt%d" % (tag, qq % 4),))
                for tc in range(16):
                    O = outs[tc % len(outs)]
                    for nb in range(nbk):
                        for c in range(DC):
                            p.step("tensor", lambda E, tc=tc, c=c, nb=nb:
                                   E.matmul(
                                ps[nb][:, :],
                                xs[:, c * T + tc * 128:c * T + (tc + 1) * 128],
                                wbt[:, c * Ndim + nb * 512:
                                    c * Ndim + (nb + 1) * 512],
                                start=(c == 0), stop=(c == DC - 1)),
                                r=(xs, "%s_wbt0" % tag, "%s_wbt1" % tag,
                                   "%s_wbt2" % tag, "%s_wbt3" % tag),
                                w=(ps[nb],) if c == 0 else (),
                                accum=(ps[nb],) if c else ())
                        p.step("vector", lambda E, nb=nb, O=O: E.tensor_copy(
                            out=O[:, nb * 512:(nb + 1) * 512],
                            in_=ps[nb][:, :]), r=(ps[nb],), w=(O,))
                    p.dma(O[:, :Ndim], out_hbm[tc * 128:(tc + 1) * 128, :],
                          q="scalar")

    def ph_dequant(pk, sc, dst, Kdim, Ndim, tag):
        kch = Kdim // 128
        NS = min(Ndim, DQNS)
        dup, w2, w3, w4, w5, w6 = (dqb["d"], dqb["2"], dqb["3"], dqb["4"],
                                   dqb["5"], dqb["6"])
        sc4, sc4b, wout = dqb["s4"], dqb["sb"], dqb["o"]
        psX = dq_psA
        c_i = dup[:].bitcast(U16)
        s_ = w2[:].bitcast(U16)
        c7 = w3[:].bitcast(U16)
        z1 = w4[:].bitcast(U16)
        z2 = w5[:].bitcast(U16)
        w6i = w6[:].bitcast(U16)
        d8 = w3[:].bitcast(FP8)
        eng = "vector"
        st_ = lambda f, rr, ww: p.step(eng, f, r=rr, w=ww)
        for kc in range(kch):
            j0 = kc * 64
            for n0 in range(0, Ndim, NS):
                nsl = slice(n0, n0 + NS)
                p.dma(pk[j0:j0 + 64, nsl], dup[0:64, :NS])
                p.dma(pk[j0:j0 + 64, nsl], dup[64:128, :NS])
                p.dma(sc[kc * 4:kc * 4 + 4, nsl], sc4[:, :NS])
                p.step("vector", lambda E: E.tensor_copy(
                    out=sc4b[:, :NS], in_=sc4[:, :NS]), r=(sc4,), w=(sc4b,))
                st_(lambda E: E.tensor_scalar(
                    out=c_i[0:64, :], in0=c_i[0:64, :], scalar1=M0F,
                    scalar2=None, op0=ALU.bitwise_and), (dup,), (dup,))
                st_(lambda E: E.tensor_scalar(
                    out=c_i[64:128, :], in0=c_i[64:128, :], scalar1=4,
                    scalar2=M0F, op0=ALU.logical_shift_right,
                    op1=ALU.bitwise_and), (dup,), (dup,))
                st_(lambda E: E.tensor_scalar(
                    out=s_, in0=c_i, scalar1=M08, scalar2=4,
                    op0=ALU.bitwise_and, op1=ALU.logical_shift_left),
                    (dup,), (w2,))
                st_(lambda E: E.tensor_scalar(
                    out=c7, in0=c_i, scalar1=M07, scalar2=None,
                    op0=ALU.bitwise_and), (dup,), (w3,))
                st_(lambda E: E.tensor_scalar(
                    out=z1, in0=c7, scalar1=1, scalar2=None,
                    op0=ALU.logical_shift_right), (w3,), (w4,))
                st_(lambda E: E.tensor_scalar(
                    out=z2, in0=z1, scalar1=1, scalar2=None,
                    op0=ALU.logical_shift_right), (w4,), (w5,))
                st_(lambda E: E.tensor_tensor(
                    out=z1, in0=z1, in1=z2, op=ALU.bitwise_or),
                    (w4, w5), (w4,))
                st_(lambda E: E.tensor_scalar(
                    out=z2, in0=c7, scalar1=M01, scalar2=None,
                    op0=ALU.bitwise_and), (w3,), (w5,))
                st_(lambda E: E.tensor_scalar(
                    out=w6i, in0=z1, scalar1=M01, scalar2=None,
                    op0=ALU.bitwise_xor), (w4,), (w6,))
                st_(lambda E: E.tensor_tensor(
                    out=w6i, in0=z2, in1=w6i, op=ALU.bitwise_and),
                    (w5, w6), (w6,))
                st_(lambda E: E.tensor_tensor(
                    out=z2, in0=z1, in1=c7, op=ALU.bitwise_or),
                    (w4, w3), (w5,))
                st_(lambda E: E.tensor_scalar(
                    out=z2, in0=z2, scalar1=M01, scalar2=None,
                    op0=ALU.bitwise_and), (w5,), (w5,))
                st_(lambda E: E.tensor_tensor(
                    out=c7, in0=c7, in1=w6i, op=ALU.bitwise_xor),
                    (w3, w6), (w3,))
                st_(lambda E: E.tensor_scalar(
                    out=c7, in0=c7, scalar1=4, scalar2=None,
                    op0=ALU.mult), (w3,), (w3,))
                st_(lambda E: E.tensor_scalar(
                    out=z2, in0=z2, scalar1=48, scalar2=None,
                    op0=ALU.mult), (w5,), (w5,))
                st_(lambda E: E.tensor_tensor(
                    out=c7, in0=c7, in1=z2, op=ALU.add), (w3, w5), (w3,))
                st_(lambda E: E.tensor_tensor(
                    out=c7, in0=c7, in1=s_, op=ALU.bitwise_or),
                    (w3, w2), (w3,))
                for s5 in range(NS // 512):
                    ssl = slice(s5 * 512, (s5 + 1) * 512)
                    p.step("tensor", lambda E, ssl=ssl: E.matmul(
                        psX[:, :], e4g[:, :], sc4b[:, ssl],
                        start=True, stop=True), r=(e4g, sc4b), w=(psX,))
                    p.step("vector", lambda E, ssl=ssl: E.tensor_tensor(
                        out=wout[:, ssl], in0=d8[:, ssl], in1=psX[:, :],
                        op=ALU.mult), r=(w3, psX), w=(wout,))
                p.dma(wout[0:64, :NS],
                      dst[kc * 128:kc * 128 + 128:2, nsl], q="scalar")
                p.dma(wout[64:128, :NS],
                      dst[kc * 128 + 1:kc * 128 + 128:2, nsl], q="scalar")
                yield

    def em_round2(eng, v, t1, t2, w, vk, t1k, t2k):
        p.step(eng, lambda E: E.tensor_scalar(
            out=t1[:, :w], in0=v[:, :w], scalar1=128.5, scalar2=None,
            op0=ALU.add), r=(vk,), w=(t1k,))
        for kk in range(7, -1, -1):
            th = float(1 << kk)
            p.step(eng, lambda E, th=th: E.tensor_scalar(
                out=t2[:, :w], in0=t1[:, :w], scalar1=th, scalar2=th,
                op0=ALU.is_ge, op1=ALU.mult), r=(t1k,), w=(t2k,))
            p.step(eng, lambda E: E.tensor_sub(t1[:, :w], t1[:, :w],
                                               t2[:, :w]), r=(t1k, t2k),
                   w=(t1k,))
        p.step(eng, lambda E: E.tensor_sub(t2[:, :w], v[:, :w], t1[:, :w]),
               r=(vk, t1k), w=(t2k,))
        p.step(eng, lambda E: E.tensor_scalar(
            out=t2[:, :w], in0=t2[:, :w], scalar1=0.5, scalar2=None,
            op0=ALU.add), r=(t2k,), w=(t2k,))

    def ph_ternarize(w_hbm, dst, Kdim, Ndim, scal_idx, tag):
        kch = Kdim // 128
        NSW = 1024
        nel = float(Kdim * Ndim)
        with ExitStack() as st:
            sbl = lambda n, sh, dt: st.enter_context(nc.sbuf_tensor(tag + n, sh, dt))
            wfb = [sbl("_w%d" % i, [128, NSW], F32) for i in range(2)]
            t1b = [sbl("_t1%d" % i, [128, NSW], F32) for i in range(2)]
            t2b = [sbl("_t2%d" % i, [128, NSW], F32) for i in range(2)]
            colb = [sbl("_c%d" % i, [128, 1], F32) for i in range(2)]
            colacc = sbl("_ca", [128, 1], F32)
            srow = sbl("_sr", [1, 2], F32)
            scol = sbl("_sc", [128, 1], F32)
            obb = [sbl("_ob%d" % i, [128, NSW], BF16) for i in range(2)]
            with nc.psum_tensor(tag + "_p", [1, 512], F32) as ps1, \
                 nc.psum_tensor(tag + "_pb", [128, 512], F32) as psb:
                p.step("vector", lambda E: E.memset(colacc[:, :], 0.0),
                       w=(colacc,))
                it = 0
                for kc in range(kch):
                    for n0 in range(0, Ndim, NSW):
                        w = min(NSW, Ndim - n0)
                        pr = it % 2
                        wf, col = wfb[pr], colb[pr]
                        p.dma(w_hbm[kc * 128:(kc + 1) * 128, n0:n0 + w],
                              wf[:, :w])
                        p.step("vector", lambda E, w=w, wf=wf, col=col:
                               E.reduce_sum(col[:, :], wf[:, :w], axis=AX.X,
                                            apply_absolute_value=True),
                               r=(wf,), w=(col,))
                        p.step("vector", lambda E, col=col: E.tensor_add(
                            colacc[:, :], colacc[:, :], col[:, :]),
                            r=(col, colacc), w=(colacc,))
                        it += 1
                        yield
                p.step("tensor", lambda E: E.matmul(
                    ps1[0:1, 0:1], ones_col[:, :], colacc[:, :],
                    start=True, stop=True), r=(ones_col, colacc), w=(ps1,))
                p.step("vector", lambda E: E.tensor_scalar_mul(
                    srow[0:1, 0:1], ps1[0:1, 0:1], 1.0 / nel),
                    r=(ps1,), w=(srow,))
                p.step("vector", lambda E: E.tensor_scalar_max(
                    srow[0:1, 0:1], srow[0:1, 0:1], 1e-5), r=(srow,), w=(srow,))
                p.dma(srow[0:1, 0:1], kt["s_wsc"][0:1, scal_idx:scal_idx + 1],
                      q="scalar")
                p.step("vector", lambda E: E.reciprocal(
                    srow[0:1, 1:2], srow[0:1, 0:1]), r=(srow,), w=(srow,))
                p.step("tensor", lambda E: E.matmul(
                    psb[:, 0:1], ones_row[:, :], srow[0:1, 1:2],
                    start=True, stop=True), r=(ones_row, srow), w=(psb,))
                p.step("vector", lambda E: E.tensor_copy(
                    out=scol[:, :], in_=psb[:, 0:1]), r=(psb,), w=(scol,))
                it = 0
                for kc in range(kch):
                    for n0 in range(0, Ndim, NSW):
                        w = min(NSW, Ndim - n0)
                        pr = it % 2
                        eng = "vector"
                        wf, t1, t2, ob = wfb[pr], t1b[pr], t2b[pr], obb[pr]
                        p.dma(w_hbm[kc * 128:(kc + 1) * 128, n0:n0 + w],
                              wf[:, :w])
                        p.step(eng, lambda E, w=w, wf=wf: E.tensor_scalar(
                            out=wf[:, :w], in0=wf[:, :w], scalar1=scol[:, 0:1],
                            scalar2=None, op0=ALU.mult), r=(wf, scol), w=(wf,))
                        p.step(eng, lambda E, w=w, wf=wf, t1=t1:
                               E.tensor_scalar(
                            out=t1[:, :w], in0=wf[:, :w], scalar1=0.5,
                            scalar2=None, op0=ALU.is_ge), r=(wf,), w=(t1,))
                        p.step(eng, lambda E, w=w, wf=wf, t2=t2:
                               E.tensor_scalar(
                            out=t2[:, :w], in0=wf[:, :w], scalar1=-0.5,
                            scalar2=None, op0=ALU.is_le), r=(wf,), w=(t2,))
                        p.step(eng, lambda E, w=w, t1=t1, t2=t2, ob=ob:
                               E.tensor_sub(ob[:, :w], t1[:, :w], t2[:, :w]),
                            r=(t1, t2), w=(ob,))
                        p.dma(ob[:, :w], dst[kc * 128:(kc + 1) * 128,
                                             n0:n0 + w], q="scalar")
                        it += 1
                        yield

    def ph_rope(buf, Din, tag):
        dch = Din // 128
        with ExitStack() as st:
            sbl = lambda n, sh, dt: st.enter_context(nc.sbuf_tensor(tag + n, sh, dt))
            qsb = [sbl("_q%d" % i, [128, 512], BF16) for i in range(2)]
            cosb = sbl("_c", [128, T], F32)
            sinb = sbl("_s", [128, T], F32)
            t1b = [sbl("_1%d" % i, [128, 512], F32) for i in range(2)]
            t2b = [sbl("_2%d" % i, [128, 512], F32) for i in range(2)]
            obb = [sbl("_o%d" % i, [128, 512], BF16) for i in range(2)]
            with nc.psum_tensor(tag + "_pA", [128, 512], F32) as psA, \
                 nc.psum_tensor(tag + "_pB", [128, 512], F32) as psB:
                psr = [psA, psB]
                p.dma(kt["cosT"], cosb[:, :])
                p.dma(kt["sinT"], sinb[:, :])
                it = 0
                for c in range(dch):
                    for tt in range(TT):
                        pr = it % 2
                        qs, t1, t2, ob, pp = (qsb[pr], t1b[pr], t2b[pr],
                                              obb[pr], psr[pr])
                        tsl = slice(tt * 512, (tt + 1) * 512)
                        p.dma(buf[c * 128:(c + 1) * 128, tsl], qs[:, :])
                        p.step("tensor", lambda E, qs=qs, pp=pp: E.matmul(
                            pp[:, :], rmat[:, :], qs[:, :],
                            start=True, stop=True), r=(rmat, qs), w=(pp,))
                        p.step("vector", lambda E, tsl=tsl, qs=qs, t1=t1:
                               E.tensor_mul(t1[:, :], qs[:, :], cosb[:, tsl]),
                               r=(qs, cosb), w=(t1,))
                        p.step("vector", lambda E, tsl=tsl, t2=t2, pp=pp:
                               E.tensor_mul(t2[:, :], pp[:, :], sinb[:, tsl]),
                               r=(pp, sinb), w=(t2,))
                        p.step("vector", lambda E, ob=ob, t1=t1, t2=t2:
                               E.tensor_add(ob[:, :], t1[:, :], t2[:, :]),
                               r=(t1, t2), w=(ob,))
                        p.dma(ob[:, :], buf[c * 128:(c + 1) * 128, tsl],
                              q="scalar")
                        it += 1

    def ph_attn(tag):
        with ExitStack() as st:
            sbl = lambda n, sh, dt: st.enter_context(nc.sbuf_tensor(tag + n, sh, dt))
            qh = [sbl("_q%d" % i, [128, 512], BF16) for i in range(2)]
            kh = [sbl("_k%d" % i, [128, T], BF16) for i in range(2)]
            vt = [sbl("_vt%d" % i, [128, 128 * 16], BF16) for i in range(2)]
            at = [sbl("_a%d" % i, [128, 512], BF16) for i in range(4)]
            msk = sbl("_m", [128, 2048], BF16)
            ob = [sbl("_o%d" % i, [128, 512], BF16) for i in range(2)]
            rd = sbl("_r", [1, 512], F32)
            rdb = sbl("_rdb", [128, 512], F32)
            with nc.psum_tensor(tag + "_s0", [128, 512], F32) as ps_s0, \
                 nc.psum_tensor(tag + "_s1", [128, 512], F32) as ps_s1, \
                 nc.psum_tensor(tag + "_s2", [128, 512], F32) as ps_s2, \
                 nc.psum_tensor(tag + "_s3", [128, 512], F32) as ps_s3, \
                 nc.psum_tensor(tag + "_po", [128, 512], F32) as ps_o, \
                 nc.psum_tensor(tag + "_pd", [1, 512], F32) as ps_d, \
                 nc.psum_tensor(tag + "_pb", [128, 512], F32) as ps_b:
                ps_s = [ps_s0, ps_s1, ps_s2, ps_s3]
                for r in range(4):
                    p.dma(kt["amask"][r * 128:(r + 1) * 128, :],
                          msk[:, r * 512:(r + 1) * 512])
                for h in range(16):
                    kv = h // 4
                    KH, VT = kh[kv % 2], vt[kv % 2]
                    if h % 4 == 0:
                        p.dma(kt["s_k"][kv * 128:(kv + 1) * 128, :], KH[:, :])
                        p.dma(kt["s_v"][:, kv * 128:(kv + 1) * 128].rearrange(
                            "(b p) n -> p b n", p=128), v3(VT[:, :], 128))
                    for qt in range(TT):
                        QH = qh[qt % 2]
                        OB = ob[qt % 2]
                        qsl = slice(qt * 512, (qt + 1) * 512)
                        p.dma(kt["s_q"][h * 128:(h + 1) * 128, qsl], QH[:, :])
                        nkc = 4 * qt + 4
                        for kc in range(nkc):
                            SS = ps_s[kc % 4]
                            AT = at[kc % 4]
                            p.step("tensor", lambda E, kc=kc, KH=KH, QH=QH,
                                   SS=SS: E.matmul(
                                SS[:, :], KH[:, kc * 128:(kc + 1) * 128],
                                QH[:, :], start=True, stop=True),
                                r=(KH, QH), w=(SS,))
                            p.step("scalar", lambda E, SS=SS, AT=AT:
                                   E.activation(
                                out=AT[:, :], in_=SS[:, :], func=AF.Exp,
                                bias=-MSH, scale=ISQ), r=(SS, cst), w=(AT,))
                            r = kc - 4 * qt
                            if r >= 0:
                                p.step("vector", lambda E, r=r, AT=AT:
                                       E.tensor_mul(
                                    AT[:, :], AT[:, :],
                                    msk[:, r * 512:(r + 1) * 512]),
                                    r=(AT, msk), w=(AT,))
                            p.step("tensor", lambda E, kc=kc, nkc=nkc, VT=VT,
                                   AT=AT: E.matmul(
                                ps_o[:, :], VT[:, kc * 128:(kc + 1) * 128],
                                AT[:, :], start=(kc == 0), stop=(kc == nkc - 1)),
                                r=(VT, AT),
                                w=(ps_o,) if kc == 0 else (),
                                accum=(ps_o,) if kc else ())
                            p.step("tensor", lambda E, kc=kc, nkc=nkc, AT=AT:
                                   E.matmul(
                                ps_d[:, :], ones_col_bf[:, :], AT[:, :],
                                start=(kc == 0), stop=(kc == nkc - 1)),
                                r=(ones_col_bf, AT),
                                w=(ps_d,) if kc == 0 else (),
                                accum=(ps_d,) if kc else ())
                        p.step("vector", lambda E: E.reciprocal(
                            rd[:, :], ps_d[0:1, :]), r=(ps_d,), w=(rd,))
                        p.step("tensor", lambda E: E.matmul(
                            ps_b[:, :], ones_row[:, :], rd[:, :],
                            start=True, stop=True), r=(ones_row, rd), w=(ps_b,))
                        p.step("vector", lambda E: E.tensor_copy(
                            out=rdb[:, :], in_=ps_b[:, :]), r=(ps_b,), w=(rdb,))
                        p.step("vector", lambda E, OB=OB: E.tensor_mul(
                            OB[:, :], ps_o[:, :], rdb[:, :]),
                            r=(ps_o, rdb), w=(OB,))
                        p.dma(OB[:, :], kt["s_o"][h * 128:(h + 1) * 128, qsl],
                              q="scalar")

    def ph_gla_prep(tag):
        with ExitStack() as st:
            sbl = lambda n, sh, dt: st.enter_context(nc.sbuf_tensor(tag + n, sh, dt))
            ab = [sbl("_a%d" % i, [128, T], F32) for i in range(2)]
            bbb = [sbl("_b%d" % i, [128, T], F32) for i in range(2)]
            cb = [sbl("_c%d" % i, [128, T], F32) for i in range(2)]
            db = [sbl("_d%d" % i, [128, T], F32) for i in range(2)]
            zz = sbl("_z", [128, T], F32)
            obb = [sbl("_o%d" % i, [128, T], BF16) for i in range(2)]
            ob2b = [sbl("_p%d" % i, [128, T], BF16) for i in range(2)]
            with nc.psum_tensor(tag + "_ptA", [128, 128], BF16) as pstA, \
                 nc.psum_tensor(tag + "_ptB", [128, 128], BF16) as pstB:
                pst = [pstA, pstB]
                p.step("vector", lambda E: E.memset(zz[:, :], 0.0), w=(zz,))
                for c in range(DC):
                    pr = c % 2
                    a, b, c_, d_, obx, ob2 = (ab[pr], bbb[pr], cb[pr], db[pr],
                                              obb[pr], ob2b[pr])
                    PT = pst[pr]
                    rs = slice(c * 128, (c + 1) * 128)
                    p.dma(kt["s_t1"][rs, :], a[:, :])
                    p.step("scalar", lambda E, a=a, obx=obx: E.activation(
                        out=obx[:, :], in_=a[:, :], func=AF.Relu, scale=ISQ),
                        r=(a, cst), w=(obx,))
                    p.dma(obx[:, :], kt["s_q"][rs, :], q="scalar")
                    p.dma(kt["s_kr"][rs, :], a[:, :])
                    p.step("scalar", lambda E, a=a, b=b: E.activation(
                        out=b[:, :], in_=a[:, :], func=AF.Sigmoid),
                        r=(a,), w=(b,))
                    p.step("scalar", lambda E, a=a, c_=c_: E.activation(
                        out=c_[:, :], in_=a[:, :], func=AF.Sigmoid, scale=-1.0),
                        r=(a, cst), w=(c_,))
                    p.step("scalar", lambda E, c_=c_: E.activation(
                        out=c_[:, :], in_=c_[:, :], func=AF.Ln),
                        r=(c_,), w=(c_,))
                    p.step("vector", lambda E, c_=c_, d_=d_: E.tensor_tensor_scan(
                        out=d_[:, :], data0=c_[:, :], data1=zz[:, :],
                        initial=0.0, op0=ALU.add, op1=ALU.add),
                        r=(c_, zz), w=(d_,))
                    p.step("vector", lambda E, a=a, d_=d_: E.tensor_copy(
                        out=a[:, 0:32], in_=d_[:, 0:32]), r=(d_,), w=(a,))
                    p.step("vector", lambda E, a=a, d_=d_: E.tensor_tensor(
                        out=a[:, 32:T].rearrange("p (bb cc) -> p bb cc", cc=32),
                        in0=d_[:, 32:T].rearrange("p (bb cc) -> p bb cc", cc=32),
                        in1=d_[:, 31:T - 32:32].unsqueeze(2).broadcast_to(
                            [128, T // 32 - 1, 32]),
                        op=ALU.subtract), r=(d_,), w=(a,))
                    p.step("vector", lambda E, a=a: E.tensor_scalar_max(
                        a[:, :], a[:, :], CLAMP), r=(a,), w=(a,))
                    p.step("scalar", lambda E, a=a, d_=d_: E.activation(
                        out=d_[:, 0:NCHUNK], in_=a[:, 31:T:32], func=AF.Exp),
                        r=(a,), w=(d_,))
                    p.dma(d_[:, 0:NCHUNK], kt["s_lam"][rs, :], q="scalar")
                    p.step("scalar", lambda E, a=a, c_=c_: E.activation(
                        out=c_[:, :], in_=a[:, :], func=AF.Exp),
                        r=(a,), w=(c_,))
                    p.dma(kt["s_q"][rs, :], ob2[:, :])
                    p.step("vector", lambda E, obx=obx, c_=c_, ob2=ob2:
                           E.tensor_mul(obx[:, :], c_[:, :], ob2[:, :]),
                           r=(c_, ob2), w=(obx,))
                    p.dma(obx[:, :], kt["s_gq"][rs, :], q="scalar")
                    p.step("scalar", lambda E, a=a, c_=c_: E.activation(
                        out=c_[:, :], in_=a[:, :], func=AF.Exp, scale=-1.0),
                        r=(a, cst), w=(c_,))
                    p.step("vector", lambda E, ob2=ob2, c_=c_, b=b:
                           E.tensor_mul(ob2[:, :], c_[:, :], b[:, :]),
                           r=(c_, b), w=(ob2,))
                    p.dma(ob2[:, :], kt["s_gk"][rs, :], q="scalar")
                    p.step("vector", lambda E, a=a, c_=c_: E.tensor_tensor(
                        out=c_[:, :].rearrange("p (bb cc) -> p bb cc", cc=32),
                        in0=a[:, 31:T:32].unsqueeze(2).broadcast_to(
                            [128, T // 32, 32]),
                        in1=a[:, :].rearrange("p (bb cc) -> p bb cc", cc=32),
                        op=ALU.subtract), r=(a,), w=(c_,))
                    p.step("scalar", lambda E, c_=c_: E.activation(
                        out=c_[:, :], in_=c_[:, :], func=AF.Exp),
                        r=(c_,), w=(c_,))
                    p.step("vector", lambda E, obx=obx, c_=c_, b=b:
                           E.tensor_mul(obx[:, :], c_[:, :], b[:, :]),
                           r=(c_, b), w=(obx,))
                    for tc in range(16):
                        p.step("tensor", lambda E, tc=tc, obx=obx, PT=PT:
                               E.transpose(PT[:, :],
                                           obx[:, tc * 128:(tc + 1) * 128],
                                           ident[:, :]),
                               r=(obx, ident), w=(PT,))
                        p.step("vector", lambda E, ob2=ob2, PT=PT, tc=tc:
                               E.tensor_copy(
                            out=ob2[:, tc * 128:(tc + 1) * 128], in_=PT[:, :]),
                            r=(PT,), w=(ob2,))
                    p.dma(v3(ob2[:, :], 128),
                          kt["s_khT"][:, rs].rearrange("(b q) n -> q b n",
                                                       q=128), q="scalar")

    def ph_gla_scan(tag):
        with ExitStack() as st:
            sbl = lambda n, sh, dt: st.enter_context(nc.sbuf_tensor(tag + n, sh, dt))
            Sf = [sbl("_Sf%d" % i, [128, 128], F32) for i in range(2)]
            Sb = [sbl("_Sb%d" % i, [128, 128], BF16) for i in range(2)]
            gq = [sbl("_gq%d" % i, [128, T], BF16) for i in range(2)]
            gk = [sbl("_gk%d" % i, [128, T], BF16) for i in range(2)]
            khT = [sbl("_kT%d" % i, [32, NCHUNK * 128], BF16) for i in range(2)]
            vg = [sbl("_vg%d" % i, [32, NCHUNK * 128], BF16) for i in range(2)]
            lam = [sbl("_lm%d" % i, [128, NCHUNK], F32) for i in range(2)]
            pm = [sbl("_pm%d" % i, [32, 32], BF16) for i in range(2)]
            ob = [sbl("_ob%d" % i, [128, 512], BF16) for i in range(2)]
            with ExitStack() as pstk:
                pp = lambda n: pstk.enter_context(
                    nc.psum_tensor(tag + n, [32, 32], F32))
                po = lambda n, sh: pstk.enter_context(
                    nc.psum_tensor(tag + n, sh, F32))
                ps_p = [pp("_pp0"), pp("_pp1")]
                ps_o = [po("_po0", [128, 512]), po("_po1", [128, 512])]
                _pu = po("_pu0", [128, 128])
                ps_u = [_pu, _pu]
                for h in range(16):
                    pr = h % 2
                    GQ, GK, KT, VG, LM = (gq[pr], gk[pr], khT[pr], vg[pr],
                                          lam[pr])
                    SF, SB, PM, OB = Sf[pr], Sb[pr], pm[pr], ob[pr]
                    PP, PO, PU = ps_p[pr], ps_o[pr], ps_u[pr]
                    rs = slice(h * 128, (h + 1) * 128)
                    p.dma(kt["s_gq"][rs, :], GQ[:, :])
                    p.dma(kt["s_gk"][rs, :], GK[:, :])
                    p.dma(kt["s_lam"][rs, :], LM[:, :])
                    p.dma(kt["s_khT"][:, rs].rearrange(
                        "(bb pp2) n -> pp2 bb n", pp2=32), v3(KT[:, :], 128))
                    p.dma(kt["s_vg"][:, rs].rearrange(
                        "(bb pp2) n -> pp2 bb n", pp2=32), v3(VG[:, :], 128))
                    p.step("vector", lambda E, SF=SF: E.memset(SF[:, :], 0.0),
                           w=(SF,))
                    p.step("vector", lambda E, SB=SB: E.memset(SB[:, :], 0.0),
                           w=(SB,))
                    for B in range(NCHUNK):
                        tb = slice(B * 32, (B + 1) * 32)
                        bsl = slice(B * 128, (B + 1) * 128)
                        osl = slice((B % 16) * 32, (B % 16) * 32 + 32)
                        p.step("tensor", lambda E, tb=tb, GK=GK, GQ=GQ, PP=PP:
                               E.matmul(PP[:, :], GK[:, tb], GQ[:, tb],
                                        start=True, stop=True),
                               r=(GK, GQ), w=(PP,))
                        p.step("vector", lambda E, PM=PM, PP=PP: E.tensor_mul(
                            PM[:, :], PP[:, :], gmask[:, :]),
                            r=(PP, gmask), w=(PM,))
                        p.step("tensor", lambda E, tb=tb, osl=osl, SB=SB,
                               GQ=GQ, PO=PO: E.matmul(
                            PO[:, osl], SB[:, :], GQ[:, tb],
                            start=True, stop=False),
                            r=(SB, GQ), w=(PO,) if B % 16 == 0 else (),
                            accum=(PO,) if B % 16 else ())
                        p.step("tensor", lambda E, bsl=bsl, osl=osl, VG=VG,
                               PM=PM, PO=PO: E.matmul(
                            PO[:, osl], VG[:, bsl], PM[:, :],
                            start=False, stop=True),
                            r=(VG, PM), accum=(PO,))
                        p.step("tensor", lambda E, bsl=bsl, KT=KT, VG=VG,
                               PU=PU: E.matmul(
                            PU[:, :], KT[:, bsl], VG[:, bsl],
                            start=True, stop=True), r=(KT, VG), w=(PU,))
                        p.step("vector", lambda E, B=B, SF=SF, LM=LM, PU=PU:
                               E.scalar_tensor_tensor(
                            out=SF[:, :], in0=SF[:, :], scalar=LM[:, B:B + 1],
                            in1=PU[:, :], op0=ALU.mult, op1=ALU.add),
                            r=(SF, LM, PU), w=(SF,))
                        p.step("vector", lambda E, SB=SB, SF=SF: E.tensor_copy(
                            out=SB[:, :], in_=SF[:, :]), r=(SF,), w=(SB,))
                        if B % 16 == 15:
                            p.step("vector", lambda E, OB=OB, PO=PO:
                                   E.tensor_copy(out=OB[:, :], in_=PO[:, :]),
                                   r=(PO,), w=(OB,))
                            t0 = (B // 16) * 512
                            p.dma(OB[:, :], kt["s_o"][rs, t0:t0 + 512],
                                  q="scalar")

    def ph_quant(src, dst, rq_row_idx, sw_idx, Din, tag):
        dch = Din // 128
        GW = 2048
        NBUF = 2 if dch <= 16 else 1
        with ExitStack() as st:
            sbl = lambda n, sh, dt: st.enter_context(nc.sbuf_tensor(tag + n, sh, dt))
            xs = [sbl("_x%d" % i, [128, dch * 512], F32) for i in range(NBUF)]
            mx = sbl("_m", [128, 512], F32)
            mn = sbl("_mn", [128, 512], F32)
            mxb = sbl("_mb", [128, 512], BF16)
            colr = sbl("_cr", [128, 4], F32)
            colb = sbl("_cb", [128, 4], BF16)
            rowb = sbl("_rb", [1, 512], F32)
            rowq = sbl("_rq", [1, 512], F32)
            swc = sbl("_sw", [1, 1], F32)
            sxb = sbl("_sx", [128, 512], F32)
            t1b = [sbl("_t1%d" % i, [128, GW], F32) for i in range(2)]
            t2b = [sbl("_t2%d" % i, [128, GW], F32) for i in range(2)]
            obb = [sbl("_obq%d" % i, [128, GW], BF16) for i in range(2)]
            with nc.psum_tensor(tag + "_pt", [128, 128], BF16) as pst, \
                 nc.psum_tensor(tag + "_pr", [1, 512], F32) as psr, \
                 nc.psum_tensor(tag + "_pb", [128, 512], F32) as psb:
                p.dma(kt["s_wsc"][0:1, sw_idx:sw_idx + 1], swc[:, :])
                for tt in range(TT):
                    X = xs[tt % NBUF]
                    tsl = slice(tt * 512, (tt + 1) * 512)
                    p.dma(fmv(src, tsl), v3(X[:, :], 512))
                    p.step("vector", lambda E, X=X: E.tensor_copy(
                        out=mx[:, :], in_=X[:, 0:512]), r=(X,), w=(mx,))
                    p.step("vector", lambda E, X=X: E.tensor_copy(
                        out=mn[:, :], in_=X[:, 0:512]), r=(X,), w=(mn,))
                    for c in range(1, dch):
                        csl = slice(c * 512, (c + 1) * 512)
                        p.step("vector", lambda E, csl=csl, X=X: E.tensor_tensor(
                            out=mx[:, :], in0=mx[:, :], in1=X[:, csl],
                            op=ALU.max), r=(mx, X), w=(mx,))
                        p.step("vector", lambda E, csl=csl, X=X: E.tensor_tensor(
                            out=mn[:, :], in0=mn[:, :], in1=X[:, csl],
                            op=ALU.min), r=(mn, X), w=(mn,))
                    p.step("vector", lambda E: E.tensor_scalar_mul(
                        mn[:, :], mn[:, :], -1.0), r=(mn,), w=(mn,))
                    p.step("vector", lambda E: E.tensor_tensor(
                        out=mx[:, :], in0=mx[:, :], in1=mn[:, :], op=ALU.max),
                        r=(mx, mn), w=(mx,))
                    p.step("vector", lambda E: E.tensor_copy(
                        out=mxb[:, :], in_=mx[:, :]), r=(mx,), w=(mxb,))
                    for q4 in range(4):
                        p.step("tensor", lambda E, q4=q4: E.transpose(
                            pst[:, :], mxb[:, q4 * 128:(q4 + 1) * 128],
                            ident[:, :]), r=(mxb, ident), w=(pst,))
                        p.step("vector", lambda E, q4=q4: E.reduce_max(
                            colr[:, q4:q4 + 1], pst[:, :], axis=AX.X,
                            apply_absolute_value=True), r=(pst,), w=(colr,))
                    for q4 in range(4):
                        p.step("vector", lambda E, q4=q4: E.tensor_copy(
                            out=colb[:, q4:q4 + 1], in_=colr[:, q4:q4 + 1]),
                            r=(colr,), w=(colb,))
                        p.step("tensor", lambda E, q4=q4: E.matmul(
                            psr[0:1, q4 * 128:(q4 + 1) * 128],
                            colb[:, q4:q4 + 1], ident[:, :],
                            start=True, stop=True), r=(colb, ident), w=(psr,))
                    p.step("vector", lambda E: E.tensor_scalar_max(
                        rowb[:, :], psr[0:1, :], 1e-5), r=(psr,), w=(rowb,))
                    p.step("vector", lambda E: E.reciprocal(
                        rowq[:, :], rowb[:, :]), r=(rowb,), w=(rowq,))
                    p.step("vector", lambda E: E.tensor_scalar_mul(
                        rowq[:, :], rowq[:, :], 127.0), r=(rowq,), w=(rowq,))
                    p.step("vector", lambda E: E.tensor_scalar(
                        out=rowb[:, :], in0=rowb[:, :], scalar1=swc[0:1, 0:1],
                        scalar2=1.0 / 127.0, op0=ALU.mult, op1=ALU.mult),
                        r=(rowb, swc), w=(rowb,))
                    p.dma(rowb[:, :],
                          kt["s_rows"][rq_row_idx:rq_row_idx + 1, tsl],
                          q="scalar")
                    p.step("vector", lambda E: E.tensor_copy(
                        out=colb[:, 0:1] if False else mxb[0:1, 0:512],
                        in_=rowb[:, :]), r=(rowb,), w=(mxb,))
                    p.dma(mxb[0:1, 0:512],
                          kt["s_rows_bf"][rq_row_idx:rq_row_idx + 1, tsl],
                          q="scalar")
                    p.step("tensor", lambda E: E.matmul(
                        psb[:, :], ones_row[:, :], rowq[0:1, :],
                        start=True, stop=True), r=(ones_row, rowq), w=(psb,))
                    p.step("vector", lambda E: E.tensor_copy(
                        out=sxb[:, :], in_=psb[:, :]), r=(psb,), w=(sxb,))
                    for c in range(dch):
                        csl = slice(c * 512, (c + 1) * 512)
                        p.step("vector", lambda E, csl=csl, X=X: E.tensor_mul(
                            X[:, csl], X[:, csl], sxb[:, :]),
                            r=(X, sxb), w=(X,))
                    ngr = (dch * 512) // GW
                    for g in range(ngr):
                        pr2 = g % 2
                        eng = "vector"
                        t1, t2, obq = t1b[pr2], t2b[pr2], obb[pr2]
                        gsl = slice(g * GW, (g + 1) * GW)
                        xv = X[:, gsl]
                        em_round2(eng, xv, t1, t2, GW, X, t1, t2)
                        p.step(eng, lambda E, t2=t2, obq=obq: E.tensor_copy(
                            out=obq[:, :], in_=t2[:, :]), r=(t2,), w=(obq,))
                        c0 = gsl.start // 512
                        c1 = gsl.stop // 512
                        p.dma(v3(obq[:, :], 512),
                              dst[c0 * 128:c1 * 128, tsl].rearrange(
                                  "(c p) t -> p c t", p=128), q="scalar")

    def ph_gelu_mult(tag):
        with ExitStack() as st:
            sbl = lambda n, sh, dt: st.enter_context(nc.sbuf_tensor(tag + n, sh, dt))
            ab = [sbl("_a%d" % i, [128, T], F32) for i in range(2)]
            bb2 = [sbl("_b%d" % i, [128, T], F32) for i in range(2)]
            obb = [sbl("_o%d" % i, [128, T], BF16) for i in range(2)]
            for c in range(128):
                pr = c % 2
                eng = "vector"
                a, b, ob = ab[pr], bb2[pr], obb[pr]
                rs = slice(c * 128, (c + 1) * 128)
                p.dma(kt["s_y1"][rs, :], a[:, :])
                p.dma(kt["s_y2"][rs, :], b[:, :])
                p.step("scalar", lambda E, a=a: E.activation(
                    out=a[:, :], in_=a[:, :], func=AF.Gelu_apprx_tanh),
                    r=(a,), w=(a,))
                p.step(eng, lambda E, ob=ob, a=a, b=b: E.tensor_mul(
                    ob[:, :], a[:, :], b[:, :]), r=(a, b), w=(ob,))
                p.dma(ob[:, :], kt["s_int"][rs, :], q="scalar")

    def ph_silu_mult(tag):
        with ExitStack() as st:
            sbl = lambda n, sh, dt: st.enter_context(nc.sbuf_tensor(tag + n, sh, dt))
            ab = [sbl("_a%d" % i, [128, T], F32) for i in range(2)]
            bb2 = [sbl("_b%d" % i, [128, T], F32) for i in range(2)]
            obb = [sbl("_o%d" % i, [128, T], F32) for i in range(2)]
            for c in range(44):
                pr = c % 2
                eng = "vector"
                a, b, ob = ab[pr], bb2[pr], obb[pr]
                p.dma(kt["s_y1"][c * 128:(c + 1) * 128, :], a[:, :])
                p.dma(kt["s_y1"][(44 + c) * 128:(45 + c) * 128, :], b[:, :])
                p.step("scalar", lambda E, a=a: E.activation(
                    out=a[:, :], in_=a[:, :], func=AF.Silu), r=(a,), w=(a,))
                p.step(eng, lambda E, ob=ob, a=a, b=b: E.tensor_mul(
                    ob[:, :], a[:, :], b[:, :]), r=(a, b), w=(ob,))
                p.dma(ob[:, :], kt["s_h"][c * 128:(c + 1) * 128, :], q="scalar")

    def pump(g, n):
        if g is None:
            return None
        p.exempt = True
        try:
            for _ in range(n):
                next(g)
        except StopIteration:
            g = None
        p.exempt = False
        return g

    def exhaust(g):
        g = pump(g, 10 ** 9)
        p.barrier_full()
        return None

    gen_g = ph_dequant(kt["PKg"], kt["SCg"], kt["s_wgq"], 2048, 16384, "dqg")
    gen_u = ph_dequant(kt["PKu"], kt["SCu"], kt["s_wuq"], 2048, 16384, "dqu")
    gen_d = ph_dequant(kt["PKd"], kt["SCd"], kt["s_wdq"], 16384, 2048, "dqd")
    gen_t1 = None
    gen_t2 = None

    for s in range(NSEQ):
        xin = kt["xT"][s]
        xcur = kt["s_x"]
        sfx = "%d" % s
        ph_layernorm(xin, kt["s_xn"], D, tag="ln0s" + sfx)
        p.barrier()
        gen_g = pump(gen_g, 8)
        ph_linear(kt["Wq"], kt["s_xn"], kt["s_q"], D, D, "pq" + sfx)
        p.barrier()
        gen_g = pump(gen_g, 12)
        ph_linear(kt["Wk"], kt["s_xn"], kt["s_k"], D, 512, "pk" + sfx)
        p.barrier()
        gen_g = pump(gen_g, 6)
        ph_linear_tok(kt["Wv"], kt["s_xn"], kt["s_v"], 512, "pv" + sfx)
        p.barrier()
        gen_g = pump(gen_g, 8)
        ph_rope(kt["s_q"], D, "rpq" + sfx)
        p.barrier()
        gen_g = pump(gen_g, 6)
        ph_rope(kt["s_k"], 512, "rpk" + sfx)
        p.barrier()
        gen_g = pump(gen_g, 8)
        ph_attn("at" + sfx)
        p.barrier()
        gen_g = pump(gen_g, 16)
        gen_u = pump(gen_u, 40)
        ph_linear(kt["Wo"], kt["s_o"], kt["s_z"], D, D, "po" + sfx,
                  out_dtype=F32)
        p.barrier()
        gen_u = pump(gen_u, 12)
        ph_resadd(xin, kt["s_z"], xcur, "ra0" + sfx)
        p.barrier()
        gen_d = pump(gen_d, 12)
        if DEBUG and s == 0:
            p.dma(xcur[:, :], kt["dbg0"][:, :])
        ph_layernorm(xcur, kt["s_xn"], D, tag="ln1s" + sfx)
        p.barrier()
        if s == 0:
            gen_g = exhaust(gen_g)
            gen_u = exhaust(gen_u)
        ph_linear_multi([(kt["s_wgq"], kt["s_y1"], "y1"),
                         (kt["s_wuq"], kt["s_y2"], "y2")],
                        kt["s_xn"], D, 16384, "fgu" + sfx,
                        post=lambda: em_gelu_mult("gm" + sfx))
        p.barrier()
        gen_d = pump(gen_d, 30)
        if s == 0:
            gen_d = exhaust(gen_d)
            dq_stk.close()
        ph_linear(kt["s_wdq"], kt["s_int"], kt["s_z"], 16384, D,
                  "fd" + sfx, w_dtype=BF16, out_dtype=F32)
        p.barrier()
        if s == 0:
            gen_t1 = ph_ternarize(kt["GWT"], kt["s_g8"], D, 11264, 0, "tg")
        gen_t1 = pump(gen_t1, 30)
        ph_resadd(xcur, kt["s_z"], xcur, "ra1" + sfx)
        p.barrier()
        if DEBUG and s == 0:
            p.dma(xcur[:, :], kt["dbg1"][:, :])
        ph_layernorm(xcur, kt["s_xn"], D, tag="ln2s" + sfx)
        p.barrier()
        gen_t1 = pump(gen_t1, 20)
        ph_linear(kt["W1q"], kt["s_xn"], kt["s_t1"], D, D, "gq" + sfx,
                  out_dtype=F32)
        p.barrier()
        gen_t1 = pump(gen_t1, 20)
        ph_linear(kt["W1k"], kt["s_xn"], kt["s_kr"], D, D, "gk" + sfx,
                  out_dtype=F32)
        p.barrier()
        gen_t1 = pump(gen_t1, 20)
        ph_linear_tok(kt["W1i"], kt["s_xn"], kt["s_vg"], D, "gv" + sfx)
        p.barrier()
        gen_t1 = pump(gen_t1, 20)
        ph_gla_prep("gp" + sfx)
        p.barrier()
        gen_t1 = pump(gen_t1, 30)
        ph_gla_scan("gs" + sfx)
        p.barrier()
        if s == 0:
            gen_t1 = exhaust(gen_t1)
            gen_t2 = ph_ternarize(kt["DWT"], kt["s_d8"], 5632, D, 1, "td")
            gen_t2 = exhaust(gen_t2)
        ph_layernorm(kt["s_o"], kt["s_o"], D, src_dtype=BF16,
                     mean_center=False, gamma_hbm=kt["gw"], tag="gn" + sfx)
        p.barrier()
        ph_linear(kt["W1o"], kt["s_o"], kt["s_z"], D, D, "go" + sfx,
                  out_dtype=F32)
        p.barrier()
        ph_resadd(xcur, kt["s_z"], xcur, "ra2" + sfx)
        p.barrier()
        if DEBUG and s == 0:
            p.dma(xcur[:, :], kt["dbg2"][:, :])
        ph_layernorm(xcur, kt["s_t1"], D, out_dtype=F32, tag="ln3s" + sfx)
        p.barrier()
        ph_layernorm(kt["s_t1"], kt["s_xr"], D, mean_center=False,
                     gamma_hbm=kt["gnw"], out_dtype=F32, tag="rm1s" + sfx)
        p.barrier()
        ph_quant(kt["s_xr"], kt["s_xq"], 0, 0, D, "q1" + sfx)
        p.barrier()
        ph_linear(kt["s_g8"], kt["s_xq"], kt["s_y1"], D, 11264,
                  "bg" + sfx, w_dtype=BF16, out_dtype=F32,
                  row_scale=kt["s_rows_bf"][0:1, :], out_keys="yb",
                  post=lambda: em_silu_mult("sm" + sfx))
        p.barrier()
        ph_layernorm(kt["s_h"], kt["s_hr"], 5632, mean_center=False,
                     gamma_hbm=kt["dnw"], out_dtype=F32, tag="rm2s" + sfx)
        p.barrier()
        ph_quant(kt["s_hr"], kt["s_hq"], 1, 1, 5632, "q2" + sfx)
        p.barrier()
        ph_linear(kt["s_d8"], kt["s_hq"], kt["s_z"], 5632, D,
                  "bd" + sfx, w_dtype=BF16, out_dtype=F32,
                  row_scale=kt["s_rows_bf"][1:2, :])
        p.barrier()
        ph_resadd(xcur, kt["s_z"], kt["yT"][s], "ra3" + sfx)
        p.barrier()
        if DEBUG and s == 0:
            p.dma(kt["yT"][s][:, :], kt["dbg3"][:, :])

    p.finalize()
    k.stk.close()
    return k


def _consts():
    ident = np.eye(128, dtype=NPBF)
    rmat = np.zeros((128, 128), dtype=NPBF)
    for m in range(64):
        rmat[m + 64, m] = -1.0
        rmat[m, m + 64] = 1.0
    e4g = np.zeros((4, 128), dtype=NPBF)
    for b in range(4):
        e4g[b, 16 * b:16 * (b + 1)] = 1
        e4g[b, 64 + 16 * b:64 + 16 * (b + 1)] = 1
    inv = np.power(10000.0, -2.0 * np.arange(64, dtype=np.float64) / 128.0)
    ang = np.arange(T, dtype=np.float64)[None, :] * inv[:, None]
    cosT = np.zeros((128, T), np.float32)
    sinT = np.zeros((128, T), np.float32)
    cosT[0:64] = np.cos(ang)
    cosT[64:128] = np.cos(ang)
    sinT[0:64] = np.sin(ang)
    sinT[64:128] = np.sin(ang)
    amask = np.zeros((512, 512), dtype=NPBF)
    ar = np.arange(512)
    for r in range(4):
        for kk2 in range(128):
            amask[r * 128 + kk2, :] = (ar >= (r * 128 + kk2))
    gmask = np.zeros((32, 32), dtype=NPBF)
    for j in range(32):
        gmask[j, j:] = 1
    return dict(ident=ident, rmat=rmat, e4g=e4g, cosT=cosT, sinT=sinT,
                amask=amask, gmask=gmask)


def _pk_bytes(w_packed_int32):
    v = np.ascontiguousarray(w_packed_int32).view(np.uint8)[:, 0::4]
    return np.ascontiguousarray(v.T)


def kernel(**inputs):
    cons = _consts()
    shared = dict(
        Wq=inputs["L0_attn_Wq"], Wk=inputs["L0_attn_Wk"],
        Wv=inputs["L0_attn_Wv"], Wo=inputs["L0_attn_Wo"],
        W1q=inputs["L1_attn_Wq"], W1k=inputs["L1_attn_Wk"],
        W1i=inputs["L1_attn_Wi"], W1o=inputs["L1_attn_Wo"],
        PKg=_pk_bytes(inputs["L0_ffn_gate_W_packed"]),
        PKu=_pk_bytes(inputs["L0_ffn_up_W_packed"]),
        PKd=_pk_bytes(inputs["L0_ffn_down_W_packed"]),
        SCg=np.ascontiguousarray(inputs["L0_ffn_gate_scales"].T),
        SCu=np.ascontiguousarray(inputs["L0_ffn_up_scales"].T),
        SCd=np.ascontiguousarray(inputs["L0_ffn_down_scales"].T),
        GWT=np.ascontiguousarray(inputs["L1_ffn_gate_w"].T),
        DWT=np.ascontiguousarray(inputs["L1_ffn_down_w"].T),
        gw=np.asarray(inputs["L1_attn_g_weight"]).reshape(16, 128).T,
        gnw=np.asarray(inputs["L1_ffn_gate_norm_w"]).reshape(16, 128).T,
        dnw=np.asarray(inputs["L1_ffn_down_norm_w"]).reshape(44, 128).T, **cons)
    shared = {k_: np.ascontiguousarray(v) for k_, v in shared.items()}
    x = inputs["L0_attn_x"]
    in_maps = []
    for core in range(8):
        xs = np.ascontiguousarray(
            np.asarray(x)[core * 2:core * 2 + 2].transpose(0, 2, 1)
        ).astype(np.float32)
        m = dict(shared)
        m["xT"] = xs
        in_maps.append(m)
    kk = build()
    res = run_bass_kernel_spmd(kk.nc, in_maps, core_ids=list(range(8)))
    out = np.empty((16, T, D), np.float32)
    for core in range(8):
        yt = res.results[core]["yT"]
        out[core * 2] = np.asarray(yt[0]).T
        out[core * 2 + 1] = np.asarray(yt[1]).T
    return out


# version 07 (final)
# speedup vs baseline: 1.0495x; 1.0430x over previous
import numpy as np
import ml_dtypes
from contextlib import ExitStack
import concourse.bass as bass
import concourse.mybir as mybir
from concourse.bass_utils import run_bass_kernel_spmd

F32 = mybir.dt.float32
BF16 = mybir.dt.bfloat16
U16 = mybir.dt.uint16
U8 = mybir.dt.uint8
FP8 = mybir.dt.float8e4
AF = mybir.ActivationFunctionType
ALU = mybir.AluOpType
AX = mybir.AxisListType
NPBF = ml_dtypes.bfloat16

D = 2048
T = 2048
NSEQ = 2
DC = 16
TT = 4
EPS = 1e-5
MSH = 15.0
C_GLA = 32
NCHUNK = T // C_GLA
CLAMP = -85.0
ISQ = float(1.0 / np.sqrt(128.0))
M0F, M07, M08, M01 = 0x0F0F, 0x0707, 0x0808, 0x0101

ENGS = ("tensor", "vector", "scalar", "gpsimd", "sync")
DEBUG = False


class P:

    NSLOT = 8

    def __init__(self, nc):
        self.nc = nc
        self.ops = {e: [] for e in ENGS}
        self.last = {}
        self.sems = {}
        for e in ENGS:
            self.sems["e" + e] = self._mk("es_" + e)
        for qn in ("sync", "scalar"):
            for s in range(self.NSLOT):
                self.sems["d%s%d" % (qn, s)] = self._mk("ds_%s%d" % (qn, s))
        self.dcnt = {qn: 0 for qn in ("sync", "scalar")}
        self.slotprev = {}
        self.lastcomp = {}
        self.barrier_deps = []
        self.exempt = False
        self.exempt_last = {}

    def _mk(self, name):
        ctx = self.nc.semaphore(name)
        h = ctx.__enter__()
        return h

    @staticmethod
    def _key(b):
        if isinstance(b, str):
            return b
        n = getattr(b, "name", None)
        if n is not None:
            return n
        t = getattr(b, "tensor", None)
        if t is not None:
            return t.name
        raise TypeError(b)

    def _deps(self, r, w, accum):
        deps = []
        for b in r:
            e = self.last.get(self._key(b))
            if e and e[0]:
                deps.append(e[0])
        acc_keys = {self._key(b) for b in accum}
        for b in list(w) + list(accum):
            kb = self._key(b)
            e = self.last.get(kb)
            if e:
                if e[0] and kb not in acc_keys:
                    deps.append(e[0])
                deps.extend(e[1])
        return deps

    def _record(self, eng, fn, r, w, accum, is_dma):
        deps = self._deps(r, w, accum)
        deps.extend(self.barrier_deps)
        if is_dma:
            di = self.dcnt[eng]
            self.dcnt[eng] = di + 1
            slot = di % self.NSLOT
            semkey = "d%s%d" % (eng, slot)
            prev = self.slotprev.get((eng, slot))
            if prev is not None:
                deps.append(prev)
        else:
            semkey = "e" + eng
        idx = len(self.ops[eng])
        h = (eng, semkey, idx)
        if is_dma:
            self.slotprev[(eng, slot)] = h
        self.ops[eng].append([fn, deps, is_dma, False, semkey])
        if not is_dma:
            if self.exempt:
                self.exempt_last[eng] = h
            else:
                self.lastcomp[eng] = h
        for b in r:
            kb = self._key(b)
            self.last.setdefault(kb, [None, []])[1].append(h)
        for b in list(w) + list(accum):
            self.last[self._key(b)] = [h, []]
        return h

    def barrier(self):
        hs = list(self.lastcomp.values()) + list(self.slotprev.values())
        self.barrier_deps = hs

    def barrier_full(self):
        hs = (list(self.lastcomp.values()) + list(self.slotprev.values())
              + list(self.exempt_last.values()))
        self.barrier_deps = hs
        self.exempt_last = {}

    def step(self, eng, fn, r=(), w=(), accum=()):
        wk = {self._key(b) for b in w} | {self._key(b) for b in accum}
        rr = list(r) if "K0" in wk else list(r) + ["K0"]
        return self._record(eng, fn, rr, w, accum, False)

    def dma(self, src_ap, dst_ap, q="sync", r=(), w=(), nokey=False):
        def _f(E, s=src_ap, t=dst_ap):
            return E.dma_start(out=t, in_=s)
        if nokey:
            return self._record(q, _f, list(r) + ["K0"], list(w), (), True)
        return self._record(q, _f, list(r) + [src_ap, "K0"],
                            list(w) + [dst_ap], (), True)

    def finalize(self):
        needed = set()
        for e in ENGS:
            for op in self.ops[e]:
                for d in op[1]:
                    needed.add(d)
        val = {}
        counters = {k: 0 for k in self.sems}
        for e in ENGS:
            for idx, op in enumerate(self.ops[e]):
                semkey = op[4]
                if (e, semkey, idx) in needed or op[2]:
                    op[3] = True
                    counters[semkey] += 16 if op[2] else 1
                    val[(e, semkey, idx)] = counters[semkey]
                else:
                    op[3] = False
        finals = dict(counters)
        sems = self.sems
        wm = {e: {} for e in ENGS}

        def run_engine(E, e):
            for idx, (fn, deps, is_dma, inc, semkey0) in enumerate(self.ops[e]):
                byk = {}
                for (de, dk, di) in deps:
                    v = val.get((de, dk, di))
                    if v is None:
                        raise RuntimeError("unmarked dep")
                    if v > byk.get(dk, 0):
                        byk[dk] = v
                for dk, v in byk.items():
                    if v > wm[e].get(dk, 0):
                        E.wait_ge(sems[dk], v)
                        wm[e][dk] = v
                ins = fn(E)
                if inc:
                    ins.then_inc(sems[semkey0], 16 if is_dma else 1)

        with self.nc.Block() as blk:
            blk.tensor(lambda E: run_engine(E, "tensor"))
            blk.vector(lambda E: run_engine(E, "vector"))
            blk.scalar(lambda E: run_engine(E, "scalar"))
            blk.gpsimd(lambda E: run_engine(E, "gpsimd"))

            def sync_body(E):
                run_engine(E, "sync")
                for kk2, v in finals.items():
                    if v:
                        E.wait_ge(sems[kk2], v)
            blk.sync(sync_body)


def fmv(hbm_ap, tsl=None):
    a = hbm_ap if tsl is None else hbm_ap[:, tsl]
    return a.rearrange("(c p) t -> p c t", p=128)


def v3(sb_ap, tw):
    return sb_ap.rearrange("p (c t) -> p c t", t=tw)


class K:
    def __init__(self):
        self.nc = bass.Bass("TRN2", target_bir_lowering=False)
        self.stk = ExitStack()
        self.t = {}

    def dram_in(self, name, shape, dtype):
        self.t[name] = self.nc.dram_tensor(name, shape, dtype,
                                           kind="ExternalInput").ap()

    def dram_out(self, name, shape, dtype):
        self.t[name] = self.nc.dram_tensor(name, shape, dtype,
                                           kind="ExternalOutput").ap()

    def dram_tmp(self, name, shape, dtype):
        self.t[name] = self.nc.dram_tensor(name, shape, dtype,
                                           kind="Internal").ap()

    def sb(self, name, shape, dtype):
        return self.stk.enter_context(self.nc.sbuf_tensor(name, shape, dtype))


def build():
    k = K()
    nc = k.nc
    kt = k.t
    k.dram_in("xT", [NSEQ, D, T], F32)
    for nm, sh in [("Wq", [D, D]), ("Wk", [D, 512]), ("Wv", [D, 512]),
                   ("Wo", [D, D]), ("W1q", [D, D]), ("W1k", [D, D]),
                   ("W1i", [D, D]), ("W1o", [D, D])]:
        k.dram_in(nm, sh, F32)
    k.dram_in("PKg", [1024, 16384], U8)
    k.dram_in("PKu", [1024, 16384], U8)
    k.dram_in("PKd", [8192, 2048], U8)
    k.dram_in("SCg", [64, 16384], F32)
    k.dram_in("SCu", [64, 16384], F32)
    k.dram_in("SCd", [512, 2048], F32)
    k.dram_in("GWT", [D, 11264], F32)
    k.dram_in("DWT", [5632, D], F32)
    k.dram_in("gw", [128, 16], F32)
    k.dram_in("gnw", [128, 16], F32)
    k.dram_in("dnw", [128, 44], F32)
    k.dram_in("ident", [128, 128], BF16)
    k.dram_in("rmat", [128, 128], BF16)
    k.dram_in("e4g", [4, 128], BF16)
    k.dram_in("cosT", [128, T], F32)
    k.dram_in("sinT", [128, T], F32)
    k.dram_in("amask", [512, 512], BF16)
    k.dram_in("gmask", [32, 32], BF16)
    k.dram_out("yT", [NSEQ, D, T], F32)
    if DEBUG:
        for i in range(4):
            k.dram_out("dbg%d" % i, [D, T], F32)

    k.dram_tmp("s_x", [D, T], F32)
    k.dram_tmp("s_xn", [D, T], BF16)
    k.dram_tmp("s_q", [D, T], BF16)
    k.dram_tmp("s_k", [512, T], BF16)
    k.dram_tmp("s_v", [T, 512], BF16)
    k.dram_tmp("s_o", [D, T], BF16)
    k.dram_tmp("s_z", [D, T], F32)
    k.dram_tmp("s_wgq", [D, 16384], BF16)
    k.dram_tmp("s_wuq", [D, 16384], BF16)
    k.dram_tmp("s_wdq", [16384, D], BF16)
    k.dram_tmp("s_y1", [16384, T], F32)
    k.dram_tmp("s_y2", [16384, T], F32)
    k.dram_tmp("s_int", [16384, T], BF16)
    k.dram_tmp("s_t1", [D, T], F32)
    k.dram_tmp("s_kr", [D, T], F32)
    k.dram_tmp("s_ks", [D, T], F32)
    k.dram_tmp("s_ar", [D, T], F32)
    k.dram_tmp("s_gq", [D, T], BF16)
    k.dram_tmp("s_gk", [D, T], BF16)
    k.dram_tmp("s_kh", [D, T], BF16)
    k.dram_tmp("s_khT", [T, D], BF16)
    k.dram_tmp("s_vg", [T, D], BF16)
    k.dram_tmp("s_lam", [D, NCHUNK], F32)
    k.dram_tmp("s_xr", [D, T], F32)
    k.dram_tmp("s_xq", [D, T], BF16)
    k.dram_tmp("s_h", [5632, T], F32)
    k.dram_tmp("s_hr", [5632, T], F32)
    k.dram_tmp("s_hq", [5632, T], BF16)
    k.dram_tmp("s_g8", [D, 11264], BF16)
    k.dram_tmp("s_d8", [5632, D], BF16)
    k.dram_tmp("s_rows", [8, T], F32)
    k.dram_tmp("s_rows_bf", [8, T], BF16)
    k.dram_tmp("s_wsc", [1, 8], F32)

    p = P(nc)

    cst = k.sb("cst", [128, 16], F32)
    CVALS = [0.0, 1.0, -1.0, 0.5, float(EPS), MSH, -MSH, ISQ, 127.0, CLAMP,
             128.5, 1.0 / 127.0]
    for i, v in enumerate(CVALS):
        nc.const_aps.register(F32, v, cst[:, i:i + 1])
        p.step("vector", lambda E, i=i, v=v: E.memset(cst[:, i:i + 1], v),
               w=(cst, "K0"))
    ones_col = k.sb("ones_col", [128, 1], F32)
    ones_row = k.sb("ones_row", [1, 128], F32)
    ones_col_bf = k.sb("ones_col_bf", [128, 1], BF16)
    ones_row_bf = k.sb("ones_row_bf", [1, 128], BF16)
    p.step("vector", lambda E: E.memset(ones_col[:, :], 1.0),
           w=(ones_col, "K0"))
    p.step("vector", lambda E: E.memset(ones_row[:, :], 1.0),
           w=(ones_row, "K0"))
    p.step("vector", lambda E: E.memset(ones_col_bf[:, :], 1.0),
           w=(ones_col_bf, "K0"))
    p.step("vector", lambda E: E.memset(ones_row_bf[:, :], 1.0),
           w=(ones_row_bf, "K0"))
    ident = k.sb("identS", [128, 128], BF16)
    rmat = k.sb("rmatS", [128, 128], BF16)
    e4g = k.sb("e4gS", [4, 128], BF16)
    gmask = k.sb("gmaskS", [32, 32], BF16)
    p.dma(kt["ident"], ident[:, :], w=("K0",))
    p.dma(kt["rmat"], rmat[:, :], w=("K0",))
    p.dma(kt["e4g"], e4g[:, :], w=("K0",))
    p.dma(kt["gmask"], gmask[:, :], w=("K0",))
    DQNS = 2048
    dq_stk = ExitStack()
    dqb = {nm: dq_stk.enter_context(nc.sbuf_tensor("dq_" + nm, [128, DQNS], U8))
           for nm in ("d", "2", "3", "4", "5", "6")}
    dqb["s4"] = dq_stk.enter_context(nc.sbuf_tensor("dq_s4", [4, DQNS], F32))
    dqb["sb"] = dq_stk.enter_context(nc.sbuf_tensor("dq_sb", [4, DQNS], BF16))
    dqb["o"] = dq_stk.enter_context(nc.sbuf_tensor("dq_o", [128, DQNS], BF16))
    dq_psA = dq_stk.enter_context(nc.psum_tensor("dq_psA", [128, 512], F32))

    def ph_layernorm(src, dst, Din, src_dtype=F32, out_dtype=BF16,
                     mean_center=True, gamma_hbm=None, tag="ln"):
        dch = Din // 128
        NBUF = 2 if dch <= 16 else 1
        with ExitStack() as st:
            sbl = lambda n, sh, dt: st.enter_context(nc.sbuf_tensor(tag + n, sh, dt))
            xs = [sbl("_x%d" % i, [128, dch * 512], F32) for i in range(NBUF)]
            xi = ([sbl("_xi%d" % i, [128, dch * 512], src_dtype)
                   for i in range(NBUF)] if src_dtype != F32 else None)
            sq = [sbl("_sq%d" % i, [128, 512], F32) for i in range(2)]
            rows = sbl("_rw", [1, 1024], F32)
            outs = (xs if (out_dtype == F32 and dch > 16) else
                    [sbl("_o%d" % i, [128, dch * 512], out_dtype)
                     for i in range(NBUF)])
            gcol = sbl("_g", [128, dch], F32) if gamma_hbm is not None else None
            with nc.psum_tensor(tag + "_p1", [1, 512], F32) as ps_s, \
                 nc.psum_tensor(tag + "_p2", [1, 512], F32) as ps_q, \
                 nc.psum_tensor(tag + "_pbr", [128, 512], F32) as psb_r, \
                 nc.psum_tensor(tag + "_pbm", [128, 512], F32) as psb_m:
                if gamma_hbm is not None:
                    p.dma(gamma_hbm, gcol[:, :])
                for tt in range(TT):
                    pr = tt % NBUF
                    X = xs[pr]
                    O = outs[pr]
                    tsl = slice(tt * 512, (tt + 1) * 512)
                    if src_dtype != F32:
                        p.dma(fmv(src, tsl), v3(xi[pr][:, :], 512))
                        p.step("vector", lambda E, pr=pr: E.tensor_copy(
                            out=xs[pr][:, :], in_=xi[pr][:, :]),
                            r=(xi[pr],), w=(X,))
                    else:
                        p.dma(fmv(src, tsl), v3(X[:, :], 512))
                    for c in range(dch):
                        csl = slice(c * 512, (c + 1) * 512)
                        sqc = sq[c % 2]
                        if mean_center:
                            p.step("tensor", lambda E, c=c, csl=csl, X=X: E.matmul(
                                ps_s[:, :], ones_col[:, :], X[:, csl],
                                start=(c == 0), stop=(c == dch - 1)),
                                r=(ones_col, X),
                                w=(ps_s,) if c == 0 else (), accum=(ps_s,) if c else ())
                        p.step("scalar", lambda E, csl=csl, X=X, sqc=sqc: E.activation(
                            out=sqc[:, :], in_=X[:, csl], func=AF.Square),
                            r=(X,), w=(sqc,))
                        p.step("tensor", lambda E, c=c, sqc=sqc: E.matmul(
                            ps_q[:, :], ones_col[:, :], sqc[:, :],
                            start=(c == 0), stop=(c == dch - 1)),
                            r=(ones_col, sqc),
                            w=(ps_q,) if c == 0 else (), accum=(ps_q,) if c else ())
                    r_r = rows[:, 0:512]
                    r_mr = rows[:, 512:1024]
                    p.step("vector", lambda E, r_r=r_r: E.tensor_scalar_mul(
                        r_r, ps_q[0:1, :], 1.0 / Din), r=(ps_q,), w=(rows,))
                    if mean_center:
                        p.step("vector", lambda E, r_mr=r_mr: E.tensor_scalar_mul(
                            r_mr, ps_s[0:1, :], 1.0 / Din), r=(ps_s,), w=(rows,))
                        p.step("vector", lambda E, r_mr=r_mr: E.scalar_tensor_tensor(
                            out=rows[0:1, 512:1024] if False else r_mr,
                            in0=r_mr, scalar=-1.0, in1=r_mr,
                            op0=ALU.mult, op1=ALU.mult), r=(rows,), w=("lnrtmp",)) \
                            if False else None
                        p.step("vector", lambda E, r_mr=r_mr: E.scalar_tensor_tensor(
                            out=sq[0][0:1, 0:512], in0=r_mr, scalar=-1.0,
                            in1=r_mr, op0=ALU.mult, op1=ALU.mult),
                            r=(rows,), w=(sq[0],))
                        p.step("vector", lambda E, r_r=r_r: E.tensor_add(
                            r_r, r_r, sq[0][0:1, 0:512]), r=(rows, sq[0]),
                            w=(rows,))
                    p.step("scalar", lambda E, r_r=r_r: E.activation(
                        out=r_r, in_=r_r, func=AF.Sqrt, bias=float(EPS)),
                        r=(rows, cst), w=(rows,))
                    p.step("vector", lambda E, r_r=r_r: E.reciprocal(r_r, r_r),
                           r=(rows,), w=(rows,))
                    if mean_center:
                        p.step("vector", lambda E, r_mr=r_mr, r_r=r_r: E.tensor_mul(
                            r_mr, r_mr, r_r), r=(rows,), w=(rows,))
                    p.step("tensor", lambda E: E.matmul(
                        psb_r[:, :], ones_row[:, :], rows[0:1, 0:512],
                        start=True, stop=True), r=(ones_row, rows), w=(psb_r,))
                    if mean_center:
                        p.step("tensor", lambda E: E.matmul(
                            psb_m[:, :], ones_row[:, :], rows[0:1, 512:1024],
                            start=True, stop=True), r=(ones_row, rows),
                            w=(psb_m,))
                    for c in range(dch):
                        csl = slice(c * 512, (c + 1) * 512)
                        p.step("vector", lambda E, csl=csl, X=X: E.tensor_mul(
                            X[:, csl], X[:, csl], psb_r[:, :]),
                            r=(X, psb_r), w=(X,))
                        if mean_center:
                            p.step("vector", lambda E, csl=csl, X=X: E.tensor_sub(
                                X[:, csl], X[:, csl], psb_m[:, :]),
                                r=(X, psb_m), w=(X,))
                        if gamma_hbm is not None:
                            p.step("vector", lambda E, c=c, csl=csl, X=X, O=O:
                                   E.tensor_scalar(
                                out=O[:, csl], in0=X[:, csl],
                                scalar1=gcol[:, c:c + 1], scalar2=None,
                                op0=ALU.mult), r=(X, gcol), w=(O,))
                        else:
                            p.step("vector", lambda E, csl=csl, X=X, O=O:
                                   E.tensor_copy(out=O[:, csl], in_=X[:, csl]),
                                   r=(X,), w=(O,))
                    p.dma(v3(O[:, :], 512), fmv(dst, tsl), q="scalar")

    def ph_resadd(xsrc, zsrc, dst, tag):
        NB2 = 1
        with ExitStack() as st:
            sbl = lambda n, sh, dt: st.enter_context(nc.sbuf_tensor(tag + n, sh, dt))
            zs = [sbl("_z%d" % i, [128, DC * 512], F32) for i in range(NB2)]
            xsb = [sbl("_x%d" % i, [128, DC * 512], F32) for i in range(NB2)]
            sq = [sbl("_sq%d" % i, [128, 512], F32) for i in range(2)]
            rows = sbl("_rw", [1, 1024], F32)
            outs = [sbl("_o%d" % i, [128, DC * 512], F32) for i in range(NB2)]
            with nc.psum_tensor(tag + "_p1", [1, 512], F32) as ps_s, \
                 nc.psum_tensor(tag + "_p2", [1, 512], F32) as ps_q, \
                 nc.psum_tensor(tag + "_pbr", [128, 512], F32) as psb_r, \
                 nc.psum_tensor(tag + "_pbm", [128, 512], F32) as psb_m:
                for tt in range(TT):
                    pr = tt % NB2
                    Z, X, O = zs[pr], xsb[pr], outs[pr]
                    tsl = slice(tt * 512, (tt + 1) * 512)
                    p.dma(fmv(zsrc, tsl), v3(Z[:, :], 512))
                    p.dma(fmv(xsrc, tsl), v3(X[:, :], 512))
                    for c in range(DC):
                        csl = slice(c * 512, (c + 1) * 512)
                        sqc = sq[c % 2]
                        p.step("tensor", lambda E, c=c, csl=csl, Z=Z: E.matmul(
                            ps_s[:, :], ones_col[:, :], Z[:, csl],
                            start=(c == 0), stop=(c == DC - 1)),
                            r=(ones_col, Z),
                            w=(ps_s,) if c == 0 else (), accum=(ps_s,) if c else ())
                        p.step("scalar", lambda E, csl=csl, Z=Z, sqc=sqc:
                               E.activation(out=sqc[:, :], in_=Z[:, csl],
                                            func=AF.Square), r=(Z,), w=(sqc,))
                        p.step("tensor", lambda E, c=c, sqc=sqc: E.matmul(
                            ps_q[:, :], ones_col[:, :], sqc[:, :],
                            start=(c == 0), stop=(c == DC - 1)),
                            r=(ones_col, sqc),
                            w=(ps_q,) if c == 0 else (), accum=(ps_q,) if c else ())
                    r_r = rows[:, 0:512]
                    r_mr = rows[:, 512:1024]
                    p.step("vector", lambda E, r_r=r_r: E.tensor_scalar_mul(
                        r_r, ps_q[0:1, :], 1.0 / D), r=(ps_q,), w=(rows,))
                    p.step("vector", lambda E, r_mr=r_mr: E.tensor_scalar_mul(
                        r_mr, ps_s[0:1, :], 1.0 / D), r=(ps_s,), w=(rows,))
                    p.step("vector", lambda E, r_mr=r_mr: E.scalar_tensor_tensor(
                        out=sq[0][0:1, 0:512], in0=r_mr, scalar=-1.0, in1=r_mr,
                        op0=ALU.mult, op1=ALU.mult), r=(rows,), w=(sq[0],))
                    p.step("vector", lambda E, r_r=r_r: E.tensor_add(
                        r_r, r_r, sq[0][0:1, 0:512]), r=(rows, sq[0]), w=(rows,))
                    p.step("scalar", lambda E, r_r=r_r: E.activation(
                        out=r_r, in_=r_r, func=AF.Sqrt, bias=float(EPS)),
                        r=(rows, cst), w=(rows,))
                    p.step("vector", lambda E, r_r=r_r: E.reciprocal(r_r, r_r),
                           r=(rows,), w=(rows,))
                    p.step("vector", lambda E, r_mr=r_mr, r_r=r_r: E.tensor_mul(
                        r_mr, r_mr, r_r), r=(rows,), w=(rows,))
                    p.step("tensor", lambda E: E.matmul(
                        psb_r[:, :], ones_row[:, :], rows[0:1, 0:512],
                        start=True, stop=True), r=(ones_row, rows), w=(psb_r,))
                    p.step("tensor", lambda E: E.matmul(
                        psb_m[:, :], ones_row[:, :], rows[0:1, 512:1024],
                        start=True, stop=True), r=(ones_row, rows), w=(psb_m,))
                    for c in range(DC):
                        csl = slice(c * 512, (c + 1) * 512)
                        p.step("vector", lambda E, csl=csl, Z=Z: E.tensor_mul(
                            Z[:, csl], Z[:, csl], psb_r[:, :]),
                            r=(Z, psb_r), w=(Z,))
                        p.step("vector", lambda E, csl=csl, Z=Z: E.tensor_sub(
                            Z[:, csl], Z[:, csl], psb_m[:, :]),
                            r=(Z, psb_m), w=(Z,))
                        p.step("vector", lambda E, csl=csl, Z=Z, X=X, O=O:
                               E.tensor_add(O[:, csl], Z[:, csl], X[:, csl]),
                               r=(Z, X), w=(O,))
                    p.dma(v3(O[:, :], 512), fmv(dst, tsl), q="scalar")

    def ph_linear(w_hbm, x_hbm, out_hbm, Kdim, Ndim, tag, w_dtype=F32,
                  out_dtype=BF16, row_scale=None, out_keys=None, post=None):
        kslab = min(Kdim, 2048)
        nslab = (Kdim + kslab - 1) // kslab
        if nslab > 1:
            kslab = 1024
            nslab = (Kdim + kslab - 1) // kslab
        NB = 256
        acc_needed = nslab > 1
        with ExitStack() as st:
            sbl = lambda n, sh, dt: st.enter_context(nc.sbuf_tensor(tag + n, sh, dt))
            xs = sbl("_x", [128, (kslab // 128) * T], BF16)
            wf = [sbl("_wf%d" % i, [128, (kslab // 128) * NB], w_dtype)
                  for i in range(2)]
            wb = ([sbl("_wb%d" % i, [128, (kslab // 128) * NB], BF16)
                   for i in range(2)] if w_dtype != BF16 else wf)
            outs = [sbl("_o%d" % i, [128, (1 if acc_needed else 2) * T],
                        out_dtype) for i in range(1 if acc_needed else 2)]
            acc = sbl("_a", [128, (Ndim // 128) * T], F32) if acc_needed else None
            scl = sbl("_s", [128, TT * 512], BF16) if row_scale is not None else None
            srow = sbl("_sr", [1, T], BF16) if row_scale is not None else None
            with ExitStack() as pst:
                ps4 = [pst.enter_context(nc.psum_tensor("%s_ps%d" % (tag, i),
                                                        [128, 512], F32))
                       for i in range(4)]
                ps = [ps4[i % 4] for i in range(8)]
                if row_scale is not None:
                    p.dma(row_scale, srow[:, :])
                    for tt in range(TT):
                        p.step("tensor", lambda E, tt=tt: E.matmul(
                            ps[0][:, :], ones_row_bf[:, :],
                            srow[0:1, tt * 512:(tt + 1) * 512],
                            start=True, stop=True), r=(ones_row_bf, srow),
                            w=(ps[0],))
                        p.step("vector", lambda E, tt=tt: E.tensor_copy(
                            out=scl[:, tt * 512:(tt + 1) * 512],
                            in_=ps[0][:, :]), r=(ps[0],), w=(scl,))
                for sl in range(nslab):
                    k0 = sl * kslab
                    kc = min(kslab, Kdim - k0) // 128
                    p.dma(fmv(x_hbm[k0:k0 + kc * 128, :]),
                          v3(xs[:, :kc * T], T))
                    for nb in range(Ndim // NB):
                        n0 = nb * NB
                        pr = nb % 2
                        WF, WB = wf[pr], wb[pr]
                        O = outs[pr % len(outs)]
                        p.dma(w_hbm[k0:k0 + kc * 128, n0:n0 + NB].rearrange(
                            "(c p) n -> p c n", p=128),
                            v3(WF[:, :kc * NB], NB))
                        if w_dtype != BF16:
                            p.step("vector", lambda E, kc=kc, WF=WF, WB=WB:
                                   E.tensor_copy(out=WB[:, :kc * NB],
                                                 in_=WF[:, :kc * NB]),
                                   r=(WF,), w=(WB,))
                        for t4 in range(TT):
                            for n2 in range(2):
                                bank = t4 * 2 + n2
                                for c in range(kc):
                                    p.step("tensor", lambda E, c=c, n2=n2,
                                           t4=t4, bank=bank, kc=kc, WB=WB:
                                           E.matmul(
                                        ps[bank][:, :],
                                        WB[:, c * NB + n2 * 128:
                                           c * NB + n2 * 128 + 128],
                                        xs[:, c * T + t4 * 512:
                                           c * T + (t4 + 1) * 512],
                                        start=(c == 0), stop=(c == kc - 1)),
                                        r=(WB, xs),
                                        w=(ps[bank],) if c == 0 else (),
                                        accum=(ps[bank],) if c else ())
                                if acc_needed:
                                    nch = n0 // 128 + n2
                                    asl = slice(nch * T + t4 * 512,
                                                nch * T + (t4 + 1) * 512)
                                    if sl == 0:
                                        p.step("vector", lambda E, bank=bank,
                                               asl=asl: E.tensor_copy(
                                            out=acc[:, asl],
                                            in_=ps[bank][:, :]),
                                            r=(ps[bank],), w=(acc,))
                                    else:
                                        p.step("vector", lambda E, bank=bank,
                                               asl=asl: E.tensor_add(
                                            acc[:, asl], acc[:, asl],
                                            ps[bank][:, :]),
                                            r=(ps[bank], acc), w=(acc,))
                                else:
                                    osl = slice(n2 * T + t4 * 512,
                                                n2 * T + (t4 + 1) * 512)
                                    if row_scale is not None:
                                        ssl = slice(t4 * 512, (t4 + 1) * 512)
                                        p.step("vector", lambda E, bank=bank,
                                               osl=osl, ssl=ssl, O=O:
                                               E.tensor_mul(
                                            O[:, osl], ps[bank][:, :],
                                            scl[:, ssl]),
                                            r=(ps[bank], scl), w=(O,))
                                    else:
                                        p.step("vector", lambda E, bank=bank,
                                               osl=osl, O=O: E.tensor_copy(
                                            out=O[:, osl], in_=ps[bank][:, :]),
                                            r=(ps[bank],), w=(O,))
                        if not acc_needed:
                            if out_keys is not None:
                                p.dma(v3(O[:, :], T),
                                      fmv(out_hbm[n0:n0 + NB, :]),
                                      q="scalar", r=(O,),
                                      w=("%s#%d" % (out_keys, nb),),
                                      nokey=True)
                            else:
                                p.dma(v3(O[:, :], T),
                                      fmv(out_hbm[n0:n0 + NB, :]),
                                      q="scalar")
                if acc_needed:
                    for nch in range(Ndim // 128):
                        O = outs[0]
                        if row_scale is not None:
                            for t4 in range(TT):
                                p.step("vector", lambda E, nch=nch, t4=t4, O=O:
                                       E.tensor_mul(
                                    O[:, t4 * 512:(t4 + 1) * 512],
                                    acc[:, nch * T + t4 * 512:
                                        nch * T + (t4 + 1) * 512],
                                    scl[:, t4 * 512:(t4 + 1) * 512]),
                                    r=(acc, scl), w=(O,))
                        else:
                            p.step("vector", lambda E, nch=nch, O=O:
                                   E.tensor_copy(
                                out=O[:, 0:T], in_=acc[:, nch * T:(nch + 1) * T]),
                                r=(acc,), w=(O,))
                        p.dma(v3(O[:, 0:T], T),
                              fmv(out_hbm[nch * 128:(nch + 1) * 128, :]),
                              q="scalar")

                if post is not None:
                    post()

    def ph_linear_multi(pairs, x_hbm, Kdim, Ndim, tag, w_dtype=BF16,
                        out_dtype=F32, post=None):
        kslab = Kdim
        NB = 256
        with ExitStack() as st:
            sbl = lambda n, sh, dt: st.enter_context(nc.sbuf_tensor(tag + n, sh, dt))
            xs = sbl("_x", [128, (kslab // 128) * T], BF16)
            wf = [sbl("_wf%d" % i, [128, (kslab // 128) * NB], w_dtype)
                  for i in range(2)]
            outs = [sbl("_o%d" % i, [128, 2 * T], out_dtype) for i in range(2)]
            with ExitStack() as pstk:
                ps4 = [pstk.enter_context(nc.psum_tensor("%s_ps%d" % (tag, i),
                                                         [128, 512], F32))
                       for i in range(4)]
                ps = [ps4[i % 4] for i in range(8)]
                kc = Kdim // 128
                p.dma(fmv(x_hbm), v3(xs[:, :kc * T], T))
                for w_hbm, out_hbm, pfx in pairs:
                    for nb in range(Ndim // NB):
                        n0 = nb * NB
                        pr = nb % 2
                        WB, O = wf[pr], outs[pr]
                        p.dma(w_hbm[0:kc * 128, n0:n0 + NB].rearrange(
                            "(c p) n -> p c n", p=128),
                            v3(WB[:, :kc * NB], NB))
                        for t4 in range(TT):
                            for n2 in range(2):
                                bank = t4 * 2 + n2
                                for c in range(kc):
                                    p.step("tensor", lambda E, c=c, n2=n2,
                                           t4=t4, bank=bank, WB=WB: E.matmul(
                                        ps[bank][:, :],
                                        WB[:, c * NB + n2 * 128:
                                           c * NB + n2 * 128 + 128],
                                        xs[:, c * T + t4 * 512:
                                           c * T + (t4 + 1) * 512],
                                        start=(c == 0), stop=(c == kc - 1)),
                                        r=(WB, xs),
                                        w=(ps[bank],) if c == 0 else (),
                                        accum=(ps[bank],) if c else ())
                                osl = slice(n2 * T + t4 * 512,
                                            n2 * T + (t4 + 1) * 512)
                                p.step("vector", lambda E, bank=bank, osl=osl,
                                       O=O: E.tensor_copy(
                                    out=O[:, osl], in_=ps[bank][:, :]),
                                    r=(ps[bank],), w=(O,))
                        p.dma(v3(O[:, :], T), fmv(out_hbm[n0:n0 + NB, :]),
                              q="scalar", r=(O,), w=("%s#%d" % (pfx, nb),),
                              nokey=True)
                if post is not None:
                    post()

    def em_gelu_mult(tag):
        with ExitStack() as st:
            sbl = lambda n, sh, dt: st.enter_context(nc.sbuf_tensor(tag + n, sh, dt))
            ab = [sbl("_a%d" % i, [128, T], F32) for i in range(2)]
            bb2 = [sbl("_b%d" % i, [128, T], F32) for i in range(2)]
            obb = [sbl("_o%d" % i, [128, T], BF16) for i in range(2)]
            for c in range(128):
                pr = c % 2
                a, b, ob = ab[pr], bb2[pr], obb[pr]
                rs = slice(c * 128, (c + 1) * 128)
                blk = c // 2
                p.dma(kt["s_y1"][rs, :], a[:, :], r=("y1#%d" % blk,),
                      w=(a,), nokey=True)
                p.dma(kt["s_y2"][rs, :], b[:, :], r=("y2#%d" % blk,),
                      w=(b,), nokey=True)
                p.step("scalar", lambda E, a=a: E.activation(
                    out=a[:, :], in_=a[:, :], func=AF.Gelu_apprx_tanh),
                    r=(a,), w=(a,))
                p.step("vector", lambda E, ob=ob, a=a, b=b: E.tensor_mul(
                    ob[:, :], a[:, :], b[:, :]), r=(a, b), w=(ob,))
                p.dma(ob[:, :], kt["s_int"][rs, :], q="scalar")

    def em_silu_mult(tag):
        with ExitStack() as st:
            sbl = lambda n, sh, dt: st.enter_context(nc.sbuf_tensor(tag + n, sh, dt))
            ab = [sbl("_a%d" % i, [128, T], F32) for i in range(2)]
            bb2 = [sbl("_b%d" % i, [128, T], F32) for i in range(2)]
            obb = [sbl("_o%d" % i, [128, T], F32) for i in range(2)]
            for c in range(44):
                pr = c % 2
                a, b, ob = ab[pr], bb2[pr], obb[pr]
                p.dma(kt["s_y1"][c * 128:(c + 1) * 128, :], a[:, :],
                      r=("yb#%d" % (c // 2),), w=(a,), nokey=True)
                p.dma(kt["s_y1"][(44 + c) * 128:(45 + c) * 128, :], b[:, :],
                      r=("yb#%d" % ((44 + c) // 2),), w=(b,), nokey=True)
                p.step("scalar", lambda E, a=a: E.activation(
                    out=a[:, :], in_=a[:, :], func=AF.Silu), r=(a,), w=(a,))
                p.step("vector", lambda E, ob=ob, a=a, b=b: E.tensor_mul(
                    ob[:, :], a[:, :], b[:, :]), r=(a, b), w=(ob,))
                p.dma(ob[:, :], kt["s_h"][c * 128:(c + 1) * 128, :],
                      q="scalar")

    def ph_linear_tok(w_hbm, x_hbm, out_hbm, Ndim, tag):
        with ExitStack() as st:
            sbl = lambda n, sh, dt: st.enter_context(nc.sbuf_tensor(tag + n, sh, dt))
            xs = sbl("_x", [128, DC * T], BF16)
            NQ = 8 if Ndim >= 2048 else 4
            wf = sbl("_wf", [128, DC * (Ndim // NQ)], F32)
            wbt = sbl("_wb", [128, DC * Ndim], BF16)
            outs = [sbl("_o%d" % i, [128, Ndim], BF16)
                    for i in range(1 if Ndim >= 2048 else 2)]
            with ExitStack() as pstk:
                nbk = Ndim // 512
                ps = [pstk.enter_context(nc.psum_tensor("%s_ps%d" % (tag, i),
                                                        [128, 512], F32))
                      for i in range(nbk)]
                p.dma(fmv(x_hbm), v3(xs[:, :], T))
                for qq in range(NQ):
                    q0 = qq * (Ndim // NQ)
                    p.dma(w_hbm[:, q0:q0 + Ndim // NQ].rearrange(
                        "(c p) n -> p c n", p=128), v3(wf[:, :], Ndim // NQ))
                    for c in range(DC):
                        p.step("vector", lambda E, c=c, q0=q0: E.tensor_copy(
                            out=wbt[:, c * Ndim + q0:c * Ndim + q0 + Ndim // NQ],
                            in_=wf[:, c * (Ndim // NQ):(c + 1) * (Ndim // NQ)]),
                            r=(wf,), w=("%s_wbt%d" % (tag, qq % 4),))
                for tc in range(16):
                    O = outs[tc % len(outs)]
                    for nb in range(nbk):
                        for c in range(DC):
                            p.step("tensor", lambda E, tc=tc, c=c, nb=nb:
                                   E.matmul(
                                ps[nb][:, :],
                                xs[:, c * T + tc * 128:c * T + (tc + 1) * 128],
                                wbt[:, c * Ndim + nb * 512:
                                    c * Ndim + (nb + 1) * 512],
                                start=(c == 0), stop=(c == DC - 1)),
                                r=(xs, "%s_wbt0" % tag, "%s_wbt1" % tag,
                                   "%s_wbt2" % tag, "%s_wbt3" % tag),
                                w=(ps[nb],) if c == 0 else (),
                                accum=(ps[nb],) if c else ())
                        p.step("vector", lambda E, nb=nb, O=O: E.tensor_copy(
                            out=O[:, nb * 512:(nb + 1) * 512],
                            in_=ps[nb][:, :]), r=(ps[nb],), w=(O,))
                    p.dma(O[:, :Ndim], out_hbm[tc * 128:(tc + 1) * 128, :],
                          q="scalar")

    def ph_dequant(pk, sc, dst, Kdim, Ndim, tag):
        kch = Kdim // 128
        NS = min(Ndim, DQNS)
        dup, w2, w3, w4, w5, w6 = (dqb["d"], dqb["2"], dqb["3"], dqb["4"],
                                   dqb["5"], dqb["6"])
        sc4, sc4b, wout = dqb["s4"], dqb["sb"], dqb["o"]
        psX = dq_psA
        c_i = dup[:].bitcast(U16)
        s_ = w2[:].bitcast(U16)
        c7 = w3[:].bitcast(U16)
        z1 = w4[:].bitcast(U16)
        z2 = w5[:].bitcast(U16)
        w6i = w6[:].bitcast(U16)
        d8 = w3[:].bitcast(FP8)
        eng = "vector"
        st_ = lambda f, rr, ww: p.step(eng, f, r=rr, w=ww)
        for kc in range(kch):
            j0 = kc * 64
            for n0 in range(0, Ndim, NS):
                nsl = slice(n0, n0 + NS)
                p.dma(pk[j0:j0 + 64, nsl], dup[0:64, :NS])
                p.dma(pk[j0:j0 + 64, nsl], dup[64:128, :NS])
                p.dma(sc[kc * 4:kc * 4 + 4, nsl], sc4[:, :NS])
                p.step("vector", lambda E: E.tensor_copy(
                    out=sc4b[:, :NS], in_=sc4[:, :NS]), r=(sc4,), w=(sc4b,))
                st_(lambda E: E.tensor_scalar(
                    out=c_i[0:64, :], in0=c_i[0:64, :], scalar1=M0F,
                    scalar2=None, op0=ALU.bitwise_and), (dup,), (dup,))
                st_(lambda E: E.tensor_scalar(
                    out=c_i[64:128, :], in0=c_i[64:128, :], scalar1=4,
                    scalar2=M0F, op0=ALU.logical_shift_right,
                    op1=ALU.bitwise_and), (dup,), (dup,))
                st_(lambda E: E.tensor_scalar(
                    out=s_, in0=c_i, scalar1=M08, scalar2=4,
                    op0=ALU.bitwise_and, op1=ALU.logical_shift_left),
                    (dup,), (w2,))
                st_(lambda E: E.tensor_scalar(
                    out=c7, in0=c_i, scalar1=M07, scalar2=None,
                    op0=ALU.bitwise_and), (dup,), (w3,))
                st_(lambda E: E.tensor_scalar(
                    out=z1, in0=c7, scalar1=1, scalar2=None,
                    op0=ALU.logical_shift_right), (w3,), (w4,))
                st_(lambda E: E.tensor_scalar(
                    out=z2, in0=z1, scalar1=1, scalar2=None,
                    op0=ALU.logical_shift_right), (w4,), (w5,))
                st_(lambda E: E.tensor_tensor(
                    out=z1, in0=z1, in1=z2, op=ALU.bitwise_or),
                    (w4, w5), (w4,))
                st_(lambda E: E.tensor_scalar(
                    out=z2, in0=c7, scalar1=M01, scalar2=None,
                    op0=ALU.bitwise_and), (w3,), (w5,))
                st_(lambda E: E.tensor_scalar(
                    out=w6i, in0=z1, scalar1=M01, scalar2=None,
                    op0=ALU.bitwise_xor), (w4,), (w6,))
                st_(lambda E: E.tensor_tensor(
                    out=w6i, in0=z2, in1=w6i, op=ALU.bitwise_and),
                    (w5, w6), (w6,))
                st_(lambda E: E.tensor_tensor(
                    out=z2, in0=z1, in1=c7, op=ALU.bitwise_or),
                    (w4, w3), (w5,))
                st_(lambda E: E.tensor_scalar(
                    out=z2, in0=z2, scalar1=M01, scalar2=None,
                    op0=ALU.bitwise_and), (w5,), (w5,))
                st_(lambda E: E.tensor_tensor(
                    out=c7, in0=c7, in1=w6i, op=ALU.bitwise_xor),
                    (w3, w6), (w3,))
                st_(lambda E: E.tensor_scalar(
                    out=c7, in0=c7, scalar1=4, scalar2=None,
                    op0=ALU.mult), (w3,), (w3,))
                st_(lambda E: E.tensor_scalar(
                    out=z2, in0=z2, scalar1=48, scalar2=None,
                    op0=ALU.mult), (w5,), (w5,))
                st_(lambda E: E.tensor_tensor(
                    out=c7, in0=c7, in1=z2, op=ALU.add), (w3, w5), (w3,))
                st_(lambda E: E.tensor_tensor(
                    out=c7, in0=c7, in1=s_, op=ALU.bitwise_or),
                    (w3, w2), (w3,))
                for s5 in range(NS // 512):
                    ssl = slice(s5 * 512, (s5 + 1) * 512)
                    p.step("tensor", lambda E, ssl=ssl: E.matmul(
                        psX[:, :], e4g[:, :], sc4b[:, ssl],
                        start=True, stop=True), r=(e4g, sc4b), w=(psX,))
                    p.step("vector", lambda E, ssl=ssl: E.tensor_tensor(
                        out=wout[:, ssl], in0=d8[:, ssl], in1=psX[:, :],
                        op=ALU.mult), r=(w3, psX), w=(wout,))
                p.dma(wout[0:64, :NS],
                      dst[kc * 128:kc * 128 + 128:2, nsl], q="scalar")
                p.dma(wout[64:128, :NS],
                      dst[kc * 128 + 1:kc * 128 + 128:2, nsl], q="scalar")
                yield

    def em_round2(eng, v, t1, t2, w, vk, t1k, t2k):
        p.step(eng, lambda E: E.tensor_scalar(
            out=t1[:, :w], in0=v[:, :w], scalar1=128.5, scalar2=None,
            op0=ALU.add), r=(vk,), w=(t1k,))
        for kk in range(7, -1, -1):
            th = float(1 << kk)
            p.step(eng, lambda E, th=th: E.tensor_scalar(
                out=t2[:, :w], in0=t1[:, :w], scalar1=th, scalar2=th,
                op0=ALU.is_ge, op1=ALU.mult), r=(t1k,), w=(t2k,))
            p.step(eng, lambda E: E.tensor_sub(t1[:, :w], t1[:, :w],
                                               t2[:, :w]), r=(t1k, t2k),
                   w=(t1k,))
        p.step(eng, lambda E: E.tensor_sub(t2[:, :w], v[:, :w], t1[:, :w]),
               r=(vk, t1k), w=(t2k,))
        p.step(eng, lambda E: E.tensor_scalar(
            out=t2[:, :w], in0=t2[:, :w], scalar1=0.5, scalar2=None,
            op0=ALU.add), r=(t2k,), w=(t2k,))

    def ph_ternarize(w_hbm, dst, Kdim, Ndim, scal_idx, tag):
        kch = Kdim // 128
        NSW = 1024
        nel = float(Kdim * Ndim)
        with ExitStack() as st:
            sbl = lambda n, sh, dt: st.enter_context(nc.sbuf_tensor(tag + n, sh, dt))
            wfb = [sbl("_w%d" % i, [128, NSW], F32) for i in range(2)]
            t1b = [sbl("_t1%d" % i, [128, NSW], F32) for i in range(2)]
            t2b = [sbl("_t2%d" % i, [128, NSW], F32) for i in range(2)]
            colb = [sbl("_c%d" % i, [128, 1], F32) for i in range(2)]
            colacc = sbl("_ca", [128, 1], F32)
            srow = sbl("_sr", [1, 2], F32)
            scol = sbl("_sc", [128, 1], F32)
            obb = [sbl("_ob%d" % i, [128, NSW], BF16) for i in range(2)]
            with nc.psum_tensor(tag + "_p", [1, 512], F32) as ps1, \
                 nc.psum_tensor(tag + "_pb", [128, 512], F32) as psb:
                p.step("vector", lambda E: E.memset(colacc[:, :], 0.0),
                       w=(colacc,))
                it = 0
                for kc in range(kch):
                    for n0 in range(0, Ndim, NSW):
                        w = min(NSW, Ndim - n0)
                        pr = it % 2
                        wf, col = wfb[pr], colb[pr]
                        p.dma(w_hbm[kc * 128:(kc + 1) * 128, n0:n0 + w],
                              wf[:, :w])
                        p.step("vector", lambda E, w=w, wf=wf, col=col:
                               E.reduce_sum(col[:, :], wf[:, :w], axis=AX.X,
                                            apply_absolute_value=True),
                               r=(wf,), w=(col,))
                        p.step("vector", lambda E, col=col: E.tensor_add(
                            colacc[:, :], colacc[:, :], col[:, :]),
                            r=(col, colacc), w=(colacc,))
                        it += 1
                        yield
                p.step("tensor", lambda E: E.matmul(
                    ps1[0:1, 0:1], ones_col[:, :], colacc[:, :],
                    start=True, stop=True), r=(ones_col, colacc), w=(ps1,))
                p.step("vector", lambda E: E.tensor_scalar_mul(
                    srow[0:1, 0:1], ps1[0:1, 0:1], 1.0 / nel),
                    r=(ps1,), w=(srow,))
                p.step("vector", lambda E: E.tensor_scalar_max(
                    srow[0:1, 0:1], srow[0:1, 0:1], 1e-5), r=(srow,), w=(srow,))
                p.dma(srow[0:1, 0:1], kt["s_wsc"][0:1, scal_idx:scal_idx + 1],
                      q="scalar")
                p.step("vector", lambda E: E.reciprocal(
                    srow[0:1, 1:2], srow[0:1, 0:1]), r=(srow,), w=(srow,))
                p.step("tensor", lambda E: E.matmul(
                    psb[:, 0:1], ones_row[:, :], srow[0:1, 1:2],
                    start=True, stop=True), r=(ones_row, srow), w=(psb,))
                p.step("vector", lambda E: E.tensor_copy(
                    out=scol[:, :], in_=psb[:, 0:1]), r=(psb,), w=(scol,))
                it = 0
                for kc in range(kch):
                    for n0 in range(0, Ndim, NSW):
                        w = min(NSW, Ndim - n0)
                        pr = it % 2
                        eng = "vector"
                        wf, t1, t2, ob = wfb[pr], t1b[pr], t2b[pr], obb[pr]
                        p.dma(w_hbm[kc * 128:(kc + 1) * 128, n0:n0 + w],
                              wf[:, :w])
                        p.step(eng, lambda E, w=w, wf=wf: E.tensor_scalar(
                            out=wf[:, :w], in0=wf[:, :w], scalar1=scol[:, 0:1],
                            scalar2=None, op0=ALU.mult), r=(wf, scol), w=(wf,))
                        p.step(eng, lambda E, w=w, wf=wf, t1=t1:
                               E.tensor_scalar(
                            out=t1[:, :w], in0=wf[:, :w], scalar1=0.5,
                            scalar2=None, op0=ALU.is_ge), r=(wf,), w=(t1,))
                        p.step(eng, lambda E, w=w, wf=wf, t2=t2:
                               E.tensor_scalar(
                            out=t2[:, :w], in0=wf[:, :w], scalar1=-0.5,
                            scalar2=None, op0=ALU.is_le), r=(wf,), w=(t2,))
                        p.step(eng, lambda E, w=w, t1=t1, t2=t2, ob=ob:
                               E.tensor_sub(ob[:, :w], t1[:, :w], t2[:, :w]),
                            r=(t1, t2), w=(ob,))
                        p.dma(ob[:, :w], dst[kc * 128:(kc + 1) * 128,
                                             n0:n0 + w], q="scalar")
                        it += 1
                        yield

    def ph_rope(buf, Din, tag):
        dch = Din // 128
        with ExitStack() as st:
            sbl = lambda n, sh, dt: st.enter_context(nc.sbuf_tensor(tag + n, sh, dt))
            qsb = [sbl("_q%d" % i, [128, 512], BF16) for i in range(2)]
            cosb = sbl("_c", [128, T], F32)
            sinb = sbl("_s", [128, T], F32)
            t1b = [sbl("_1%d" % i, [128, 512], F32) for i in range(2)]
            t2b = [sbl("_2%d" % i, [128, 512], F32) for i in range(2)]
            obb = [sbl("_o%d" % i, [128, 512], BF16) for i in range(2)]
            with nc.psum_tensor(tag + "_pA", [128, 512], F32) as psA, \
                 nc.psum_tensor(tag + "_pB", [128, 512], F32) as psB:
                psr = [psA, psB]
                p.dma(kt["cosT"], cosb[:, :])
                p.dma(kt["sinT"], sinb[:, :])
                it = 0
                for c in range(dch):
                    for tt in range(TT):
                        pr = it % 2
                        qs, t1, t2, ob, pp = (qsb[pr], t1b[pr], t2b[pr],
                                              obb[pr], psr[pr])
                        tsl = slice(tt * 512, (tt + 1) * 512)
                        p.dma(buf[c * 128:(c + 1) * 128, tsl], qs[:, :])
                        p.step("tensor", lambda E, qs=qs, pp=pp: E.matmul(
                            pp[:, :], rmat[:, :], qs[:, :],
                            start=True, stop=True), r=(rmat, qs), w=(pp,))
                        p.step("vector", lambda E, tsl=tsl, qs=qs, t1=t1:
                               E.tensor_mul(t1[:, :], qs[:, :], cosb[:, tsl]),
                               r=(qs, cosb), w=(t1,))
                        p.step("vector", lambda E, tsl=tsl, t2=t2, pp=pp:
                               E.tensor_mul(t2[:, :], pp[:, :], sinb[:, tsl]),
                               r=(pp, sinb), w=(t2,))
                        p.step("vector", lambda E, ob=ob, t1=t1, t2=t2:
                               E.tensor_add(ob[:, :], t1[:, :], t2[:, :]),
                               r=(t1, t2), w=(ob,))
                        p.dma(ob[:, :], buf[c * 128:(c + 1) * 128, tsl],
                              q="scalar")
                        it += 1

    def ph_attn(tag):
        with ExitStack() as st:
            sbl = lambda n, sh, dt: st.enter_context(nc.sbuf_tensor(tag + n, sh, dt))
            qh = [sbl("_q%d" % i, [128, 512], BF16) for i in range(2)]
            kh = [sbl("_k%d" % i, [128, T], BF16) for i in range(2)]
            vt = [sbl("_vt%d" % i, [128, 128 * 16], BF16) for i in range(2)]
            at = [sbl("_a%d" % i, [128, 512], BF16) for i in range(4)]
            msk = sbl("_m", [128, 2048], BF16)
            ob = [sbl("_o%d" % i, [128, 512], BF16) for i in range(2)]
            rd = sbl("_r", [1, 512], F32)
            rdb = sbl("_rdb", [128, 512], F32)
            with nc.psum_tensor(tag + "_s0", [128, 512], F32) as ps_s0, \
                 nc.psum_tensor(tag + "_s1", [128, 512], F32) as ps_s1, \
                 nc.psum_tensor(tag + "_s2", [128, 512], F32) as ps_s2, \
                 nc.psum_tensor(tag + "_s3", [128, 512], F32) as ps_s3, \
                 nc.psum_tensor(tag + "_po", [128, 512], F32) as ps_o, \
                 nc.psum_tensor(tag + "_pd", [1, 512], F32) as ps_d, \
                 nc.psum_tensor(tag + "_pb", [128, 512], F32) as ps_b:
                ps_s = [ps_s0, ps_s1, ps_s2, ps_s3]
                for r in range(4):
                    p.dma(kt["amask"][r * 128:(r + 1) * 128, :],
                          msk[:, r * 512:(r + 1) * 512])
                for h in range(16):
                    kv = h // 4
                    KH, VT = kh[kv % 2], vt[kv % 2]
                    if h % 4 == 0:
                        p.dma(kt["s_k"][kv * 128:(kv + 1) * 128, :], KH[:, :])
                        p.dma(kt["s_v"][:, kv * 128:(kv + 1) * 128].rearrange(
                            "(b p) n -> p b n", p=128), v3(VT[:, :], 128))
                    for qt in range(TT):
                        QH = qh[qt % 2]
                        OB = ob[qt % 2]
                        qsl = slice(qt * 512, (qt + 1) * 512)
                        p.dma(kt["s_q"][h * 128:(h + 1) * 128, qsl], QH[:, :])
                        nkc = 4 * qt + 4

                        def em_score(kc, KH=KH, QH=QH, qt=qt):
                            SS = ps_s[kc % 4]
                            AT = at[kc % 4]
                            p.step("tensor", lambda E, kc=kc, KH=KH, QH=QH,
                                   SS=SS: E.matmul(
                                SS[:, :], KH[:, kc * 128:(kc + 1) * 128],
                                QH[:, :], start=True, stop=True),
                                r=(KH, QH), w=(SS,))
                            p.step("scalar", lambda E, SS=SS, AT=AT:
                                   E.activation(
                                out=AT[:, :], in_=SS[:, :], func=AF.Exp,
                                bias=-MSH, scale=ISQ), r=(SS, cst), w=(AT,))
                            r = kc - 4 * qt
                            if r >= 0:
                                p.step("vector", lambda E, r=r, AT=AT:
                                       E.tensor_mul(
                                    AT[:, :], AT[:, :],
                                    msk[:, r * 512:(r + 1) * 512]),
                                    r=(AT, msk), w=(AT,))

                        def em_avden(kc, VT=VT, nkc=nkc):
                            AT = at[kc % 4]
                            p.step("tensor", lambda E, kc=kc, nkc=nkc, VT=VT,
                                   AT=AT: E.matmul(
                                ps_o[:, :], VT[:, kc * 128:(kc + 1) * 128],
                                AT[:, :], start=(kc == 0), stop=(kc == nkc - 1)),
                                r=(VT, AT),
                                w=(ps_o,) if kc == 0 else (),
                                accum=(ps_o,) if kc else ())
                            p.step("tensor", lambda E, kc=kc, nkc=nkc, AT=AT:
                                   E.matmul(
                                ps_d[:, :], ones_col_bf[:, :], AT[:, :],
                                start=(kc == 0), stop=(kc == nkc - 1)),
                                r=(ones_col_bf, AT),
                                w=(ps_d,) if kc == 0 else (),
                                accum=(ps_d,) if kc else ())

                        STAG = 2
                        for kc in range(nkc):
                            em_score(kc)
                            if kc >= STAG:
                                em_avden(kc - STAG)
                        for kc in range(max(0, nkc - STAG), nkc):
                            em_avden(kc)
                        p.step("vector", lambda E: E.reciprocal(
                            rd[:, :], ps_d[0:1, :]), r=(ps_d,), w=(rd,))
                        p.step("tensor", lambda E: E.matmul(
                            ps_b[:, :], ones_row[:, :], rd[:, :],
                            start=True, stop=True), r=(ones_row, rd), w=(ps_b,))
                        p.step("vector", lambda E: E.tensor_copy(
                            out=rdb[:, :], in_=ps_b[:, :]), r=(ps_b,), w=(rdb,))
                        p.step("vector", lambda E, OB=OB: E.tensor_mul(
                            OB[:, :], ps_o[:, :], rdb[:, :]),
                            r=(ps_o, rdb), w=(OB,))
                        p.dma(OB[:, :], kt["s_o"][h * 128:(h + 1) * 128, qsl],
                              q="scalar")

    def ph_gla_prep(tag):
        with ExitStack() as st:
            sbl = lambda n, sh, dt: st.enter_context(nc.sbuf_tensor(tag + n, sh, dt))
            ab = [sbl("_a%d" % i, [128, T], F32) for i in range(2)]
            bbb = [sbl("_b%d" % i, [128, T], F32) for i in range(2)]
            cb = [sbl("_c%d" % i, [128, T], F32) for i in range(2)]
            db = [sbl("_d%d" % i, [128, T], F32) for i in range(2)]
            zz = sbl("_z", [128, T], F32)
            obb = [sbl("_o%d" % i, [128, T], BF16) for i in range(2)]
            ob2b = [sbl("_p%d" % i, [128, T], BF16) for i in range(2)]
            with nc.psum_tensor(tag + "_ptA", [128, 128], BF16) as pstA, \
                 nc.psum_tensor(tag + "_ptB", [128, 128], BF16) as pstB:
                pst = [pstA, pstB]
                p.step("vector", lambda E: E.memset(zz[:, :], 0.0), w=(zz,))
                for c in range(DC):
                    pr = c % 2
                    a, b, c_, d_, obx, ob2 = (ab[pr], bbb[pr], cb[pr], db[pr],
                                              obb[pr], ob2b[pr])
                    PT = pst[pr]
                    rs = slice(c * 128, (c + 1) * 128)
                    p.dma(kt["s_t1"][rs, :], a[:, :])
                    p.step("scalar", lambda E, a=a, obx=obx: E.activation(
                        out=obx[:, :], in_=a[:, :], func=AF.Relu, scale=ISQ),
                        r=(a, cst), w=(obx,))
                    p.dma(obx[:, :], kt["s_q"][rs, :], q="scalar")
                    p.dma(kt["s_kr"][rs, :], a[:, :])
                    p.step("scalar", lambda E, a=a, b=b: E.activation(
                        out=b[:, :], in_=a[:, :], func=AF.Sigmoid),
                        r=(a,), w=(b,))
                    p.step("scalar", lambda E, a=a, c_=c_: E.activation(
                        out=c_[:, :], in_=a[:, :], func=AF.Sigmoid, scale=-1.0),
                        r=(a, cst), w=(c_,))
                    p.step("scalar", lambda E, c_=c_: E.activation(
                        out=c_[:, :], in_=c_[:, :], func=AF.Ln),
                        r=(c_,), w=(c_,))
                    p.step("vector", lambda E, c_=c_, d_=d_: E.tensor_tensor_scan(
                        out=d_[:, :], data0=c_[:, :], data1=zz[:, :],
                        initial=0.0, op0=ALU.add, op1=ALU.add),
                        r=(c_, zz), w=(d_,))
                    p.step("vector", lambda E, a=a, d_=d_: E.tensor_copy(
                        out=a[:, 0:32], in_=d_[:, 0:32]), r=(d_,), w=(a,))
                    p.step("vector", lambda E, a=a, d_=d_: E.tensor_tensor(
                        out=a[:, 32:T].rearrange("p (bb cc) -> p bb cc", cc=32),
                        in0=d_[:, 32:T].rearrange("p (bb cc) -> p bb cc", cc=32),
                        in1=d_[:, 31:T - 32:32].unsqueeze(2).broadcast_to(
                            [128, T // 32 - 1, 32]),
                        op=ALU.subtract), r=(d_,), w=(a,))
                    p.step("vector", lambda E, a=a: E.tensor_scalar_max(
                        a[:, :], a[:, :], CLAMP), r=(a,), w=(a,))
                    p.step("scalar", lambda E, a=a, d_=d_: E.activation(
                        out=d_[:, 0:NCHUNK], in_=a[:, 31:T:32], func=AF.Exp),
                        r=(a,), w=(d_,))
                    p.dma(d_[:, 0:NCHUNK], kt["s_lam"][rs, :], q="scalar")
                    p.step("scalar", lambda E, a=a, c_=c_: E.activation(
                        out=c_[:, :], in_=a[:, :], func=AF.Exp),
                        r=(a,), w=(c_,))
                    p.dma(kt["s_q"][rs, :], ob2[:, :])
                    p.step("vector", lambda E, obx=obx, c_=c_, ob2=ob2:
                           E.tensor_mul(obx[:, :], c_[:, :], ob2[:, :]),
                           r=(c_, ob2), w=(obx,))
                    p.dma(obx[:, :], kt["s_gq"][rs, :], q="scalar")
                    p.step("scalar", lambda E, a=a, c_=c_: E.activation(
                        out=c_[:, :], in_=a[:, :], func=AF.Exp, scale=-1.0),
                        r=(a, cst), w=(c_,))
                    p.step("vector", lambda E, ob2=ob2, c_=c_, b=b:
                           E.tensor_mul(ob2[:, :], c_[:, :], b[:, :]),
                           r=(c_, b), w=(ob2,))
                    p.dma(ob2[:, :], kt["s_gk"][rs, :], q="scalar")
                    p.step("vector", lambda E, a=a, c_=c_: E.tensor_tensor(
                        out=c_[:, :].rearrange("p (bb cc) -> p bb cc", cc=32),
                        in0=a[:, 31:T:32].unsqueeze(2).broadcast_to(
                            [128, T // 32, 32]),
                        in1=a[:, :].rearrange("p (bb cc) -> p bb cc", cc=32),
                        op=ALU.subtract), r=(a,), w=(c_,))
                    p.step("scalar", lambda E, c_=c_: E.activation(
                        out=c_[:, :], in_=c_[:, :], func=AF.Exp),
                        r=(c_,), w=(c_,))
                    p.step("vector", lambda E, obx=obx, c_=c_, b=b:
                           E.tensor_mul(obx[:, :], c_[:, :], b[:, :]),
                           r=(c_, b), w=(obx,))
                    for tc in range(16):
                        p.step("tensor", lambda E, tc=tc, obx=obx, PT=PT:
                               E.transpose(PT[:, :],
                                           obx[:, tc * 128:(tc + 1) * 128],
                                           ident[:, :]),
                               r=(obx, ident), w=(PT,))
                        p.step("vector", lambda E, ob2=ob2, PT=PT, tc=tc:
                               E.tensor_copy(
                            out=ob2[:, tc * 128:(tc + 1) * 128], in_=PT[:, :]),
                            r=(PT,), w=(ob2,))
                    p.dma(v3(ob2[:, :], 128),
                          kt["s_khT"][:, rs].rearrange("(b q) n -> q b n",
                                                       q=128), q="scalar")

    def ph_gla_scan(tag):
        with ExitStack() as st:
            sbl = lambda n, sh, dt: st.enter_context(nc.sbuf_tensor(tag + n, sh, dt))
            Sf = [sbl("_Sf%d" % i, [128, 128], F32) for i in range(2)]
            Sb = [sbl("_Sb%d" % i, [128, 128], BF16) for i in range(2)]
            gq = [sbl("_gq%d" % i, [128, T], BF16) for i in range(2)]
            gk = [sbl("_gk%d" % i, [128, T], BF16) for i in range(2)]
            khT = [sbl("_kT%d" % i, [32, NCHUNK * 128], BF16) for i in range(2)]
            vg = [sbl("_vg%d" % i, [32, NCHUNK * 128], BF16) for i in range(2)]
            lam = [sbl("_lm%d" % i, [128, NCHUNK], F32) for i in range(2)]
            pm = [sbl("_pm%d" % i, [32, 32], BF16) for i in range(2)]
            ob = [sbl("_ob%d" % i, [128, 512], BF16) for i in range(2)]
            with ExitStack() as pstk:
                pp = lambda n: pstk.enter_context(
                    nc.psum_tensor(tag + n, [32, 32], F32))
                po = lambda n, sh: pstk.enter_context(
                    nc.psum_tensor(tag + n, sh, F32))
                ps_p = [pp("_pp0"), pp("_pp1")]
                ps_o = [po("_po0", [128, 512]), po("_po1", [128, 512])]
                ps_u = [po("_pu0", [128, 128]), po("_pu1", [128, 128])]

                def em_load(h):
                    pr = h % 2
                    rs = slice(h * 128, (h + 1) * 128)
                    p.dma(kt["s_gq"][rs, :], gq[pr][:, :])
                    p.dma(kt["s_gk"][rs, :], gk[pr][:, :])
                    p.dma(kt["s_lam"][rs, :], lam[pr][:, :])
                    p.dma(kt["s_khT"][:, rs].rearrange(
                        "(bb pp2) n -> pp2 bb n", pp2=32),
                        v3(khT[pr][:, :], 128))
                    p.dma(kt["s_vg"][:, rs].rearrange(
                        "(bb pp2) n -> pp2 bb n", pp2=32),
                        v3(vg[pr][:, :], 128))
                    p.step("vector", lambda E, pr=pr: E.memset(
                        Sf[pr][:, :], 0.0), w=(Sf[pr],))
                    p.step("vector", lambda E, pr=pr: E.memset(
                        Sb[pr][:, :], 0.0), w=(Sb[pr],))

                def em_chunk(h, B):
                    pr = h % 2
                    GQ, GK, KT, VG, LM = (gq[pr], gk[pr], khT[pr], vg[pr],
                                          lam[pr])
                    SF, SB, PM = Sf[pr], Sb[pr], pm[pr]
                    PP, PO, PU = ps_p[pr], ps_o[pr], ps_u[pr]
                    rs = slice(h * 128, (h + 1) * 128)
                    tb = slice(B * 32, (B + 1) * 32)
                    bsl = slice(B * 128, (B + 1) * 128)
                    osl = slice((B % 16) * 32, (B % 16) * 32 + 32)
                    p.step("tensor", lambda E, tb=tb, GK=GK, GQ=GQ, PP=PP:
                           E.matmul(PP[:, :], GK[:, tb], GQ[:, tb],
                                    start=True, stop=True),
                           r=(GK, GQ), w=(PP,))
                    p.step("vector", lambda E, PM=PM, PP=PP: E.tensor_mul(
                        PM[:, :], PP[:, :], gmask[:, :]),
                        r=(PP, gmask), w=(PM,))
                    p.step("tensor", lambda E, tb=tb, osl=osl, SB=SB,
                           GQ=GQ, PO=PO: E.matmul(
                        PO[:, osl], SB[:, :], GQ[:, tb],
                        start=True, stop=False),
                        r=(SB, GQ), w=(PO,) if B % 16 == 0 else (),
                        accum=(PO,) if B % 16 else ())
                    p.step("tensor", lambda E, bsl=bsl, osl=osl, VG=VG,
                           PM=PM, PO=PO: E.matmul(
                        PO[:, osl], VG[:, bsl], PM[:, :],
                        start=False, stop=True),
                        r=(VG, PM), accum=(PO,))
                    p.step("tensor", lambda E, bsl=bsl, KT=KT, VG=VG,
                           PU=PU: E.matmul(
                        PU[:, :], KT[:, bsl], VG[:, bsl],
                        start=True, stop=True), r=(KT, VG), w=(PU,))
                    p.step("vector", lambda E, B=B, SF=SF, LM=LM, PU=PU:
                           E.scalar_tensor_tensor(
                        out=SF[:, :], in0=SF[:, :], scalar=LM[:, B:B + 1],
                        in1=PU[:, :], op0=ALU.mult, op1=ALU.add),
                        r=(SF, LM, PU), w=(SF,))
                    p.step("vector", lambda E, SB=SB, SF=SF: E.tensor_copy(
                        out=SB[:, :], in_=SF[:, :]), r=(SF,), w=(SB,))
                    if B % 16 == 15:
                        OB = ob[pr]
                        p.step("vector", lambda E, OB=OB, PO=PO:
                               E.tensor_copy(out=OB[:, :], in_=PO[:, :]),
                               r=(PO,), w=(OB,))
                        t0 = (B // 16) * 512
                        p.dma(OB[:, :], kt["s_o"][rs, t0:t0 + 512],
                              q="scalar")

                for hp in range(8):
                    em_load(2 * hp)
                    em_load(2 * hp + 1)
                    for B in range(NCHUNK):
                        em_chunk(2 * hp, B)
                        em_chunk(2 * hp + 1, B)

    def ph_quant(src, dst, rq_row_idx, sw_idx, Din, tag):
        dch = Din // 128
        GW = 2048
        NBUF = 2 if dch <= 16 else 1
        with ExitStack() as st:
            sbl = lambda n, sh, dt: st.enter_context(nc.sbuf_tensor(tag + n, sh, dt))
            xs = [sbl("_x%d" % i, [128, dch * 512], F32) for i in range(NBUF)]
            mx = sbl("_m", [128, 512], F32)
            mn = sbl("_mn", [128, 512], F32)
            mxb = sbl("_mb", [128, 512], BF16)
            colr = sbl("_cr", [128, 4], F32)
            colb = sbl("_cb", [128, 4], BF16)
            rowb = sbl("_rb", [1, 512], F32)
            rowq = sbl("_rq", [1, 512], F32)
            swc = sbl("_sw", [1, 1], F32)
            sxb = sbl("_sx", [128, 512], F32)
            t1b = [sbl("_t1%d" % i, [128, GW], F32) for i in range(2)]
            t2b = [sbl("_t2%d" % i, [128, GW], F32) for i in range(2)]
            obb = [sbl("_obq%d" % i, [128, GW], BF16) for i in range(2)]
            with nc.psum_tensor(tag + "_pt", [128, 128], BF16) as pst, \
                 nc.psum_tensor(tag + "_pr", [1, 512], F32) as psr, \
                 nc.psum_tensor(tag + "_pb", [128, 512], F32) as psb:
                p.dma(kt["s_wsc"][0:1, sw_idx:sw_idx + 1], swc[:, :])
                for tt in range(TT):
                    X = xs[tt % NBUF]
                    tsl = slice(tt * 512, (tt + 1) * 512)
                    p.dma(fmv(src, tsl), v3(X[:, :], 512))
                    p.step("vector", lambda E, X=X: E.tensor_copy(
                        out=mx[:, :], in_=X[:, 0:512]), r=(X,), w=(mx,))
                    p.step("vector", lambda E, X=X: E.tensor_copy(
                        out=mn[:, :], in_=X[:, 0:512]), r=(X,), w=(mn,))
                    for c in range(1, dch):
                        csl = slice(c * 512, (c + 1) * 512)
                        p.step("vector", lambda E, csl=csl, X=X: E.tensor_tensor(
                            out=mx[:, :], in0=mx[:, :], in1=X[:, csl],
                            op=ALU.max), r=(mx, X), w=(mx,))
                        p.step("vector", lambda E, csl=csl, X=X: E.tensor_tensor(
                            out=mn[:, :], in0=mn[:, :], in1=X[:, csl],
                            op=ALU.min), r=(mn, X), w=(mn,))
                    p.step("vector", lambda E: E.tensor_scalar_mul(
                        mn[:, :], mn[:, :], -1.0), r=(mn,), w=(mn,))
                    p.step("vector", lambda E: E.tensor_tensor(
                        out=mx[:, :], in0=mx[:, :], in1=mn[:, :], op=ALU.max),
                        r=(mx, mn), w=(mx,))
                    p.step("vector", lambda E: E.tensor_copy(
                        out=mxb[:, :], in_=mx[:, :]), r=(mx,), w=(mxb,))
                    for q4 in range(4):
                        p.step("tensor", lambda E, q4=q4: E.transpose(
                            pst[:, :], mxb[:, q4 * 128:(q4 + 1) * 128],
                            ident[:, :]), r=(mxb, ident), w=(pst,))
                        p.step("vector", lambda E, q4=q4: E.reduce_max(
                            colr[:, q4:q4 + 1], pst[:, :], axis=AX.X,
                            apply_absolute_value=True), r=(pst,), w=(colr,))
                    for q4 in range(4):
                        p.step("vector", lambda E, q4=q4: E.tensor_copy(
                            out=colb[:, q4:q4 + 1], in_=colr[:, q4:q4 + 1]),
                            r=(colr,), w=(colb,))
                        p.step("tensor", lambda E, q4=q4: E.matmul(
                            psr[0:1, q4 * 128:(q4 + 1) * 128],
                            colb[:, q4:q4 + 1], ident[:, :],
                            start=True, stop=True), r=(colb, ident), w=(psr,))
                    p.step("vector", lambda E: E.tensor_scalar_max(
                        rowb[:, :], psr[0:1, :], 1e-5), r=(psr,), w=(rowb,))
                    p.step("vector", lambda E: E.reciprocal(
                        rowq[:, :], rowb[:, :]), r=(rowb,), w=(rowq,))
                    p.step("vector", lambda E: E.tensor_scalar_mul(
                        rowq[:, :], rowq[:, :], 127.0), r=(rowq,), w=(rowq,))
                    p.step("vector", lambda E: E.tensor_scalar(
                        out=rowb[:, :], in0=rowb[:, :], scalar1=swc[0:1, 0:1],
                        scalar2=1.0 / 127.0, op0=ALU.mult, op1=ALU.mult),
                        r=(rowb, swc), w=(rowb,))
                    p.dma(rowb[:, :],
                          kt["s_rows"][rq_row_idx:rq_row_idx + 1, tsl],
                          q="scalar")
                    p.step("vector", lambda E: E.tensor_copy(
                        out=colb[:, 0:1] if False else mxb[0:1, 0:512],
                        in_=rowb[:, :]), r=(rowb,), w=(mxb,))
                    p.dma(mxb[0:1, 0:512],
                          kt["s_rows_bf"][rq_row_idx:rq_row_idx + 1, tsl],
                          q="scalar")
                    p.step("tensor", lambda E: E.matmul(
                        psb[:, :], ones_row[:, :], rowq[0:1, :],
                        start=True, stop=True), r=(ones_row, rowq), w=(psb,))
                    p.step("vector", lambda E: E.tensor_copy(
                        out=sxb[:, :], in_=psb[:, :]), r=(psb,), w=(sxb,))
                    for c in range(dch):
                        csl = slice(c * 512, (c + 1) * 512)
                        p.step("vector", lambda E, csl=csl, X=X: E.tensor_mul(
                            X[:, csl], X[:, csl], sxb[:, :]),
                            r=(X, sxb), w=(X,))
                    ngr = (dch * 512) // GW
                    for g in range(ngr):
                        pr2 = g % 2
                        eng = "vector"
                        t1, t2, obq = t1b[pr2], t2b[pr2], obb[pr2]
                        gsl = slice(g * GW, (g + 1) * GW)
                        xv = X[:, gsl]
                        em_round2(eng, xv, t1, t2, GW, X, t1, t2)
                        p.step(eng, lambda E, t2=t2, obq=obq: E.tensor_copy(
                            out=obq[:, :], in_=t2[:, :]), r=(t2,), w=(obq,))
                        c0 = gsl.start // 512
                        c1 = gsl.stop // 512
                        p.dma(v3(obq[:, :], 512),
                              dst[c0 * 128:c1 * 128, tsl].rearrange(
                                  "(c p) t -> p c t", p=128), q="scalar")

    def ph_gelu_mult(tag):
        with ExitStack() as st:
            sbl = lambda n, sh, dt: st.enter_context(nc.sbuf_tensor(tag + n, sh, dt))
            ab = [sbl("_a%d" % i, [128, T], F32) for i in range(2)]
            bb2 = [sbl("_b%d" % i, [128, T], F32) for i in range(2)]
            obb = [sbl("_o%d" % i, [128, T], BF16) for i in range(2)]
            for c in range(128):
                pr = c % 2
                eng = "vector"
                a, b, ob = ab[pr], bb2[pr], obb[pr]
                rs = slice(c * 128, (c + 1) * 128)
                p.dma(kt["s_y1"][rs, :], a[:, :])
                p.dma(kt["s_y2"][rs, :], b[:, :])
                p.step("scalar", lambda E, a=a: E.activation(
                    out=a[:, :], in_=a[:, :], func=AF.Gelu_apprx_tanh),
                    r=(a,), w=(a,))
                p.step(eng, lambda E, ob=ob, a=a, b=b: E.tensor_mul(
                    ob[:, :], a[:, :], b[:, :]), r=(a, b), w=(ob,))
                p.dma(ob[:, :], kt["s_int"][rs, :], q="scalar")

    def ph_silu_mult(tag):
        with ExitStack() as st:
            sbl = lambda n, sh, dt: st.enter_context(nc.sbuf_tensor(tag + n, sh, dt))
            ab = [sbl("_a%d" % i, [128, T], F32) for i in range(2)]
            bb2 = [sbl("_b%d" % i, [128, T], F32) for i in range(2)]
            obb = [sbl("_o%d" % i, [128, T], F32) for i in range(2)]
            for c in range(44):
                pr = c % 2
                eng = "vector"
                a, b, ob = ab[pr], bb2[pr], obb[pr]
                p.dma(kt["s_y1"][c * 128:(c + 1) * 128, :], a[:, :])
                p.dma(kt["s_y1"][(44 + c) * 128:(45 + c) * 128, :], b[:, :])
                p.step("scalar", lambda E, a=a: E.activation(
                    out=a[:, :], in_=a[:, :], func=AF.Silu), r=(a,), w=(a,))
                p.step(eng, lambda E, ob=ob, a=a, b=b: E.tensor_mul(
                    ob[:, :], a[:, :], b[:, :]), r=(a, b), w=(ob,))
                p.dma(ob[:, :], kt["s_h"][c * 128:(c + 1) * 128, :], q="scalar")

    def pump(g, n):
        if g is None:
            return None
        p.exempt = True
        try:
            for _ in range(n):
                next(g)
        except StopIteration:
            g = None
        p.exempt = False
        return g

    def exhaust(g):
        g = pump(g, 10 ** 9)
        p.barrier_full()
        return None

    gen_g = ph_dequant(kt["PKg"], kt["SCg"], kt["s_wgq"], 2048, 16384, "dqg")
    gen_u = ph_dequant(kt["PKu"], kt["SCu"], kt["s_wuq"], 2048, 16384, "dqu")
    gen_d = ph_dequant(kt["PKd"], kt["SCd"], kt["s_wdq"], 16384, 2048, "dqd")
    gen_t1 = None
    gen_t2 = None

    for s in range(NSEQ):
        xin = kt["xT"][s]
        xcur = kt["s_x"]
        sfx = "%d" % s
        ph_layernorm(xin, kt["s_xn"], D, tag="ln0s" + sfx)
        p.barrier()
        gen_g = pump(gen_g, 8)
        ph_linear(kt["Wq"], kt["s_xn"], kt["s_q"], D, D, "pq" + sfx)
        p.barrier()
        gen_g = pump(gen_g, 12)
        ph_linear(kt["Wk"], kt["s_xn"], kt["s_k"], D, 512, "pk" + sfx)
        p.barrier()
        gen_g = pump(gen_g, 6)
        ph_linear_tok(kt["Wv"], kt["s_xn"], kt["s_v"], 512, "pv" + sfx)
        p.barrier()
        gen_g = pump(gen_g, 8)
        ph_rope(kt["s_q"], D, "rpq" + sfx)
        p.barrier()
        gen_g = pump(gen_g, 6)
        ph_rope(kt["s_k"], 512, "rpk" + sfx)
        p.barrier()
        gen_g = pump(gen_g, 8)
        ph_attn("at" + sfx)
        p.barrier()
        gen_g = pump(gen_g, 16)
        gen_u = pump(gen_u, 40)
        ph_linear(kt["Wo"], kt["s_o"], kt["s_z"], D, D, "po" + sfx,
                  out_dtype=F32)
        p.barrier()
        gen_u = pump(gen_u, 12)
        ph_resadd(xin, kt["s_z"], xcur, "ra0" + sfx)
        p.barrier()
        gen_d = pump(gen_d, 12)
        if DEBUG and s == 0:
            p.dma(xcur[:, :], kt["dbg0"][:, :])
        ph_layernorm(xcur, kt["s_xn"], D, tag="ln1s" + sfx)
        p.barrier()
        if s == 0:
            gen_g = exhaust(gen_g)
            gen_u = exhaust(gen_u)
        ph_linear_multi([(kt["s_wgq"], kt["s_y1"], "y1"),
                         (kt["s_wuq"], kt["s_y2"], "y2")],
                        kt["s_xn"], D, 16384, "fgu" + sfx,
                        post=lambda: em_gelu_mult("gm" + sfx))
        p.barrier()
        gen_d = pump(gen_d, 30)
        if s == 0:
            gen_d = exhaust(gen_d)
            dq_stk.close()
        ph_linear(kt["s_wdq"], kt["s_int"], kt["s_z"], 16384, D,
                  "fd" + sfx, w_dtype=BF16, out_dtype=F32)
        p.barrier()
        if s == 0:
            gen_t1 = ph_ternarize(kt["GWT"], kt["s_g8"], D, 11264, 0, "tg")
        gen_t1 = pump(gen_t1, 30)
        ph_resadd(xcur, kt["s_z"], xcur, "ra1" + sfx)
        p.barrier()
        if DEBUG and s == 0:
            p.dma(xcur[:, :], kt["dbg1"][:, :])
        ph_layernorm(xcur, kt["s_xn"], D, tag="ln2s" + sfx)
        p.barrier()
        gen_t1 = pump(gen_t1, 20)
        ph_linear(kt["W1q"], kt["s_xn"], kt["s_t1"], D, D, "gq" + sfx,
                  out_dtype=F32)
        p.barrier()
        gen_t1 = pump(gen_t1, 20)
        ph_linear(kt["W1k"], kt["s_xn"], kt["s_kr"], D, D, "gk" + sfx,
                  out_dtype=F32)
        p.barrier()
        gen_t1 = pump(gen_t1, 20)
        ph_linear_tok(kt["W1i"], kt["s_xn"], kt["s_vg"], D, "gv" + sfx)
        p.barrier()
        gen_t1 = pump(gen_t1, 20)
        ph_gla_prep("gp" + sfx)
        p.barrier()
        gen_t1 = pump(gen_t1, 30)
        ph_gla_scan("gs" + sfx)
        p.barrier()
        if s == 0:
            gen_t1 = exhaust(gen_t1)
            gen_t2 = ph_ternarize(kt["DWT"], kt["s_d8"], 5632, D, 1, "td")
            gen_t2 = exhaust(gen_t2)
        ph_layernorm(kt["s_o"], kt["s_o"], D, src_dtype=BF16,
                     mean_center=False, gamma_hbm=kt["gw"], tag="gn" + sfx)
        p.barrier()
        ph_linear(kt["W1o"], kt["s_o"], kt["s_z"], D, D, "go" + sfx,
                  out_dtype=F32)
        p.barrier()
        ph_resadd(xcur, kt["s_z"], xcur, "ra2" + sfx)
        p.barrier()
        if DEBUG and s == 0:
            p.dma(xcur[:, :], kt["dbg2"][:, :])
        ph_layernorm(xcur, kt["s_t1"], D, out_dtype=F32, tag="ln3s" + sfx)
        p.barrier()
        ph_layernorm(kt["s_t1"], kt["s_xr"], D, mean_center=False,
                     gamma_hbm=kt["gnw"], out_dtype=F32, tag="rm1s" + sfx)
        p.barrier()
        ph_quant(kt["s_xr"], kt["s_xq"], 0, 0, D, "q1" + sfx)
        p.barrier()
        ph_linear(kt["s_g8"], kt["s_xq"], kt["s_y1"], D, 11264,
                  "bg" + sfx, w_dtype=BF16, out_dtype=F32,
                  row_scale=kt["s_rows_bf"][0:1, :], out_keys="yb",
                  post=lambda: em_silu_mult("sm" + sfx))
        p.barrier()
        ph_layernorm(kt["s_h"], kt["s_hr"], 5632, mean_center=False,
                     gamma_hbm=kt["dnw"], out_dtype=F32, tag="rm2s" + sfx)
        p.barrier()
        ph_quant(kt["s_hr"], kt["s_hq"], 1, 1, 5632, "q2" + sfx)
        p.barrier()
        ph_linear(kt["s_d8"], kt["s_hq"], kt["s_z"], 5632, D,
                  "bd" + sfx, w_dtype=BF16, out_dtype=F32,
                  row_scale=kt["s_rows_bf"][1:2, :])
        p.barrier()
        ph_resadd(xcur, kt["s_z"], kt["yT"][s], "ra3" + sfx)
        p.barrier()
        if DEBUG and s == 0:
            p.dma(kt["yT"][s][:, :], kt["dbg3"][:, :])

    p.finalize()
    k.stk.close()
    return k


def _consts():
    ident = np.eye(128, dtype=NPBF)
    rmat = np.zeros((128, 128), dtype=NPBF)
    for m in range(64):
        rmat[m + 64, m] = -1.0
        rmat[m, m + 64] = 1.0
    e4g = np.zeros((4, 128), dtype=NPBF)
    for b in range(4):
        e4g[b, 16 * b:16 * (b + 1)] = 1
        e4g[b, 64 + 16 * b:64 + 16 * (b + 1)] = 1
    inv = np.power(10000.0, -2.0 * np.arange(64, dtype=np.float64) / 128.0)
    ang = np.arange(T, dtype=np.float64)[None, :] * inv[:, None]
    cosT = np.zeros((128, T), np.float32)
    sinT = np.zeros((128, T), np.float32)
    cosT[0:64] = np.cos(ang)
    cosT[64:128] = np.cos(ang)
    sinT[0:64] = np.sin(ang)
    sinT[64:128] = np.sin(ang)
    amask = np.zeros((512, 512), dtype=NPBF)
    ar = np.arange(512)
    for r in range(4):
        for kk2 in range(128):
            amask[r * 128 + kk2, :] = (ar >= (r * 128 + kk2))
    gmask = np.zeros((32, 32), dtype=NPBF)
    for j in range(32):
        gmask[j, j:] = 1
    return dict(ident=ident, rmat=rmat, e4g=e4g, cosT=cosT, sinT=sinT,
                amask=amask, gmask=gmask)


def _pk_bytes(w_packed_int32):
    v = np.ascontiguousarray(w_packed_int32).view(np.uint8)[:, 0::4]
    return np.ascontiguousarray(v.T)


def kernel(**inputs):
    cons = _consts()
    shared = dict(
        Wq=inputs["L0_attn_Wq"], Wk=inputs["L0_attn_Wk"],
        Wv=inputs["L0_attn_Wv"], Wo=inputs["L0_attn_Wo"],
        W1q=inputs["L1_attn_Wq"], W1k=inputs["L1_attn_Wk"],
        W1i=inputs["L1_attn_Wi"], W1o=inputs["L1_attn_Wo"],
        PKg=_pk_bytes(inputs["L0_ffn_gate_W_packed"]),
        PKu=_pk_bytes(inputs["L0_ffn_up_W_packed"]),
        PKd=_pk_bytes(inputs["L0_ffn_down_W_packed"]),
        SCg=np.ascontiguousarray(inputs["L0_ffn_gate_scales"].T),
        SCu=np.ascontiguousarray(inputs["L0_ffn_up_scales"].T),
        SCd=np.ascontiguousarray(inputs["L0_ffn_down_scales"].T),
        GWT=np.ascontiguousarray(inputs["L1_ffn_gate_w"].T),
        DWT=np.ascontiguousarray(inputs["L1_ffn_down_w"].T),
        gw=np.asarray(inputs["L1_attn_g_weight"]).reshape(16, 128).T,
        gnw=np.asarray(inputs["L1_ffn_gate_norm_w"]).reshape(16, 128).T,
        dnw=np.asarray(inputs["L1_ffn_down_norm_w"]).reshape(44, 128).T, **cons)
    shared = {k_: np.ascontiguousarray(v) for k_, v in shared.items()}
    x = inputs["L0_attn_x"]
    in_maps = []
    for core in range(8):
        xs = np.ascontiguousarray(
            np.asarray(x)[core * 2:core * 2 + 2].transpose(0, 2, 1)
        ).astype(np.float32)
        m = dict(shared)
        m["xT"] = xs
        in_maps.append(m)
    kk = build()
    res = run_bass_kernel_spmd(kk.nc, in_maps, core_ids=list(range(8)))
    out = np.empty((16, T, D), np.float32)
    for core in range(8):
        yt = res.results[core]["yT"]
        out[core * 2] = np.asarray(yt[0]).T
        out[core * 2 + 1] = np.asarray(yt[1]).T
    return out
